# version 05
# baseline (speedup 1.0000x reference)
import contextlib
import os as _os
import numpy as np
K_DBG = _os.environ.get("K_DBG", "") == "1"

import concourse.bass as bass
import concourse.mybir as mybir
from concourse.bass_utils import run_bass_kernel_spmd

F32 = mybir.dt.float32
BF16 = mybir.dt.bfloat16
I32 = mybir.dt.int32
ALU = mybir.AluOpType
ACT = mybir.ActivationFunctionType
AX = mybir.AxisListType.X

N_CORES = 8
B, S, D = 16, 2048, 1024
SEQ_PER_CORE = B // N_CORES
TOK = SEQ_PER_CORE * S
NT = S // 128
NSTATE = 256
DFF0 = 2736
DFF1 = 3072
NH, HD = 16, 64
NB = 128
WIN = 256
SCALE_ATT = 0.125
EPS_RMS = 1e-6
EPS_LN = 1e-5
NEG = -3.0e38

ENGS = ("pe", "act", "dve", "pool", "sp")
DMA_ENGS = ("sp", "pool", "act")


DMA_K = {"sp": 6, "pool": 6, "act": 2}


class Prog:
    def __init__(self, nc):
        self.nc = nc
        self.plan = {e: [] for e in ENGS}
        self.count = {e: 0 for e in ENGS}
        self.dma_count = {e: 0 for e in DMA_ENGS}
        self.writes = {}
        self.readers = {}
        self.last_wait = {e: {} for e in ENGS}
        self.sems = {}
        self.dma_sems = {}

    def setup_sems(self, stack):
        for e in ENGS:
            self.sems[e] = stack.enter_context(self.nc.semaphore(f"sem_{e}"))
        for e in DMA_ENGS:
            self.dma_sems[e] = [stack.enter_context(self.nc.semaphore(f"dsem_{e}{j}"))
                                for j in range(DMA_K[e])]

    def _wait(self, eng, oeng, kind, tick, force=False):
        if tick <= 0:
            return
        cur = self.last_wait[eng].get((oeng, kind), 0)
        if tick <= cur:
            return
        self.last_wait[eng][(oeng, kind)] = tick
        if kind == "c":
            sem, val = self.sems[oeng], tick
        else:
            sem, val = self.dma_sems[oeng][kind[1]], 16 * tick
        self.plan[eng].append(lambda E, sem=sem, val=val: E.wait_ge(sem, val))

    def dep(self, eng, reads=(), writes=(), sreads=()):
        for k in tuple(reads) + tuple(sreads):
            for w in self.writes.get(k, ()):
                self._wait(eng, w[0], w[1], w[2])
        for k in writes:
            for w in self.writes.get(k, ()):
                self._wait(eng, w[0], w[1], w[2])
            for (oeng, okind), t in self.readers.get(k, {}).items():
                self._wait(eng, oeng, okind, t)

    def commit(self, eng, kind, reads=(), writes=()):
        if kind == "c":
            self.count[eng] += 1
            tick = self.count[eng]
            sem = self.sems[eng]
        else:
            m = self.dma_count[eng]
            self.dma_count[eng] = m + 1
            slot = m % DMA_K[eng]
            tick = m // DMA_K[eng] + 1
            kind = ("d", slot)
            sem = self.dma_sems[eng][slot]
        for k in writes:
            if self.readers.get(k):
                self.writes[k] = [(eng, kind, tick)]
                self.readers[k] = {}
            else:
                self.writes.setdefault(k, []).append((eng, kind, tick))
        for k in reads:
            self.readers.setdefault(k, {})[(eng, kind)] = tick
        return sem, tick

    def op(self, eng, fn, reads=(), writes=(), sreads=()):
        self.dep(eng, reads, writes, sreads)
        sem, _ = self.commit(eng, "c", tuple(reads) + tuple(sreads), writes)
        self.plan[eng].append(lambda E, fn=fn, s=sem: fn(E).then_inc(s, 1))

    def ops(self, eng, fns, reads=(), writes=(), sreads=()):
        self.dep(eng, reads, writes, sreads)
        sem, _ = self.commit(eng, "c", tuple(reads) + tuple(sreads), writes)
        for fn in fns[:-1]:
            self.plan[eng].append(lambda E, fn=fn: fn(E))
        self.plan[eng].append(lambda E, fn=fns[-1], s=sem: fn(E).then_inc(s, 1))

    def dma(self, eng, fn, reads=(), writes=()):
        self.dep(eng, reads, writes)
        m = self.dma_count[eng]
        slot, gen = m % DMA_K[eng], m // DMA_K[eng]
        if gen > 0:
            self._wait(eng, eng, ("d", slot), gen)
        sem, tick = self.commit(eng, "d", reads, writes)
        self.plan[eng].append(lambda E, fn=fn, s=sem: fn(E).then_inc(s, 16))

    def raw(self, eng, fn):
        self.plan[eng].append(fn)

    def barrier(self):
        for e in ENGS:
            for o in ENGS:
                if o != e:
                    self._wait(e, o, "c", self.count[o])
            for q in DMA_ENGS:
                m = self.dma_count[q]
                for j in range(DMA_K[q]):
                    n = m // DMA_K[q] + (1 if m % DMA_K[q] > j else 0)
                    self._wait(e, q, ("d", j), n)
        self.writes.clear()
        self.readers.clear()

    def emit(self, block):
        def body(E, eng):
            for fn in self.plan[eng]:
                fn(E)
        block.tensor(lambda E: body(E, "pe"))
        block.scalar(lambda E: body(E, "act"))
        block.vector(lambda E: body(E, "dve"))
        block.gpsimd(lambda E: body(E, "pool"))
        block.sync(lambda E: body(E, "sp"))



def host_layouts(inputs):
    f32 = lambda a: np.ascontiguousarray(a, np.float32)
    hd_perm = np.concatenate([np.arange(0, HD, 2), np.arange(1, HD, 2)])
    perm = np.concatenate([h * HD + hd_perm for h in range(NH)])

    c = {}
    c["BwT"] = f32(np.asarray(inputs["L0_attn_B_w"]).T)
    c["CwT"] = f32(np.asarray(inputs["L0_attn_C_w"]).T)
    c["log_lambda"] = f32(inputs["L0_attn_log_lambda"])
    c["ssm_w"] = f32(inputs["L0_attn_ssm_norm_w"])
    c["D_skip"] = f32(inputs["L0_attn_D_skip"])
    c["ffn_w"] = f32(inputs["L0_attn_ffn_norm_w"])
    c["w1T"] = f32(np.asarray(inputs["L0_attn_w1"]).T)
    c["w3T"] = f32(np.asarray(inputs["L0_attn_w3"]).T)
    c["w2T"] = f32(np.asarray(inputs["L0_attn_w2"]).T)

    G1R = np.asarray(inputs["L0_ffn_G1_R"]).reshape(32, 4, 32)
    G2R = np.asarray(inputs["L0_ffn_G2_R"]).reshape(4, 16, 16)
    G1I = np.asarray(inputs["L0_ffn_G1_I"]).reshape(32, 4, 32)
    G2I = np.asarray(inputs["L0_ffn_G2_I"]).reshape(4, 16, 16)

    def tt_place(G1, G2):
        lhs = np.zeros((4, 512, 512), np.float32)
        rhs = np.zeros((4, 512, 512), np.float32)
        bg_b = np.repeat(np.arange(32), 16)
        bg_g = np.tile(np.arange(16), 32)
        for cc in range(4):
            for d in range(32):
                lhs[cc, np.arange(512), d * 16 + bg_g] = G1[bg_b, cc, d]
            for b in range(32):
                rows = b * 16 + np.arange(16)
                for e in range(16):
                    rhs[cc, rows, b * 16 + e] = G2[cc, e, np.arange(16)]
        return lhs, rhs

    c["ttlR"], c["ttrR"] = [f32(a) for a in tt_place(G1R, G2R)]
    c["ttlI"], c["ttrI"] = [f32(a) for a in tt_place(G1I, G2I)]
    c["bias_ri"] = f32(np.concatenate([np.asarray(inputs["L0_ffn_bias_real"]),
                                       np.asarray(inputs["L0_ffn_bias_imag"])]))

    c["WqP"] = f32(np.asarray(inputs["L1_attn_Wq"])[:, perm])
    c["WkP"] = f32(np.asarray(inputs["L1_attn_Wk"])[:, perm])
    c["Wv"] = f32(inputs["L1_attn_Wv"])
    c["Wo"] = f32(inputs["L1_attn_Wo"])
    c["bo"] = f32(inputs["L1_attn_bo"])
    c["peP"] = f32(np.asarray(inputs["L1_attn_pe"])[:, perm])
    c["pe_raw"] = f32(inputs["L1_attn_pe"])
    c["Wg"] = f32(inputs["L1_attn_Wg"])
    c["bg"] = f32(inputs["L1_attn_bg"])
    wk = np.asarray(inputs["L1_attn_wk_comp"]); wv = np.asarray(inputs["L1_attn_wv_comp"])
    Wck = np.zeros((128, 8), np.float32); Wcv = np.zeros((128, 8), np.float32)
    for j in range(8):
        Wck[16 * j + np.arange(16), j] = wk
        Wcv[16 * j + np.arange(16), j] = wv
    c["Wck"], c["Wcv"] = Wck, Wcv
    c["wk_comp"], c["wv_comp"] = f32(wk), f32(wv)

    half = HD // 2
    inv = (1.0 / 10000.0 ** (np.arange(half, dtype=np.float32) * 2.0 / HD)).astype(np.float32)
    ang = np.arange(S, dtype=np.float32)[:, None] * inv[None, :]
    c["cos_t"] = f32(np.cos(ang))
    c["sin_t"] = f32(np.sin(ang))

    tt = np.arange(S)[:, None]
    nn = np.arange(NB)[None, :]
    mc = np.where((nn + 1) * 16 <= tt, 0.0, NEG).astype(np.float32)
    c["mask_comp"] = f32(mc.reshape(16, 128, NB))
    mw = np.zeros((3, 128, 384), np.float32)
    for var in range(3):
        for pp in range(128):
            for j in range(384):
                if var == 0:
                    r, kt = pp, j
                elif var == 1:
                    r, kt = 128 + pp, j
                else:
                    r, kt = pp, j
    j = np.arange(384)[None, :]
    pp = np.arange(128)[:, None]
    mw0 = np.where((j <= pp) & (j > pp - WIN), 0.0, NEG)
    mw1 = np.where((j <= pp + 128) & (j > pp + 128 - WIN), 0.0, NEG)
    mw2 = np.where((j > pp) & (j <= pp + WIN), 0.0, NEG)
    c["mask_win"] = f32(np.stack([mw0[:, :384], mw1[:, :384], mw2]))

    c["w_upT"] = f32(np.asarray(inputs["L1_ffn_w_up"]).T)
    c["w_gateT"] = f32(np.asarray(inputs["L1_ffn_w_gate"]).T)
    c["w_downT"] = f32(np.asarray(inputs["L1_ffn_w_down"]).T)
    c["w_up_n"] = f32(inputs["L1_ffn_w_up"])
    c["w_gate_n"] = f32(inputs["L1_ffn_w_gate"])
    c["w_down_n"] = f32(inputs["L1_ffn_w_down"])
    repu = np.zeros((4, 128), np.float32)
    repu[np.arange(128) // 32, np.arange(128)] = 1.0
    c["REPU"] = repu

    c["identf"] = np.eye(128, dtype=np.float32)
    c["ones_row"] = np.ones((1, 128), np.float32)
    c["iota_p"] = np.arange(128, dtype=np.float32).reshape(128, 1)
    return c


HOST_KEYS = None



def build_core_program(n_stages=5, cc_world=8):
    nc = bass.Bass("TRN2", target_bir_lowering=False)
    st = contextlib.ExitStack()

    ext = {}
    def ein(name, shape):
        ext[name] = nc.dram_tensor(name, list(shape), F32, kind="ExternalInput")

    ein("x", (TOK, D))
    ein("BwT", (D, NSTATE)); ein("CwT", (NSTATE, D))
    ein("log_lambda", (NSTATE,)); ein("ssm_w", (D,)); ein("D_skip", (D,))
    ein("ffn_w", (D,))
    ein("w1T", (D, DFF0)); ein("w3T", (D, DFF0)); ein("w2T", (DFF0, D))
    ein("ttlR", (4, 512, 512)); ein("ttrR", (4, 512, 512))
    ein("ttlI", (4, 512, 512)); ein("ttrI", (4, 512, 512))
    ein("bias_ri", (2 * 512,))
    ein("WqP", (D, D)); ein("WkP", (D, D)); ein("Wv", (D, D)); ein("Wo", (D, D))
    ein("bo", (D,)); ein("peP", (16, D)); ein("pe_raw", (16, D)); ein("Wg", (D, 3)); ein("bg", (3,))
    ein("Wck", (128, 8)); ein("Wcv", (128, 8))
    ein("wk_comp", (16,)); ein("wv_comp", (16,))
    ein("cos_t", (S, 32)); ein("sin_t", (S, 32))
    ein("mask_comp", (16, 128, NB)); ein("mask_win", (3, 128, 384))
    ein("w_upT", (D, DFF1)); ein("w_gateT", (D, DFF1)); ein("w_downT", (DFF1, D))
    ein("w_up_n", (DFF1, D)); ein("w_gate_n", (DFF1, D)); ein("w_down_n", (D, DFF1))
    ein("REPU", (4, 128))
    ein("identf", (128, 128)); ein("ones_row", (1, 128)); ein("iota_p", (128, 1))

    out = nc.dram_tensor("out", [TOK, D], F32, kind="ExternalOutput")
    dbg = {}
    if K_DBG:
        for nm, shp in (("d_vseq", [128, NT * 1024]), ("d_uT", [128, 8 * S]),
                        ("d_Bu", [128, 2 * S]), ("d_hs", [128, 2 * S]),
                        ("d_BwTw", [128, 8 * NSTATE]), ("d_CwT", [128, 2 * D]),
                        ("d_lam", [128, 2]), ("d_Dw", [128, D])):
            dbg[nm] = nc.dram_tensor(nm, shp, F32, kind="ExternalOutput")
    xr = [nc.dram_tensor(f"xr{i}", [TOK, D], F32, kind="Internal")
          for i in range(1, 5)]
    dsts = xr[:4] + [out]
    if n_stages <= 4:
        dsts[n_stages - 1] = out

    p = Prog(nc)
    p.setup_sems(st)

    def sb(name, shape, dt=F32):
        return st.enter_context(nc.sbuf_tensor(name, list(shape), dt))

    identf = sb("identf_sb", [128, 128])
    identb = sb("identb_sb", [128, 128], BF16)
    ones_row = sb("ones_row_sb", [1, 128])
    iota_p = sb("iota_p_sb", [128, 1])
    sq_scr = sb("sq_scr", [128, 1024])
    cols = sb("cols", [128, 16])
    cpool = sb("cpool", [128, 4])
    colsB = sb("colsB", [128, 8])
    i32_scr = sb("i32_scr", [128, 1024], I32)

    p.dma("sp", lambda E: E.dma_start(identf[:], ext["identf"].ap()), writes=["identf"])
    p.dma("pool", lambda E: E.dma_start(identb[:], ext["identf"].ap()), writes=["identb"])
    p.dma("sp", lambda E: E.dma_start(ones_row[:], ext["ones_row"].ap()), writes=["ones_row"])
    p.dma("sp", lambda E: E.dma_start(iota_p[:], ext["iota_p"].ap()), writes=["iota_p"])
    for j, val in enumerate((EPS_RMS, EPS_LN)):
        p.op("dve", lambda E, j=j, val=val: E.memset(cpool[:, j:j + 1], val),
             writes=["cpool"])
        nc.const_aps.register(F32, val, cpool[:, j:j + 1])

    psf = [st.enter_context(nc.psum_tensor(f"psf{i}", [128, 512], F32))
           for i in range(6)]
    psb = [st.enter_context(nc.psum_tensor(f"psb{i}", [128, 1024], BF16))
           for i in range(2)]
    PSF = [f"psf{i}" for i in range(6)]
    PSB = [f"psb{i}" for i in range(2)]

    def bcast_row(row_ap, out_sb, n, rkeys, wkey, banks=(0, 1), dt_out=None):
        o = 0
        bi = 0
        while o < n:
            w = min(512, n - o)
            bank = banks[bi % len(banks)]
            bi += 1
            p.op("pe", lambda E, o=o, w=w, bank=bank: E.matmul(
                psf[bank][:, 0:w], ones_row[0:1, :], row_ap[0:1, o:o + w],
                start=True, stop=True),
                reads=list(rkeys) + ["ones_row"], writes=[PSF[bank]])
            p.op("dve", lambda E, o=o, w=w, bank=bank: E.tensor_copy(
                out_sb[:, o:o + w], psf[bank][:, 0:w]),
                reads=[PSF[bank]], writes=[wkey])
            o += w

    lnpool = sb("lnpool", [128, 12])

    def emit_ln(xv, xk, dst, t0, it, dkey, slot=0):
        c0 = slot * 6
        mu = lnpool[:, c0:c0 + 1]
        ssv = lnpool[:, c0 + 1:c0 + 2]
        var = lnpool[:, c0 + 2:c0 + 3]
        rst = lnpool[:, c0 + 3:c0 + 4]
        nmr = lnpool[:, c0 + 4:c0 + 5]
        kmu, kss, kvar, krst, knmr = (f"ln{slot}mu", f"ln{slot}ss",
                                      f"ln{slot}var", f"ln{slot}rstd",
                                      f"ln{slot}nmr")
        p.op("dve", lambda E: E.reduce_sum(mu, xv, axis=AX),
             reads=[xk], writes=[kmu])
        p.op("act", lambda E: E.activation(
            sq_scr[:], xv, ACT.Square, accum_out=ssv),
            reads=[xk], writes=["sq_scr", kss])
        p.op("dve", lambda E: E.tensor_scalar_mul(mu, mu, 1.0 / D),
             reads=[kmu], writes=[kmu])
        p.op("dve", lambda E: E.tensor_mul(var, mu, mu),
             sreads=[kmu], writes=[kvar])
        p.op("dve", lambda E: E.scalar_tensor_tensor(
            var, ssv, 1.0 / D, var, ALU.mult, ALU.subtract),
            reads=[kss], sreads=[kvar], writes=[kvar])
        p.op("act", lambda E: E.activation(
            var, var, ACT.Sqrt, scale=1.0, bias=EPS_LN),
            reads=[kvar, "cpool"], writes=[kvar])
        p.op("dve", lambda E: E.reciprocal(rst, var),
             reads=[kvar], writes=[krst])
        p.op("dve", lambda E: E.tensor_mul(nmr, mu, rst),
             sreads=[kmu, krst], writes=[knmr])
        p.op("dve", lambda E: E.tensor_scalar_mul(nmr, nmr, -1.0),
             sreads=[knmr], writes=[knmr])
        p.op("dve", lambda E: E.tensor_scalar(
            xv, xv, rst, nmr, ALU.mult, ALU.add),
            reads=[xk], sreads=[krst, knmr], writes=[xk])
        p.dma("sp", lambda E: E.dma_start(
            dst.ap()[t0 + it * 128:t0 + (it + 1) * 128, :], xv),
            reads=[xk], writes=[dkey])

    def stage0(src, dst):
        s0 = contextlib.ExitStack()
        def lsb(name, shape, dt=F32):
            return s0.enter_context(nc.sbuf_tensor(name, list(shape), dt))

        BwTw = lsb("BwTw", [128, 8 * NSTATE], BF16)
        CwT = lsb("CwT_sb", [128, 2 * D], BF16)
        lam_col = lsb("lam_col", [128, 2])
        Dw_bc = lsb("Dw_bc", [128, D], BF16)
        wrow = lsb("wrow", [1, 2 * D])
        wcol = lsb("wcol", [128, 8])
        xseq = lsb("xseq", [128, NT * 1024])
        vseq = lsb("vseq", [128, NT * 1024], BF16)
        uT = lsb("uT", [128, 8 * S], BF16)
        Bu = lsb("Bu", [128, 2 * S])
        du_scr = lsb("du_scr", [128, 1024], BF16)
        hs = lsb("hs", [128, 2 * S], BF16)

        for dc in range(8):
            p.dma("pool", lambda E, dc=dc: E.dma_start(
                BwTw[:, dc * NSTATE:(dc + 1) * NSTATE],
                ext["BwT"].ap()[dc * 128:(dc + 1) * 128, :]), writes=[f"BwTw{dc}"])
        p.dma("pool", lambda E: E.dma_start(
            wcol[:], bass.AP(ext["ssm_w"], 0, [[1, 128], [128, 8]]),
            allow_slow_non_contiguous=True), writes=["wcol"])
        for dc in range(8):
            p.op("dve", lambda E, dc=dc: E.tensor_scalar_mul(
                BwTw[:, dc * NSTATE:(dc + 1) * NSTATE],
                BwTw[:, dc * NSTATE:(dc + 1) * NSTATE], wcol[:, dc:dc + 1]),
                reads=[f"BwTw{dc}", "wcol"], writes=[f"BwTw{dc}"])
        for c in range(2):
            p.dma("pool", lambda E, c=c: E.dma_start(
                CwT[:, c * D:(c + 1) * D],
                ext["CwT"].ap()[c * 128:(c + 1) * 128, :]), writes=["CwT"])
        p.dma("pool", lambda E: E.dma_start(
            lam_col[:], bass.AP(ext["log_lambda"], 0, [[1, 128], [128, 2]]),
            allow_slow_non_contiguous=True), writes=["lam_col"])
        p.op("act", lambda E: E.activation(lam_col[:], lam_col[:], ACT.Sigmoid),
             reads=["lam_col"], writes=["lam_col"])
        p.dma("sp", lambda E: E.dma_start(wrow[0:1, 0:D], ext["ssm_w"].ap()[None, :]),
              writes=["wrow"])
        p.dma("sp", lambda E: E.dma_start(wrow[0:1, D:2 * D], ext["D_skip"].ap()[None, :]),
              writes=["wrow2"])
        p.op("dve", lambda E: E.tensor_mul(wrow[0:1, D:2 * D], wrow[0:1, 0:D],
                                           wrow[0:1, D:2 * D]),
             reads=["wrow", "wrow2"], writes=["wrow2"])
        bcast_row(wrow[0:1, D:2 * D], Dw_bc, D, ["wrow2"], "Dw_bc", banks=(0, 1))

        rms = cols[:, 0:1]
        rstd = cols[:, 1:2]

        for s in range(SEQ_PER_CORE):
            t0 = s * S
            for i in range(NT):
                xv = xseq[:, i * 1024:(i + 1) * 1024]
                vv = vseq[:, i * 1024:(i + 1) * 1024]
                p.dma("sp", lambda E, i=i, xv=xv, t0=t0: E.dma_start(
                    xv, ext["x"].ap()[t0 + i * 128:t0 + (i + 1) * 128, :]),
                    writes=[f"xt{i}"])
                p.op("act", lambda E, xv=xv: E.activation(
                    sq_scr[:], xv, ACT.Square, accum_out=rms),
                    reads=[f"xt{i}"], writes=["sq_scr", "rms"])
                p.op("act", lambda E: E.activation(
                    rms, rms, ACT.Sqrt, scale=1.0 / D, bias=EPS_RMS),
                    reads=["rms", "cpool"], writes=["rms"])
                p.op("dve", lambda E: E.reciprocal(rstd, rms),
                     reads=["rms"], writes=["rstd"])
                p.op("dve", lambda E, xv=xv, vv=vv: E.tensor_scalar_mul(
                    vv, xv, rstd),
                    reads=[f"xt{i}"], sreads=["rstd"], writes=[f"v{i}"])
                bank = i % 2
                fns = []
                for dc in range(8):
                    fns.append(lambda E, vv=vv, dc=dc, bank=bank: E.transpose(
                        psb[bank][:, dc * 128:(dc + 1) * 128],
                        vv[:, dc * 128:(dc + 1) * 128], identb[:]))
                p.ops("pe", fns, reads=[f"v{i}", "identb"], writes=[PSB[bank]])
                p.op("dve" if i % 2 == 0 else "act",
                     lambda E, i=i, bank=bank: (E.tensor_copy if i % 2 == 0 else E.copy)(
                         uT[:].rearrange("p (c t) -> p c t", c=8)[:, :, i * 128:(i + 1) * 128],
                         psb[bank][:].rearrange("p (c t) -> p c t", c=8)),
                     reads=[PSB[bank]], writes=[f"uTt{i}"])
            for ncnk in range(2):
                for tc in range(4):
                    bank = 2 + (tc % 2)
                    fns = []
                    for dc in range(8):
                        fns.append(lambda E, ncnk=ncnk, tc=tc, dc=dc, bank=bank: E.matmul(
                            psf[bank][:],
                            BwTw[:, dc * NSTATE + ncnk * 128: dc * NSTATE + (ncnk + 1) * 128],
                            uT[:, dc * S + tc * 512:dc * S + (tc + 1) * 512],
                            start=(dc == 0), stop=(dc == 7)))
                    p.ops("pe", fns,
                          reads=[f"BwTw{dc}" for dc in range(8)] +
                                [f"uTt{i}" for i in range(NT)],
                          writes=[PSF[bank]])
                    p.op("act", lambda E, ncnk=ncnk, tc=tc, bank=bank: E.copy(
                        Bu[:, ncnk * S + tc * 512:ncnk * S + (tc + 1) * 512],
                        psf[bank][:]),
                        reads=[PSF[bank]], writes=[f"Bu{ncnk}"])
            for ncnk in range(2):
                p.op("dve", lambda E, ncnk=ncnk: E.tensor_tensor_scan(
                    hs[:, ncnk * S:(ncnk + 1) * S],
                    lam_col[:, ncnk:ncnk + 1].broadcast_to([128, S]),
                    Bu[:, ncnk * S:(ncnk + 1) * S],
                    0.0, ALU.mult, ALU.add),
                    reads=[f"Bu{ncnk}", "lam_col"], writes=[f"hs{ncnk}"])
            if K_DBG and s == 0:
                dbg_keys = {"d_vseq": [f"v{i}" for i in range(NT)],
                            "d_uT": [f"uTt{i}" for i in range(NT)],
                            "d_Bu": ["Bu0", "Bu1"], "d_hs": ["hs0", "hs1"],
                            "d_BwTw": [f"BwTw{dc}" for dc in range(8)],
                            "d_CwT": ["CwT"], "d_lam": ["lam_col"],
                            "d_Dw": ["Dw_bc"]}
                for nm, t_ in (("d_vseq", vseq), ("d_uT", uT), ("d_Bu", Bu),
                               ("d_hs", hs), ("d_BwTw", BwTw), ("d_CwT", CwT),
                               ("d_lam", lam_col), ("d_Dw", Dw_bc)):
                    p.dma("pool", lambda E, nm=nm, t_=t_: E.dma_start(
                        dbg[nm].ap(), t_[:]),
                        reads=dbg_keys[nm], writes=[f"dbg{nm}"])
            for i in range(NT):
                xv = xseq[:, i * 1024:(i + 1) * 1024]
                vv = vseq[:, i * 1024:(i + 1) * 1024]
                b0, b1 = 4, 5
                for dh, bank in ((0, b0), (1, b1)):
                    fns = []
                    for ncnk in range(2):
                        fns.append(lambda E, i=i, dh=dh, ncnk=ncnk, bank=bank: E.matmul(
                            psf[bank][:],
                            hs[:, ncnk * S + i * 128:ncnk * S + (i + 1) * 128],
                            CwT[:, ncnk * D + dh * 512:ncnk * D + (dh + 1) * 512],
                            start=(ncnk == 0), stop=(ncnk == 1)))
                    p.ops("pe", fns, reads=["hs0", "hs1", "CwT"], writes=[PSF[bank]])
                p.op("dve", lambda E, vv=vv: E.tensor_mul(
                    du_scr[:], vv, Dw_bc[:]),
                    reads=[f"v{i}", "Dw_bc"], writes=["du"])
                p.op("dve", lambda E, xv=xv: E.tensor_add(
                    xv, xv, du_scr[:]),
                    reads=["du", f"xt{i}"], writes=[f"xt{i}"])
                for dh, bank in ((0, b0), (1, b1)):
                    p.op("dve", lambda E, xv=xv, dh=dh, bank=bank: E.tensor_add(
                        xv[:, dh * 512:(dh + 1) * 512],
                        xv[:, dh * 512:(dh + 1) * 512], psf[bank][:]),
                        reads=[PSF[bank], f"xt{i}"], writes=[f"xt{i}"])
                p.dma("sp", lambda E, i=i, xv=xv, t0=t0: E.dma_start(
                    dst.ap()[t0 + i * 128:t0 + (i + 1) * 128, :], xv),
                    reads=[f"xt{i}"], writes=[f"dst0_{i}_{s}"])
        s0.close()

    def stage1(src, dst):
        s1 = contextlib.ExitStack()
        def lsb(name, shape, dt=F32):
            return s1.enter_context(nc.sbuf_tensor(name, list(shape), dt))

        FCH = [128] * 21 + [48]
        w1w = [lsb(f"w1w{dc}", [128, DFF0], BF16) for dc in range(8)]
        w3w = [lsb(f"w3w{dc}", [128, DFF0], BF16) for dc in range(8)]
        w2r = [lsb(f"w2r{j}", [128, 1024], BF16) for j in range(22)]
        wfcol = lsb("wfcol", [128, 8])
        xs = lsb("xs", [128, 4 * 1024])
        zT = lsb("zT", [128, 8 * 512], BF16)
        gvT = lsb("gvT", [128, 22 * 512], BF16)
        sg = lsb("sg", [128, 2 * 512], BF16)
        x0t = lsb("x0t", [128, 2 * 1024])
        zscr = lsb("zscr", [128, 2 * 1024], BF16)

        for dc in range(8):
            p.dma("pool", lambda E, dc=dc: E.dma_start(
                w1w[dc][:], ext["w1T"].ap()[dc * 128:(dc + 1) * 128, :]),
                writes=[f"w1w{dc}"])
            p.dma("pool", lambda E, dc=dc: E.dma_start(
                w3w[dc][:], ext["w3T"].ap()[dc * 128:(dc + 1) * 128, :]),
                writes=[f"w3w{dc}"])
        p.dma("sp", lambda E: E.dma_start(
            wfcol[:], bass.AP(ext["ffn_w"], 0, [[1, 128], [128, 8]]),
            allow_slow_non_contiguous=True), writes=["wfcol"])
        for fc in range(22):
            p.dma("pool", lambda E, fc=fc: E.dma_start(
                w2r[fc][0:FCH[fc], :],
                ext["w2T"].ap()[fc * 128:fc * 128 + FCH[fc], :]),
                writes=[f"s1w2_{fc}"])
        for dc in range(8):
            p.op("dve", lambda E, dc=dc: E.tensor_scalar_mul(
                w1w[dc][:], w1w[dc][:], wfcol[:, dc:dc + 1]),
                reads=[f"w1w{dc}", "wfcol"], writes=[f"w1w{dc}"])
            p.op("dve", lambda E, dc=dc: E.tensor_scalar_mul(
                w3w[dc][:], w3w[dc][:], wfcol[:, dc:dc + 1]),
                reads=[f"w3w{dc}", "wfcol"], writes=[f"w3w{dc}"])

        rms = cols[:, 2:3]
        rstd = cols[:, 3:4]

        for s in range(SEQ_PER_CORE):
            for tc in range(4):
                t0 = s * S + tc * 512
                tcb = 0
                for it in range(4):
                    pr = it % 2
                    xv = xs[:, (tcb * 4 + it) * 1024:(tcb * 4 + it + 1) * 1024]
                    xk = f"s1x{tcb}_{it}"
                    x0v = x0t[:, pr * 1024:(pr + 1) * 1024]
                    x0k = f"s1x0t{pr}"
                    rmsv = cols[:, 2 + pr:3 + pr]
                    rstdv = colsB[:, pr:pr + 1]
                    rk = f"s1rms{pr}"
                    rsk = f"s1rstd{pr}"
                    p.dma("sp", lambda E, x0v=x0v, t0=t0, it=it: E.dma_start(
                        x0v, src.ap()[t0 + it * 128:t0 + (it + 1) * 128, :]),
                        writes=[x0k])
                    p.dma("act", lambda E, xv=xv, t0=t0, it=it: E.dma_start(
                        xv, ext["x"].ap()[t0 + it * 128:t0 + (it + 1) * 128, :]),
                        writes=[xk])
                    p.op("act", lambda E, x0v=x0v, pr=pr, rmsv=rmsv: E.activation(
                        sq_scr[:, pr * 512:pr * 512 + 512]
                        if False else sq_scr[:], x0v, ACT.Square, accum_out=rmsv),
                        reads=[x0k], writes=["sq_scr", rk])
                    p.op("act", lambda E, rmsv=rmsv: E.activation(
                        rmsv, rmsv, ACT.Sqrt, scale=1.0 / D, bias=EPS_RMS),
                        reads=[rk, "cpool"], writes=[rk])
                    p.op("dve", lambda E, rmsv=rmsv, rstdv=rstdv: E.reciprocal(
                        rstdv, rmsv), reads=[rk], writes=[rsk])
                    p.op("dve", lambda E, xv=xv, x0v=x0v: E.tensor_add(xv, xv, x0v),
                         reads=[xk, x0k], writes=[xk])
                    zv = zscr[:, pr * 1024:(pr + 1) * 1024]
                    zwk = f"s1z{pr}"
                    p.op("dve", lambda E, zv=zv, x0v=x0v, rstdv=rstdv: E.tensor_scalar_mul(
                        zv, x0v, rstdv),
                        reads=[x0k], sreads=[rsk],
                        writes=[zwk])
                    bank = it % 2
                    fns = []
                    for dc in range(8):
                        fns.append(lambda E, zv=zv, dc=dc, bank=bank: E.transpose(
                            psb[bank][:, dc * 128:(dc + 1) * 128],
                            zv[:, dc * 128:(dc + 1) * 128], identb[:]))
                    p.ops("pe", fns, reads=[zwk, "identb"], writes=[PSB[bank]])
                    p.op("dve" if it % 2 == 0 else "act",
                         lambda E, it=it, bank=bank, tcb=tcb: (
                             (E.tensor_copy if it % 2 == 0 else E.copy)(
                                 zT[:].rearrange("p (c t) -> p c t", c=8)
                                 [:, :, it * 128:(it + 1) * 128],
                                 psb[bank][:].rearrange("p (c t) -> p c t", c=8))),
                         reads=[PSB[bank]], writes=[f"s1zT{tcb}_{it}"])
                zkeys = [f"s1zT{tcb}_{it}" for it in range(4)]
                fo = 0
                for fc in range(22):
                    fw = FCH[fc]
                    gb = (fc % 2) * 2
                    vb = gb + 1
                    sgp = fc % 2
                    for which, wsb, bank in (("g", w1w, gb), ("v", w3w, vb)):
                        fns = []
                        for dc in range(8):
                            fns.append(lambda E, dc=dc, fo=fo, fw=fw, wsb=wsb,
                                       bank=bank, tcb=tcb: E.matmul(
                                psf[bank][0:fw, :],
                                wsb[dc][:, fo:fo + fw],
                                zT[:, tcb * 8 * 512 + dc * 512:
                                   tcb * 8 * 512 + (dc + 1) * 512],
                                start=(dc == 0), stop=(dc == 7)))
                        p.ops("pe", fns,
                              reads=[f"w{1 if which == 'g' else 3}w{dc}" for dc in range(8)] + zkeys,
                              writes=[PSF[bank]])
                    p.op("act", lambda E, fw=fw, gb=gb, sgp=sgp: E.activation(
                        sg[0:fw, sgp * 512:(sgp + 1) * 512], psf[gb][0:fw, :],
                        ACT.Silu),
                        reads=[PSF[gb]], writes=[f"s1sg{sgp}"])
                    p.op("dve", lambda E, fc=fc, fw=fw, vb=vb, sgp=sgp: E.tensor_mul(
                        gvT[0:fw, fc * 512:(fc + 1) * 512],
                        sg[0:fw, sgp * 512:(sgp + 1) * 512], psf[vb][0:fw, :]),
                        reads=[f"s1sg{sgp}", PSF[vb]], writes=[f"s1gv{fc}"])
                    fo += fw
                gvkeys = [f"s1gv{fc}" for fc in range(22)]
                for half in range(2):
                    for fc in range(22):
                        for ih in range(2):
                            it = half * 2 + ih
                            for dh in range(2):
                                bank = ih * 2 + dh
                                p.op("pe", lambda E, fc=fc, it=it, dh=dh,
                                     bank=bank: E.matmul(
                                    psf[bank][:],
                                    gvT[0:FCH[fc], fc * 512 + it * 128:
                                        fc * 512 + (it + 1) * 128],
                                    w2r[fc][0:FCH[fc], dh * 512:(dh + 1) * 512],
                                    start=(fc == 0), stop=(fc == 21)),
                                    reads=[f"s1gv{fc}", f"s1w2_{fc}"],
                                    writes=[PSF[bank]])
                    for ih in range(2):
                        it = half * 2 + ih
                        xv = xs[:, (tcb * 4 + it) * 1024:(tcb * 4 + it + 1) * 1024]
                        xk = f"s1x{tcb}_{it}"
                        for dh in range(2):
                            bank = ih * 2 + dh
                            p.op("dve", lambda E, xv=xv, dh=dh, bank=bank: E.tensor_add(
                                xv[:, dh * 512:(dh + 1) * 512],
                                xv[:, dh * 512:(dh + 1) * 512], psf[bank][:]),
                                reads=[PSF[bank], xk], writes=[xk])
                        emit_ln(xv, xk, dst, t0, it, f"s1dst{t0}_{it}", slot=it % 2)
        s1.close()

    def stage2(src, dst):
        s2 = contextlib.ExitStack()
        def lsb(name, shape, dt=F32):
            return s2.enter_context(nc.sbuf_tensor(name, list(shape), dt))

        ttl = {pp: lsb(f"sttl{pp}", [128, 4 * 4 * 512], BF16) for pp in "RI"}
        ttr = {pp: lsb(f"sttr{pp}", [128, 4 * 4 * 512], BF16) for pp in "RI"}
        WT = {pp: lsb(f"WT{pp}", [128, 4 * 512], BF16) for pp in "RI"}
        WnI = lsb("WnI", [128, 4 * 512], BF16)
        bias_bc = lsb("bias_bc", [128, 1024])
        brow = lsb("brow", [1, 1024])
        xbuf = lsb("s2xbuf", [128, 2 * 1024])
        xTb = lsb("s2xT", [128, 2 * 8 * 128], BF16)
        zb = lsb("s2zb", [128, 1024], BF16)

        for pp in "RI":
            for cc in range(4):
                for kc in range(4):
                    off = (cc * 4 + kc) * 512
                    p.dma("pool", lambda E, pp=pp, cc=cc, kc=kc, off=off: E.dma_start(
                        ttl[pp][:, off:off + 512],
                        ext["ttl" + pp].ap()[cc, kc * 128:(kc + 1) * 128, :]),
                        writes=[f"ttl{pp}"])
                    p.dma("pool", lambda E, pp=pp, cc=cc, kc=kc, off=off: E.dma_start(
                        ttr[pp][:, off:off + 512],
                        ext["ttr" + pp].ap()[cc, kc * 128:(kc + 1) * 128, :]),
                        writes=[f"ttr{pp}"])
        for pp in "RI":
            for ic in range(4):
                bank = ic % 2
                fns = []
                for cc in range(4):
                    for kc in range(4):
                        off = (cc * 4 + kc) * 512
                        fns.append(lambda E, pp=pp, ic=ic, cc=cc, kc=kc, off=off,
                                   bank=bank: E.matmul(
                            psf[bank][:],
                            ttl[pp][:, off + ic * 128:off + (ic + 1) * 128],
                            ttr[pp][:, off:off + 512],
                            start=(cc == 0 and kc == 0),
                            stop=(cc == 3 and kc == 3)))
                p.ops("pe", fns, reads=[f"ttl{pp}", f"ttr{pp}"],
                      writes=[PSF[bank]])
                p.op("dve", lambda E, pp=pp, ic=ic, bank=bank: E.tensor_copy(
                    WT[pp][:, ic * 512:(ic + 1) * 512], psf[bank][:]),
                    reads=[PSF[bank]], writes=[f"WT{pp}"])
        p.op("dve", lambda E: E.tensor_scalar_mul(WnI[:], WT["I"][:], -1.0),
             reads=["WTI"], writes=["WnI"])
        p.dma("sp", lambda E: E.dma_start(brow[0:1, :], ext["bias_ri"].ap()[None, :]),
              writes=["brow"])
        bcast_row(brow[0:1, :], bias_bc, 1024, ["brow"], "bias_bc", banks=(2, 3))

        for s in range(SEQ_PER_CORE):
            for it in range(NT):
                t0 = s * S
                tb = it % 2
                xv = xbuf[:, tb * 1024:(tb + 1) * 1024]
                xk = f"s2x{tb}"
                p.dma("sp", lambda E, xv=xv, t0=t0, it=it: E.dma_start(
                    xv, src.ap()[t0 + it * 128:t0 + (it + 1) * 128, :]),
                    writes=[xk])
                p.op("dve", lambda E, xv=xv: E.tensor_copy(zb[:], xv),
                     reads=[xk], writes=["s2z"])
                fns = []
                for dc in range(8):
                    fns.append(lambda E, dc=dc, tb=tb: E.transpose(
                        psb[tb][:, dc * 128:(dc + 1) * 128],
                        zb[:, dc * 128:(dc + 1) * 128], identb[:]))
                p.ops("pe", fns, reads=["s2z", "identb"], writes=[PSB[tb]])
                p.op("dve", lambda E, tb=tb: E.tensor_copy(
                    xTb[:, tb * 1024:(tb + 1) * 1024], psb[tb][:]),
                    reads=[PSB[tb]], writes=[f"s2xT{tb}"])
                for om, bank in ((0, 4), (1, 5)):
                    fns = []
                    for dc in range(8):
                        if om == 0:
                            wsb, wk = (WT["R"], "WTR") if dc < 4 else (WnI, "WnI")
                        else:
                            wsb, wk = (WT["I"], "WTI") if dc < 4 else (WT["R"], "WTR")
                        ic = dc % 4
                        fns.append(lambda E, dc=dc, ic=ic, wsb=wsb, tb=tb,
                                   bank=bank: E.matmul(
                            psf[bank][:],
                            xTb[:, tb * 1024 + dc * 128:tb * 1024 + (dc + 1) * 128],
                            wsb[:, ic * 512:(ic + 1) * 512],
                            start=(dc == 0), stop=(dc == 7)))
                    p.ops("pe", fns,
                          reads=[f"s2xT{tb}", "WTR", "WTI", "WnI"],
                          writes=[PSF[bank]])
                for om, bank in ((0, 4), (1, 5)):
                    p.op("dve", lambda E, xv=xv, om=om, bank=bank: E.tensor_add(
                        xv[:, om * 512:(om + 1) * 512],
                        xv[:, om * 512:(om + 1) * 512], psf[bank][:]),
                        reads=[PSF[bank], xk], writes=[xk])
                p.op("dve", lambda E, xv=xv: E.tensor_add(xv, xv, bias_bc[:]),
                     reads=[xk, "bias_bc"], writes=[xk])
                emit_ln(xv, xk, dst, t0, it, f"s2dst{t0}_{it}", slot=it % 2)
        s2.close()

    def stage3(src, dst):
        s3 = contextlib.ExitStack()
        def lsb(name, shape, dt=F32):
            return s3.enter_context(nc.sbuf_tensor(name, list(shape), dt))

        xT_h = nc.dram_tensor("xT_h", [128, 8 * S], BF16, kind="Internal")
        qT_h = nc.dram_tensor("qT_h", [128, 8 * S], BF16, kind="Internal")
        kt_h = nc.dram_tensor("kt_h", [128, NT * D], BF16, kind="Internal")

        kT = lsb("kT", [128, 8 * S], BF16)
        vtok = lsb("vtok", [128, NT * D], BF16)
        wbuf = lsb("wbuf", [128, 8 * D], BF16)
        wgb = lsb("wgb", [128, 8 * 3], BF16)
        PcP = [lsb(f"PcA{j}", [128, NH * NB], BF16) for j in range(2)]
        PwP = [lsb(f"PwA{j}", [128, NH * 384], BF16) for j in range(2)]
        PsP = [lsb(f"PsA{j}", [128, NH * 16], BF16) for j in range(2)]
        ckT = lsb("ckT", [128, 8 * NB], BF16)
        cvt = lsb("cvt", [128, D], BF16)
        cvstg = lsb("cvstg", [8, D], BF16)
        selkT = lsb("selkT", [128, 8 * 16], BF16)
        selv = lsb("selv", [16, D], BF16)
        selacc = lsb("selacc", [128, 8 * 16])
        selvacc = lsb("selvacc", [16, D])
        mkc = lsb("mkc", [128, NT * NB], BF16)
        mkw = lsb("mkw", [128, 3 * 384], BF16)
        cosb = lsb("cosb", [128, NT * 32], BF16)
        sinb = lsb("sinb", [128, NT * 32], BF16)
        g_all = lsb("g_all", [128, NT * 4])
        RSP = [lsb(f"RS{j}", [128, 3 * NH]) for j in range(2)]
        RCP = [lsb(f"RC{j}", [128, 3 * NH]) for j in range(2)]
        qb8 = lsb("qb8", [128, 8])
        qb8b = lsb("qb8b", [128, 8], BF16)
        xtile = lsb("s3xt", [128, 2 * D])
        xTt = lsb("s3xTt", [128, 2 * 8 * 128], BF16)
        qTt = lsb("s3qTt", [128, 2 * 8 * 128], BF16)
        qraw = lsb("qraw", [128, D], BF16)
        qrope = lsb("qrope", [128, D], BF16)
        rsc1 = lsb("rsc1", [128, 512], BF16)
        rsc2 = lsb("rsc2", [128, 512], BF16)
        NMP = [lsb(f"NMA{j}", [128, 3 * NH]) for j in range(2)]
        pscr = [lsb(f"pscr{j}", [128, 384], BF16) for j in range(2)]
        ptb = [lsb(f"ptb{j}", [128, 384], BF16) for j in range(2)]
        attnTP = [lsb(f"attnTA{j}", [128, 8 * 128], BF16) for j in range(2)]
        imps = lsb("imps", [1, 4 * NB])
        m12 = lsb("m12", [1, 8])
        posr = lsb("posr", [1, 16])
        posb = lsb("posb", [128, 16])
        selmP = [lsb(f"selmA{j}", [128, 16]) for j in range(2)]
        selm = selmP[0]
        selmb = lsb("selmb", [128, 16], BF16)
        rcolP = [lsb(f"rcolA{j}", [128, 1]) for j in range(2)]
        rcol = rcolP[0]
        iota_row = lsb("iota_row", [1, 128])
        cpe_r = lsb("cpe_r", [1, 2 * D])
        wkv_c = lsb("wkv_c", [16, 2])
        bgr = lsb("bgr", [1, 4])
        bor = lsb("bor", [1, D])
        pep = lsb("pep", [16, D])
        pep2 = lsb("pep2", [16, D])
        wcmp = lsb("wcmp", [128, 16], BF16)

        p.op("pe", lambda E: E.transpose(psf[0][0:1, 0:128], iota_p[:], identf[:]),
             reads=["iota_p", "identf"], writes=[PSF[0]])
        p.op("dve", lambda E: E.tensor_copy(iota_row[:], psf[0][0:1, 0:128]),
             reads=[PSF[0]], writes=["iota_row"])
        for i in range(NT):
            p.dma("pool", lambda E, i=i: E.dma_start(
                mkc[:, i * NB:(i + 1) * NB], ext["mask_comp"].ap()[i]),
                writes=["mkc"])
        for vv_ in range(3):
            p.dma("pool", lambda E, vv_=vv_: E.dma_start(
                mkw[:, vv_ * 384:(vv_ + 1) * 384], ext["mask_win"].ap()[vv_]),
                writes=["mkw"])
        for i in range(NT):
            p.dma("pool", lambda E, i=i: E.dma_start(
                cosb[:, i * 32:(i + 1) * 32],
                ext["cos_t"].ap()[i * 128:(i + 1) * 128, :]), writes=["cosb"])
            p.dma("pool", lambda E, i=i: E.dma_start(
                sinb[:, i * 32:(i + 1) * 32],
                ext["sin_t"].ap()[i * 128:(i + 1) * 128, :]), writes=["sinb"])
        for dc in range(8):
            p.dma("pool", lambda E, dc=dc: E.dma_start(
                wgb[:, dc * 3:(dc + 1) * 3],
                ext["Wg"].ap()[dc * 128:(dc + 1) * 128, :]), writes=["wgb"])
        p.dma("sp", lambda E: E.dma_start(bgr[0:1, 0:3], ext["bg"].ap()[None, :]),
              writes=["bgr"])
        p.dma("sp", lambda E: E.dma_start(bor[0:1, :], ext["bo"].ap()[None, :]),
              writes=["bor"])
        p.dma("sp", lambda E: E.dma_start(pep[:], ext["peP"].ap()), writes=["pep"])
        p.dma("sp", lambda E: E.dma_start(pep2[:], ext["pe_raw"].ap()), writes=["pep2"])
        p.dma("sp", lambda E: E.dma_start(
            wkv_c[:, 0:1], ext["wk_comp"].ap()[:, None]), writes=["wkv"])
        p.dma("sp", lambda E: E.dma_start(
            wkv_c[:, 1:2], ext["wv_comp"].ap()[:, None]), writes=["wkv"])
        p.dma("pool", lambda E: E.dma_start(wcmp[:, 0:8], ext["Wck"].ap()), writes=["wcmp"])
        p.dma("pool", lambda E: E.dma_start(wcmp[:, 8:16], ext["Wcv"].ap()), writes=["wcmp"])
        for which in range(2):
            for hh in range(2):
                bank = which
                p.op("pe", lambda E, which=which, hh=hh, bank=bank: E.matmul(
                    psf[bank][0:1, :],
                    wkv_c[:, which:which + 1],
                    (pep if which == 0 else pep2)[:, hh * 512:(hh + 1) * 512],
                    start=True, stop=True),
                    reads=["wkv", "pep", "pep2"], writes=[PSF[bank]])
                p.op("dve", lambda E, which=which, hh=hh, bank=bank: E.tensor_copy(
                    cpe_r[0:1, which * D + hh * 512:which * D + (hh + 1) * 512],
                    psf[bank][0:1, :]),
                    reads=[PSF[bank]], writes=["cpe_r"])

        def load_w(name, wkey=None):
            for dc in range(8):
                p.dma("pool", lambda E, dc=dc, name=name: E.dma_start(
                    wbuf[:, dc * D:(dc + 1) * D],
                    ext[name].ap()[dc * 128:(dc + 1) * 128, :]),
                    writes=["wbuf"])

        def rope(i, srcb, srck, dstb, dstk):
            cs = cosb[:, i * 32:(i + 1) * 32].unsqueeze(1).broadcast_to([128, 16, 32])
            sn = sinb[:, i * 32:(i + 1) * 32].unsqueeze(1).broadcast_to([128, 16, 32])
            e = srcb[:].rearrange("p (h c) -> p h c", c=64)[:, :, 0:32]
            o = srcb[:].rearrange("p (h c) -> p h c", c=64)[:, :, 32:64]
            de = dstb[:].rearrange("p (h c) -> p h c", c=64)[:, :, 0:32]
            do = dstb[:].rearrange("p (h c) -> p h c", c=64)[:, :, 32:64]
            r1 = rsc1[:].rearrange("p (h c) -> p h c", c=32)
            r2 = rsc2[:].rearrange("p (h c) -> p h c", c=32)
            p.op("dve", lambda E, e=e, cs=cs, r1=r1: E.tensor_mul(r1, e, cs),
                 reads=[srck, "cosb"], writes=["rsc1"])
            p.op("dve", lambda E, o=o, sn=sn, r2=r2: E.tensor_mul(r2, o, sn),
                 reads=[srck, "sinb"], writes=["rsc2"])
            p.op("dve", lambda E, de=de, r1=r1, r2=r2: E.tensor_sub(de, r1, r2),
                 reads=["rsc1", "rsc2"], writes=[dstk])
            p.op("dve", lambda E, e=e, sn=sn, r1=r1: E.tensor_mul(r1, e, sn),
                 reads=[srck, "sinb"], writes=["rsc1"])
            p.op("dve", lambda E, o=o, cs=cs, r2=r2: E.tensor_mul(r2, o, cs),
                 reads=[srck, "cosb"], writes=["rsc2"])
            p.op("dve", lambda E, do=do, r1=r1, r2=r2: E.tensor_add(do, r1, r2),
                 reads=["rsc1", "rsc2"], writes=[dstk])

        def proj_tile(i, outb, outk):
            tb = i % 2
            xTk = f"s3xTl{tb}"
            p.dma("sp", lambda E, tb=tb, i=i: E.dma_start(
                xTt[:, tb * 1024:(tb + 1) * 1024].rearrange("p (c t) -> p c t", c=8),
                xT_h.ap().rearrange("p (c t) -> p c t", c=8)
                [:, :, i * 128:(i + 1) * 128]),
                writes=[xTk])
            for hh in range(2):
                bank = hh
                fns = []
                for dc in range(8):
                    fns.append(lambda E, dc=dc, tb=tb, hh=hh, bank=bank: E.matmul(
                        psf[bank][:],
                        xTt[:, tb * 1024 + dc * 128:tb * 1024 + (dc + 1) * 128],
                        wbuf[:, dc * D + hh * 512:dc * D + (hh + 1) * 512],
                        start=(dc == 0), stop=(dc == 7)))
                p.ops("pe", fns, reads=[xTk, "wbuf"], writes=[PSF[bank]])
            for hh in range(2):
                p.op("act", lambda E, hh=hh, outb=outb: E.copy(
                    outb[:, hh * 512:(hh + 1) * 512], psf[hh][:]),
                    reads=[PSF[hh]], writes=[outk])
            return tb

        for s in range(SEQ_PER_CORE):
            t0 = s * S
            for i in range(NT):
                tb = i % 2
                xv = xtile[:, tb * D:(tb + 1) * D]
                xk = f"s3x{tb}"
                p.dma("sp", lambda E, xv=xv, t0=t0, i=i: E.dma_start(
                    xv, src.ap()[t0 + i * 128:t0 + (i + 1) * 128, :]), writes=[xk])
                p.op("dve", lambda E, xv=xv: E.tensor_copy(qraw[:], xv),
                     reads=[xk], writes=["qraw"])
                fns = []
                for dc in range(8):
                    fns.append(lambda E, dc=dc, tb=tb: E.transpose(
                        psb[tb][:, dc * 128:(dc + 1) * 128],
                        qraw[:, dc * 128:(dc + 1) * 128], identb[:]))
                p.ops("pe", fns, reads=["qraw", "identb"], writes=[PSB[tb]])
                xTk = f"s3xTl{tb}"
                p.op("dve", lambda E, tb=tb: E.tensor_copy(
                    xTt[:, tb * 1024:(tb + 1) * 1024], psb[tb][:]),
                    reads=[PSB[tb]], writes=[xTk])
                p.dma("sp", lambda E, tb=tb, i=i: E.dma_start(
                    xT_h.ap().rearrange("p (c t) -> p c t", c=8)
                    [:, :, i * 128:(i + 1) * 128],
                    xTt[:, tb * 1024:(tb + 1) * 1024].rearrange(
                        "p (c t) -> p c t", c=8)),
                    reads=[xTk], writes=[f"xTh{i}"])
                fns = []
                for dc in range(8):
                    fns.append(lambda E, dc=dc, tb=tb: E.matmul(
                        psf[4][:, 0:3],
                        xTt[:, tb * 1024 + dc * 128:tb * 1024 + (dc + 1) * 128],
                        wgb[:, dc * 3:(dc + 1) * 3],
                        start=(dc == 0), stop=False))
                fns.append(lambda E: E.matmul(
                    psf[4][:, 0:3], ones_row[0:1, :], bgr[0:1, 0:3],
                    start=False, stop=True))
                p.ops("pe", fns, reads=[xTk, "wgb", "bgr", "ones_row"],
                      writes=[PSF[4]])
                gm = cols[:, 9:10]
                gd = cols[:, 10:11]
                p.op("dve", lambda E: E.reduce_max(gm, psf[4][:, 0:3], axis=AX),
                     reads=[PSF[4]], writes=["s3gm"])
                p.op("dve", lambda E: E.tensor_scalar_mul(gm, gm, -1.0),
                     sreads=["s3gm"], writes=["s3gm"])
                gv = g_all[:, i * 4:i * 4 + 3]
                p.op("act", lambda E, gv=gv: E.activation(
                    gv, psf[4][:, 0:3], ACT.Exp, bias=gm, accum_out=gd),
                    reads=[PSF[4], "s3gm"], writes=[f"s3g{i}", "s3gd"])
                p.op("dve", lambda E: E.reciprocal(gd, gd),
                     reads=["s3gd"], writes=["s3gd"])
                p.op("dve", lambda E, gv=gv: E.tensor_scalar_mul(gv, gv, gd),
                     reads=[f"s3g{i}"], sreads=["s3gd"], writes=[f"s3g{i}"])
            load_w("WqP")
            qbar = cols[:, 11:12]
            p.op("dve", lambda E: E.memset(qb8[:], 0.0), writes=["qbar8"])
            for i in range(NT):
                tb = proj_tile(i, qraw, "qraw")
                rope(i, qraw, "qraw", qrope, "qrope")
                fns = []
                for dc in range(8):
                    fns.append(lambda E, dc=dc, tb=tb: E.transpose(
                        psb[tb][:, dc * 128:(dc + 1) * 128],
                        qrope[:, dc * 128:(dc + 1) * 128], identb[:]))
                p.ops("pe", fns, reads=["qrope", "identb"], writes=[PSB[tb]])
                qTk = f"s3qT{tb}"
                p.op("dve", lambda E, tb=tb: E.tensor_copy(
                    qTt[:, tb * 1024:(tb + 1) * 1024], psb[tb][:]),
                    reads=[PSB[tb]], writes=[qTk])
                p.dma("sp", lambda E, tb=tb, i=i: E.dma_start(
                    qT_h.ap().rearrange("p (c t) -> p c t", c=8)
                    [:, :, i * 128:(i + 1) * 128],
                    qTt[:, tb * 1024:(tb + 1) * 1024].rearrange(
                        "p (c t) -> p c t", c=8)),
                    reads=[qTk], writes=[f"qTh{i}"])
                for dc in range(8):
                    p.op("dve", lambda E, tb=tb, dc=dc: E.reduce_sum(
                        qbar, qTt[:, tb * 1024 + dc * 128:tb * 1024 + (dc + 1) * 128],
                        axis=AX), reads=[qTk], writes=["s3qbar"])
                    p.op("dve", lambda E, dc=dc: E.tensor_add(
                        qb8[:, dc:dc + 1], qb8[:, dc:dc + 1], qbar),
                        reads=["qbar8"], sreads=["s3qbar"], writes=["qbar8"])
            load_w("Wv")
            for i in range(NT):
                tb = i % 2
                vk = f"s3v{i}"
                proj_tile(i, qraw, "qraw")
                p.op("dve", lambda E, i=i: E.tensor_copy(
                    vtok[:, i * D:(i + 1) * D], qraw[:]),
                    reads=["qraw"], writes=[vk])
                for hh in range(2):
                    bank = 2 + hh
                    fns = [lambda E, i=i, hh=hh, bank=bank: E.matmul(
                        psf[bank][0:8, :],
                        wcmp[:, 8:16],
                        vtok[:, i * D + hh * 512:i * D + (hh + 1) * 512],
                        start=True, stop=False),
                        lambda E, hh=hh, bank=bank: E.matmul(
                        psf[bank][0:8, :], ones_row[0:1, 0:8],
                        cpe_r[0:1, D + hh * 512:D + (hh + 1) * 512],
                        start=False, stop=True)]
                    p.ops("pe", fns, reads=[vk, "wcmp", "ones_row", "cpe_r"],
                          writes=[PSF[bank]])
                    p.op("dve", lambda E, i=i, hh=hh, bank=bank: E.tensor_copy(
                        cvstg[:, hh * 512:(hh + 1) * 512],
                        psf[bank][0:8, :]),
                        reads=[PSF[bank]], writes=["cvstg"])
                p.dma("pool", lambda E, i=i: E.dma_start(
                    cvt[8 * i:8 * i + 8, :], cvstg[:]),
                    reads=["cvstg"], writes=["cvt"])
            load_w("WkP")
            for i in range(NT):
                tb = i % 2
                proj_tile(i, qraw, "qraw")
                rope(i, qraw, "qraw", qrope, "qrope")
                p.dma("sp", lambda E, i=i: E.dma_start(
                    kt_h.ap()[:, i * D:(i + 1) * D], qrope[:]),
                    reads=["qrope"], writes=[f"kth{i}"])
                fns = []
                for dc in range(8):
                    fns.append(lambda E, dc=dc, tb=tb: E.transpose(
                        psb[tb][:, dc * 128:(dc + 1) * 128],
                        qrope[:, dc * 128:(dc + 1) * 128], identb[:]))
                p.ops("pe", fns, reads=["qrope", "identb"], writes=[PSB[tb]])
                p.op("dve", lambda E, tb=tb, i=i: E.tensor_copy(
                    kT[:].rearrange("p (c t) -> p c t", c=8)
                    [:, :, i * 128:(i + 1) * 128],
                    psb[tb][:].rearrange("p (c t) -> p c t", c=8)),
                    reads=[PSB[tb]], writes=[f"kTt{i}"])
                fns = []
                for dc in range(8):
                    fns.append(lambda E, i=i, dc=dc: E.matmul(
                        psf[4][:, dc * 8:(dc + 1) * 8],
                        qrope[:, dc * 128:(dc + 1) * 128],
                        wcmp[:, 0:8],
                        start=True, stop=False))
                    fns.append(lambda E, dc=dc: E.matmul(
                        psf[4][:, dc * 8:(dc + 1) * 8],
                        cpe_r[0:1, dc * 128:(dc + 1) * 128],
                        ones_row[0:1, 0:8],
                        start=False, stop=True))
                p.ops("pe", fns, reads=["qrope", "wcmp", "ones_row", "cpe_r"],
                      writes=[PSF[4]])
                p.op("dve", lambda E, i=i: E.tensor_copy(
                    ckT[:].rearrange("p (c n) -> p c n", c=8)[:, :, 8 * i:8 * i + 8],
                    psf[4][:, 0:64].rearrange("p (c n) -> p c n", n=8)),
                    reads=[PSF[4]], writes=["ckT"])
            p.op("dve", lambda E: E.tensor_copy(qb8b[:], qb8[:]),
                 reads=["qbar8"], writes=["qb8b"])
            fns = []
            for dc in range(8):
                fns.append(lambda E, dc=dc: E.matmul(
                    psf[0][0:1, 0:NB],
                    qb8b[:, dc:dc + 1],
                    ckT[:, dc * NB:(dc + 1) * NB],
                    start=(dc == 0), stop=(dc == 7)))
            p.ops("pe", fns, reads=["qb8b", "ckT"], writes=[PSF[0]])
            imp = imps[0:1, 0:NB]
            d1 = imps[0:1, NB:2 * NB]
            e1 = imps[0:1, 2 * NB:3 * NB]
            p.op("dve", lambda E: E.tensor_copy(imp, psf[0][0:1, 0:NB]),
                 reads=[PSF[0]], writes=["imp"])
            m1 = m12[0:1, 0:1]; m2 = m12[0:1, 1:2]
            i1v = m12[0:1, 2:3]; i2v = m12[0:1, 3:4]
            p.op("dve", lambda E: E.reduce_max(m1, imp, axis=AX),
                 reads=["imp"], writes=["m1"])
            p.op("dve", lambda E: E.tensor_scalar(d1, imp, m1, -1.0e12,
                                                  ALU.subtract, ALU.mult),
                 reads=["imp"], sreads=["m1"], writes=["d1"])
            p.op("dve", lambda E: E.tensor_add(d1, d1, iota_row[:]),
                 reads=["iota_row"], sreads=["d1"], writes=["d1"])
            p.op("dve", lambda E: E.tensor_reduce(i1v, d1, axis=AX, op=ALU.min),
                 reads=["d1"], writes=["i1v"])
            p.op("dve", lambda E: E.tensor_scalar(e1, imp, m1, -1.0e18,
                                                  ALU.is_ge, ALU.mult),
                 reads=["imp"], sreads=["m1"], writes=["e1"])
            p.op("dve", lambda E: E.tensor_add(e1, e1, imp),
                 reads=["imp"], sreads=["e1"], writes=["e1"])
            p.op("dve", lambda E: E.reduce_max(m2, e1, axis=AX),
                 reads=["e1"], writes=["m2"])
            p.op("dve", lambda E: E.tensor_scalar(d1, e1, m2, -1.0e12,
                                                  ALU.subtract, ALU.mult),
                 reads=["e1"], sreads=["m2"], writes=["d1"])
            p.op("dve", lambda E: E.tensor_add(d1, d1, iota_row[:]),
                 reads=["iota_row"], sreads=["d1"], writes=["d1"])
            p.op("dve", lambda E: E.tensor_reduce(i2v, d1, axis=AX, op=ALU.min),
                 reads=["d1"], writes=["i2v"])
            p.op("dve", lambda E: E.tensor_scalar_mul(i1v, i1v, 16.0),
                 sreads=["i1v"], writes=["i1v"])
            p.op("dve", lambda E: E.tensor_scalar_mul(i2v, i2v, 16.0),
                 sreads=["i2v"], writes=["i2v"])
            p.op("dve", lambda E: E.tensor_scalar(
                posr[0:1, 0:8], iota_row[0:1, 0:8], i1v, None, ALU.add),
                reads=["iota_row"], sreads=["i1v"], writes=["posr"])
            p.op("dve", lambda E: E.tensor_scalar(
                posr[0:1, 8:16], iota_row[0:1, 0:8], i2v, None, ALU.add),
                reads=["iota_row"], sreads=["i2v"], writes=["posr"])
            for hh in range(1):
                p.op("pe", lambda E: E.matmul(
                    psf[1][:, 0:16], ones_row[0:1, :], posr[0:1, :],
                    start=True, stop=True),
                    reads=["posr", "ones_row"], writes=[PSF[1]])
            p.op("dve", lambda E: E.tensor_copy(posb[:], psf[1][:, 0:16]),
                 reads=[PSF[1]], writes=["posb"])
            p.op("dve", lambda E: E.memset(selacc[:], 0.0), writes=["selacc"])
            p.op("dve", lambda E: E.memset(selvacc[:], 0.0), writes=["selvacc"])
            for i in range(NT):
                p.dma("sp", lambda E, i=i: E.dma_start(
                    qraw[:], kt_h.ap()[:, i * D:(i + 1) * D]),
                    writes=["qraw"])
                gk = "s3G"
                p.op("dve", lambda E, i=i: E.tensor_scalar(
                    rcol[:], iota_p[:], 1.0, float(i * 128), ALU.mult, ALU.add),
                    reads=["iota_p"], writes=["s3rcol"])
                p.op("dve", lambda E: E.tensor_scalar(
                    selmb[:], posb[:], rcol[:, 0:1], None, ALU.is_equal),
                    reads=["posb"], sreads=["s3rcol"], writes=[gk])
                fns = []
                for dc in range(8):
                    fns.append(lambda E, i=i, dc=dc: E.matmul(
                        psf[2][:, dc * 16:(dc + 1) * 16],
                        qraw[:, dc * 128:(dc + 1) * 128],
                        selmb[:],
                        start=(dc == 0), stop=(dc == 7)))
                p.ops("pe", fns, reads=["qraw", gk], writes=[PSF[2]])
                p.op("dve", lambda E: E.tensor_add(
                    selacc[:], selacc[:], psf[2][:, 0:128]),
                    reads=[PSF[2], "selacc"], writes=["selacc"])
                for hh in range(2):
                    p.op("pe", lambda E, i=i, hh=hh: E.matmul(
                        psf[3][0:16, 0:512] if hh == 0 else psf[5][0:16, 0:512],
                        selmb[:],
                        vtok[:, i * D + hh * 512:i * D + (hh + 1) * 512],
                        start=True, stop=True),
                        reads=[gk, f"s3v{i}"],
                        writes=[PSF[3] if hh == 0 else PSF[5]])
                    p.op("dve", lambda E, hh=hh: E.tensor_add(
                        selvacc[0:16, hh * 512:(hh + 1) * 512],
                        selvacc[0:16, hh * 512:(hh + 1) * 512],
                        (psf[3] if hh == 0 else psf[5])[0:16, :]),
                        reads=[PSF[3] if hh == 0 else PSF[5], "selvacc"],
                        writes=["selvacc"])
            p.op("dve", lambda E: E.tensor_copy(selkT[:], selacc[:]),
                 reads=["selacc"], writes=["selkT"])
            p.op("dve", lambda E: E.tensor_copy(selv[:], selvacc[0:16, :]),
                 reads=["selvacc"], writes=["selv"])
            load_w("Wo")
            for i in range(NT):
                tb = i % 2
                qTk = f"s3qTl{tb}"
                p.dma("sp", lambda E, tb=tb, i=i: E.dma_start(
                    qTt[:, tb * 1024:(tb + 1) * 1024].rearrange(
                        "p (c t) -> p c t", c=8),
                    qT_h.ap().rearrange("p (c t) -> p c t", c=8)
                    [:, :, i * 128:(i + 1) * 128]),
                    writes=[qTk])
                if i == 0:
                    W0, wlen, mv = 0, 128, 0
                elif i == 1:
                    W0, wlen, mv = 0, 256, 1
                else:
                    W0, wlen, mv = (i - 2) * 128, 384, 2
                ip = i % 2
                Pc, Pw, Ps = PcP[ip], PwP[ip], PsP[ip]
                mskC, mskW, mskS = Pc, Pw, Ps
                NM, RS, RC = NMP[ip], RSP[ip], RCP[ip]
                attnT = attnTP[ip]
                selmv = selmP[ip]
                rcolv = rcolP[ip]
                X_ = lambda k, ip=ip: f"{k}_{ip}"
                Pc3 = Pc[:].rearrange("p (h n) -> p h n", n=NB)
                Pw3f = Pw[:].rearrange("p (h w) -> p h w", w=384)
                Ps3 = Ps[:, 0:NH * 16].rearrange("p (h n) -> p h n", n=16)
                o3 = lambda ap: ap.rearrange("p (h o) -> p h o", o=1)
                p.op("dve", lambda E, i=i, rcolv=rcolv: E.tensor_scalar(
                    rcolv[:], iota_p[:], 1.0, float(i * 128), ALU.mult, ALU.add),
                    reads=["iota_p"], writes=[X_("s3rcol")])
                p.op("dve", lambda E, selmv=selmv, rcolv=rcolv: E.tensor_scalar(
                    selmv[:], posb[:], rcolv[:, 0:1], None, ALU.is_le),
                    reads=["posb"], sreads=[X_("s3rcol")], writes=[X_("s3selm")])
                p.op("dve", lambda E, selmv=selmv: E.tensor_scalar(
                    selmv[:], selmv[:], 3.0e38, NEG, ALU.mult, ALU.add),
                    sreads=[X_("s3selm")], writes=[X_("s3selm")])
                for h in range(NH):
                    qsl = qTt[(h % 2) * 64:(h % 2) * 64 + 64,
                              tb * 1024 + (h // 2) * 128:
                              tb * 1024 + (h // 2) * 128 + 128]
                    bank = h % 4
                    p.op("pe", lambda E, h=h, bank=bank, qsl=qsl: E.matmul(
                        psf[bank][:, 0:NB],
                        qsl,
                        ckT[(h % 2) * 64:(h % 2) * 64 + 64,
                            (h // 2) * NB:(h // 2) * NB + NB],
                        start=True, stop=True),
                        reads=[qTk, "ckT"], writes=[PSF[bank]])
                    p.op("dve", lambda E, i=i, h=h, bank=bank, mskC=mskC: E.scalar_tensor_tensor(
                        mskC[:, h * NB:(h + 1) * NB], psf[bank][:, 0:NB], SCALE_ATT,
                        mkc[:, i * NB:(i + 1) * NB], ALU.mult, ALU.add),
                        reads=[PSF[bank], "mkc"], writes=[X_("Pc") + str(h)])
                    p.op("dve", lambda E, h=h, mskC=mskC, NM=NM: E.tensor_reduce(
                        NM[:, h:h + 1], mskC[:, h * NB:(h + 1) * NB], axis=AX,
                        op=ALU.max, negate=True),
                        reads=[X_("Pc") + str(h)], writes=[X_("nmc") + str(h)])
                    p.op("act", lambda E, h=h, Pc=Pc, mskC=mskC, NM=NM, RS=RS: E.activation(
                        Pc[:, h * NB:(h + 1) * NB], mskC[:, h * NB:(h + 1) * NB],
                        ACT.Exp, bias=NM[:, h:h + 1], accum_out=RS[:, h:h + 1]),
                        reads=[X_("Pc") + str(h), X_("nmc") + str(h)], writes=[X_("Pc") + str(h), X_("RSc") + str(h)])
                    bank = (h + 1) % 4
                    p.op("pe", lambda E, h=h, bank=bank, qsl=qsl, W0=W0,
                         wlen=wlen: E.matmul(
                        psf[bank][:, 0:wlen],
                        qsl,
                        kT[(h % 2) * 64:(h % 2) * 64 + 64,
                           (h // 2) * S + W0:(h // 2) * S + W0 + wlen],
                        start=True, stop=True),
                        reads=[qTk, f"kTt{i}", f"kTt{max(i-1,0)}", f"kTt{max(i-2,0)}"],
                        writes=[PSF[bank]])
                    p.op("dve", lambda E, h=h, bank=bank, mv=mv, wlen=wlen, mskW=mskW: E.scalar_tensor_tensor(
                        mskW[:, h * 384:h * 384 + wlen], psf[bank][:, 0:wlen], SCALE_ATT,
                        mkw[:, mv * 384:mv * 384 + wlen], ALU.mult, ALU.add),
                        reads=[PSF[bank], "mkw"], writes=[X_("Pw") + str(h)])
                    p.op("dve", lambda E, h=h, wlen=wlen, mskW=mskW, NM=NM: E.tensor_reduce(
                        NM[:, NH + h:NH + h + 1], mskW[:, h * 384:h * 384 + wlen],
                        axis=AX, op=ALU.max, negate=True),
                        reads=[X_("Pw") + str(h)], writes=[X_("nmw") + str(h)])
                    p.op("act", lambda E, h=h, wlen=wlen, Pw=Pw, mskW=mskW, NM=NM, RS=RS: E.activation(
                        Pw[:, h * 384:h * 384 + wlen], mskW[:, h * 384:h * 384 + wlen],
                        ACT.Exp, bias=NM[:, NH + h:NH + h + 1],
                        accum_out=RS[:, NH + h:NH + h + 1]),
                        reads=[X_("Pw") + str(h), X_("nmw") + str(h)], writes=[X_("Pw") + str(h), X_("RSw") + str(h)])
                    bank = (h + 2) % 4
                    p.op("pe", lambda E, h=h, bank=bank, qsl=qsl: E.matmul(
                        psf[bank][:, 0:16],
                        qsl,
                        selkT[(h % 2) * 64:(h % 2) * 64 + 64,
                              (h // 2) * 16:(h // 2) * 16 + 16],
                        start=True, stop=True),
                        reads=[qTk, "selkT"], writes=[PSF[bank]])
                    p.op("dve", lambda E, h=h, bank=bank, mskS=mskS, selmv=selmv: E.scalar_tensor_tensor(
                        mskS[:, h * 16:(h + 1) * 16], psf[bank][:, 0:16], SCALE_ATT,
                        selmv[:], ALU.mult, ALU.add),
                        reads=[PSF[bank], X_("s3selm")], writes=[X_("Ps") + str(h)])
                    p.op("dve", lambda E, h=h, mskS=mskS, NM=NM: E.tensor_reduce(
                        NM[:, 2 * NH + h:2 * NH + h + 1], mskS[:, h * 16:(h + 1) * 16],
                        axis=AX, op=ALU.max, negate=True),
                        reads=[X_("Ps") + str(h)], writes=[X_("nms") + str(h)])
                    p.op("act", lambda E, h=h, Ps=Ps, mskS=mskS, NM=NM, RS=RS: E.activation(
                        Ps[:, h * 16:(h + 1) * 16], mskS[:, h * 16:(h + 1) * 16],
                        ACT.Exp, bias=NM[:, 2 * NH + h:2 * NH + h + 1],
                        accum_out=RS[:, 2 * NH + h:2 * NH + h + 1]),
                        reads=[X_("Ps") + str(h), X_("nms") + str(h)], writes=[X_("Ps") + str(h), X_("RSs") + str(h)])
                rsck = [X_("RSc") + str(h) for h in range(NH)]
                rswk = [X_("RSw") + str(h) for h in range(NH)]
                rssk = [X_("RSs") + str(h) for h in range(NH)]
                nmck = [X_("nmc") + str(h) for h in range(NH)]
                nmsk = [X_("nms") + str(h) for h in range(NH)]
                p.op("dve", lambda E, RS=RS, RC=RC: E.reciprocal(RC[:, 0:NH], RS[:, 0:NH]),
                     reads=rsck, writes=[X_(X_("RCc"))])
                p.op("dve", lambda E, RS=RS, RC=RC: E.reciprocal(RC[:, NH:2 * NH], RS[:, NH:2 * NH]),
                     reads=rswk, writes=[X_(X_("RCw"))])
                p.op("dve", lambda E, RS=RS, RC=RC: E.reciprocal(RC[:, 2 * NH:3 * NH],
                                                   RS[:, 2 * NH:3 * NH]),
                     reads=rssk, writes=[X_(X_("RCs"))])
                p.op("dve", lambda E, i=i, RC=RC: E.tensor_scalar_mul(
                    RC[:, 0:NH], RC[:, 0:NH], g_all[:, i * 4:i * 4 + 1]),
                    reads=[f"s3g{i}"], sreads=[X_(X_("RCc"))], writes=[X_(X_("RCc"))])
                p.op("dve", lambda E, i=i, RC=RC: E.tensor_scalar_mul(
                    RC[:, NH:2 * NH], RC[:, NH:2 * NH], g_all[:, i * 4 + 2:i * 4 + 3]),
                    reads=[f"s3g{i}"], sreads=[X_(X_("RCw"))], writes=[X_(X_("RCw"))])
                p.op("dve", lambda E, i=i, RC=RC: E.tensor_scalar_mul(
                    RC[:, 2 * NH:3 * NH], RC[:, 2 * NH:3 * NH],
                    g_all[:, i * 4 + 1:i * 4 + 2]),
                    reads=[f"s3g{i}"], sreads=[X_(X_("RCs"))], writes=[X_(X_("RCs"))])
                p.op("dve", lambda E, NM=NM: E.tensor_scalar(
                    NM[:, 0:NH], NM[:, 0:NH], 1.0e37, None, ALU.is_lt),
                    reads=nmck, writes=[X_(X_("VMc"))] + nmck)
                p.op("dve", lambda E, NM=NM: E.tensor_scalar(
                    NM[:, 2 * NH:3 * NH], NM[:, 2 * NH:3 * NH], 1.0e37, None,
                    ALU.is_lt), reads=nmsk, writes=[X_(X_("VMs"))] + nmsk)
                p.op("dve", lambda E, NM=NM, RC=RC: E.tensor_mul(
                    RC[:, 0:NH], RC[:, 0:NH], NM[:, 0:NH]),
                    reads=[X_(X_("VMc"))], sreads=[X_(X_("RCc"))], writes=[X_(X_("RCc"))])
                p.op("dve", lambda E, NM=NM, RC=RC: E.tensor_mul(
                    RC[:, 2 * NH:3 * NH], RC[:, 2 * NH:3 * NH], NM[:, 2 * NH:3 * NH]),
                    reads=[X_(X_("VMs"))], sreads=[X_(X_("RCs"))], writes=[X_(X_("RCs"))])
                pck = [X_("Pc") + str(h) for h in range(NH)]
                pwk2 = [X_("Pw") + str(h) for h in range(NH)]
                psk = [X_("Ps") + str(h) for h in range(NH)]
                p.op("dve", lambda E, Pc3=Pc3, o3=o3, RC=RC: E.tensor_mul(
                    Pc3, Pc3, o3(RC[:, 0:NH]).broadcast_to([128, NH, NB])),
                    reads=pck + [X_(X_("RCc"))], writes=[X_("PcS")])
                p.op("dve", lambda E, Pw3f=Pw3f, o3=o3, wlen=wlen, RC=RC: E.tensor_mul(
                    Pw3f[:, :, 0:wlen], Pw3f[:, :, 0:wlen],
                    o3(RC[:, NH:2 * NH]).broadcast_to([128, NH, wlen])),
                    reads=pwk2 + [X_(X_("RCw"))], writes=[X_("PwS")])
                p.op("dve", lambda E, Ps3=Ps3, o3=o3, RC=RC: E.tensor_mul(
                    Ps3, Ps3, o3(RC[:, 2 * NH:3 * NH]).broadcast_to([128, NH, 16])),
                    reads=psk + [X_(X_("RCs"))], writes=[X_("PsS")])
                for h in range(NH):
                    ob = 4 + (h // 2) % 2
                    obp = (h % 2) * 64
                    pj = h % 2
                    pt_ = ptb[pj]
                    ptk = f"s3pt{pj}"
                    tbank = pj
                    p.op("pe", lambda E, h=h, Pc=Pc, tbank=tbank: E.transpose(
                        psb[tbank][:, 0:128], Pc[:, h * NB:(h + 1) * NB],
                        identb[:]),
                        reads=[X_("PcS"), "identb"], writes=[PSB[tbank]])
                    p.op("act", lambda E, pt_=pt_, tbank=tbank: E.copy(
                        pt_[:, 0:128], psb[tbank][:, 0:128]),
                        reads=[PSB[tbank]], writes=[ptk])
                    p.op("pe", lambda E, h=h, ob=ob, obp=obp, pt_=pt_: E.matmul(
                        psf[ob][obp:obp + 64, 0:128],
                        cvt[:, h * 64:(h + 1) * 64],
                        pt_[:, 0:128],
                        start=True, stop=False,
                        tile_position=(0, obp)),
                        reads=[ptk, "cvt"], writes=[PSF[ob]])
                    p.op("pe", lambda E, h=h, Ps=Ps, tbank=tbank: E.transpose(
                        psb[tbank][0:16, 128:256], Ps[:, h * 16:(h + 1) * 16],
                        identb[:]),
                        reads=[X_("PsS"), "identb"], writes=[PSB[tbank]])
                    p.op("act", lambda E, pt_=pt_, tbank=tbank: E.copy(
                        pt_[0:16, 128:256], psb[tbank][0:16, 128:256]),
                        reads=[PSB[tbank]], writes=[ptk])
                    p.op("pe", lambda E, h=h, ob=ob, obp=obp, pt_=pt_: E.matmul(
                        psf[ob][obp:obp + 64, 0:128],
                        selv[0:16, h * 64:(h + 1) * 64],
                        pt_[0:16, 128:256],
                        start=False, stop=False,
                        tile_position=(0, obp)),
                        reads=[ptk, "selv"], writes=[PSF[ob]])
                    nwc = wlen // 128
                    fns = []
                    for wc in range(nwc):
                        fns.append(lambda E, h=h, wc=wc, tbank=tbank, Pw=Pw: E.transpose(
                            psb[tbank][:, 256 + wc * 128:256 + (wc + 1) * 128],
                            Pw[:, h * 384 + wc * 128:h * 384 + (wc + 1) * 128],
                            identb[:]))
                    p.ops("pe", fns, reads=[X_("PwS"), "identb"], writes=[PSB[tbank]])
                    p.op("act", lambda E, nwc=nwc, pt_=pt_, tbank=tbank: E.copy(
                        pt_[:, 0:nwc * 128], psb[tbank][:, 256:256 + nwc * 128]),
                        reads=[PSB[tbank]], writes=[ptk])
                    fns = []
                    for wc in range(nwc):
                        kt_tile = (W0 // 128) + wc
                        fns.append(lambda E, h=h, ob=ob, obp=obp, wc=wc,
                                   kt_tile=kt_tile, nwc=nwc, pt_=pt_: E.matmul(
                            psf[ob][obp:obp + 64, 0:128],
                            vtok[:, kt_tile * D + h * 64:kt_tile * D + (h + 1) * 64],
                            pt_[:, wc * 128:(wc + 1) * 128],
                            start=False, stop=(wc == nwc - 1),
                            tile_position=(0, obp)))
                    p.ops("pe", fns,
                          reads=[ptk] + [f"s3v{(W0 // 128) + wc}" for wc in range(nwc)],
                          writes=[PSF[ob]])
                    if h % 2 == 1:
                        pr = h // 2
                        p.op("act", lambda E, pr=pr, ob=ob, attnT=attnT: E.copy(
                            attnT[:, pr * 128:(pr + 1) * 128], psf[ob][:, 0:128]),
                            reads=[PSF[ob]], writes=[X_("attnT")])
                xv = xtile[:, tb * D:(tb + 1) * D]
                xk = f"s3x{tb}"
                p.dma("sp", lambda E, xv=xv, t0=t0, i=i, attnT=attnT: E.dma_start(
                    xv, src.ap()[t0 + i * 128:t0 + (i + 1) * 128, :]), writes=[xk])
                for hh in range(2):
                    bank = hh
                    fns = []
                    for fc in range(8):
                        fns.append(lambda E, fc=fc, hh=hh, bank=bank, attnT=attnT: E.matmul(
                            psf[bank][:],
                            attnT[:, fc * 128:(fc + 1) * 128],
                            wbuf[:, fc * D + hh * 512:fc * D + (hh + 1) * 512],
                            start=(fc == 0), stop=False))
                    fns.append(lambda E, hh=hh, bank=bank, attnT=attnT: E.matmul(
                        psf[bank][:], ones_row[0:1, :],
                        bor[0:1, hh * 512:(hh + 1) * 512],
                        start=False, stop=True))
                    p.ops("pe", fns, reads=[X_("attnT"), "wbuf", "bor", "ones_row"],
                          writes=[PSF[bank]])
                for hh in range(2):
                    p.op("dve", lambda E, xv=xv, hh=hh: E.tensor_add(
                        xv[:, hh * 512:(hh + 1) * 512],
                        xv[:, hh * 512:(hh + 1) * 512], psf[hh][:]),
                        reads=[PSF[hh], xk], writes=[xk])
                emit_ln(xv, xk, dst, t0, i, f"s3dst{t0}_{i}", slot=i % 2)
        s3.close()

    def stage4(src, dst):
        s4 = contextlib.ExitStack()
        def lsb(name, shape, dt=F32):
            return s4.enter_context(nc.sbuf_tensor(name, list(shape), dt))

        FCN = 24
        up_h = nc.dram_tensor("up_h", [DFF1, TOK], BF16, kind="Internal")
        gate_h = nc.dram_tensor("gate_h", [DFF1, TOK], BF16, kind="Internal")
        p2_h = nc.dram_tensor("p2_h", [DFF1, TOK], BF16, kind="Internal")
        p3_h = nc.dram_tensor("p3_h", [DFF1, TOK], BF16, kind="Internal")
        ccio = [(nc.dram_tensor(f"cci{j}", [1, 2], F32, kind="Internal"),
                 nc.dram_tensor(f"cco{j}", [1, 2], F32, kind="Internal"))
                for j in range(4)]
        RG = [list(range(cc_world))]

        arena = lsb("wqarena", [128, 2 * 8 * DFF1], BF16)
        wnat = lsb("wnat", [128, 2 * DFF1], BF16)
        bmax = lsb("bmax", [128, 32])
        sblk = lsb("sblk", [128, 32])
        rblk = lsb("rblk", [128, 32])
        wqn = lsb("wqn", [128, D], BF16)
        scr1P = [lsb(f"s4scr1_{j}", [128, 1024]) for j in range(2)]
        scr2P = [lsb(f"s4scr2_{j}", [128, 1024]) for j in range(2)]
        scrbP = [lsb(f"s4scrb_{j}", [128, 1024], BF16) for j in range(2)]
        scrb2P = [lsb(f"s4scrb2_{j}", [128, 1024], BF16) for j in range(2)]
        i32P = [lsb(f"s4i32_{j}", [128, 1024], I32) for j in range(2)]
        Otc = lsb("Otc", [128, FCN * 512], BF16)
        xtile = lsb("s4xt", [128, 2 * D])
        XTt = lsb("s4XT", [128, 8 * 512], BF16)
        mxc = lsb("mxc", [128, 4])
        amxs = lsb("amxs", [128, 384])
        scal4 = lsb("scal4", [128, 8])
        srow = lsb("srow", [1, 24])
        ccs = lsb("ccs", [1, 4])

        def bcast_scalar(src_ap, dst_col, rkeys, wkey, bank=0):
            p.op("pe", lambda E, src_ap=src_ap, bank=bank: E.matmul(
                psf[bank][:, 0:1], ones_row[0:1, :], src_ap, start=True, stop=True),
                reads=list(rkeys) + ["ones_row"], writes=[PSF[bank]])
            p.op("dve", lambda E, dst_col=dst_col, bank=bank: E.tensor_copy(
                dst_col, psf[bank][:, 0:1]), reads=[PSF[bank]], writes=[wkey])

        def part_amax(col_ap, out_row, rkeys, wkey, bank=1):
            p.op("pe", lambda E, col_ap=col_ap, bank=bank: E.transpose(
                psf[bank][0:1, 0:128], col_ap, identf[:]),
                reads=list(rkeys) + ["identf"], writes=[PSF[bank]])
            p.op("dve", lambda E, out_row=out_row, bank=bank: E.tensor_reduce(
                out_row, psf[bank][0:1, 0:128], axis=AX, op=ALU.max),
                reads=[PSF[bank]], writes=[wkey])

        def allreduce_max(j, in_row, n, rkeys, wkey):
            ci, co = ccio[j]
            p.dma("pool", lambda E, in_row=in_row, ci=ci, n=n: E.dma_start(
                ci.ap()[0:1, 0:n], in_row), reads=list(rkeys), writes=[f"cci{j}"])
            p.op("pool", lambda E, ci=ci, co=co, n=n: E.collective_compute(
                "AllReduce", ALU.max, RG, [ci.ap()[0:1, 0:n]], [co.ap()[0:1, 0:n]]),
                reads=[f"cci{j}"], writes=[f"cco{j}"])
            p.dma("pool", lambda E, co=co, n=n, j=j: E.dma_start(
                ccs[0:1, 0:n], co.ap()[0:1, 0:n]), reads=[f"cco{j}"], writes=[wkey])

        def round_ip(src_ap, dst_ap, n, rkeys, wkey, pj=0):
            iv = i32P[pj][:, 0:n]
            p.op("dve", lambda E: E.tensor_copy(iv, src_ap),
                 reads=list(rkeys), writes=[f"i32_{pj}"])
            p.op("dve", lambda E: E.tensor_copy(dst_ap, iv),
                 reads=[f"i32_{pj}"], writes=[wkey])

        def dequant(name_nat, nrows, ncols, dest3d, wkey):
            b3 = lambda ap: ap.rearrange("p (b c) -> p b c", c=32)
            for fc in range(nrows // 128):
                for sub in range(ncols // 1024):
                    wb = (fc + sub) % 2
                    nat = wnat[:, wb * D:(wb + 1) * D]
                    nk = f"s4nat{wb}"
                    p.dma("pool", lambda E, fc=fc, sub=sub, nat=nat,
                          name_nat=name_nat: E.dma_start(
                        nat, ext[name_nat].ap()[fc * 128:(fc + 1) * 128,
                                                sub * 1024:(sub + 1) * 1024]),
                        writes=[nk])
                    p.op("dve", lambda E, nat=nat: E.tensor_reduce(
                        bmax[:].rearrange("p (b o) -> p b o", o=1),
                        b3(nat), axis=AX, op=ALU.max, apply_absolute_value=True),
                        reads=[nk], writes=["s4bmax"])
                    p.op("dve", lambda E: E.tensor_scalar(
                        sblk[:], bmax[:], 1.0 / 7.0, 1.0e-12, ALU.mult, ALU.max),
                        reads=["s4bmax"], writes=["s4sblk"])
                    p.op("dve", lambda E: E.reciprocal(rblk[:], sblk[:]),
                         reads=["s4sblk"], writes=["s4rblk"])
                    rbc = rblk[:].rearrange("p (b o) -> p b o", o=1).broadcast_to([128, 32, 32])
                    sbc = sblk[:].rearrange("p (b o) -> p b o", o=1).broadcast_to([128, 32, 32])
                    iv, sB = i32P[wb], scr2P[wb]
                    p.op("dve", lambda E, nat=nat, rbc=rbc, iv=iv: E.tensor_mul(
                        iv[:].rearrange("p (b c) -> p b c", c=32), b3(nat), rbc),
                        reads=[nk, "s4rblk"], writes=[f"i32_{wb}"])
                    p.op("dve", lambda E, sB=sB, iv=iv: E.tensor_copy(
                        sB[:], iv[:]), reads=[f"i32_{wb}"], writes=[f"K_scr2{wb}"])
                    p.op("dve", lambda E, sbc=sbc, sB=sB: E.tensor_mul(
                        wqn[:].rearrange("p (b c) -> p b c", c=32),
                        sB[:].rearrange("p (b c) -> p b c", c=32), sbc),
                        reads=[f"K_scr2{wb}", "s4sblk"], writes=["wqn"])
                    bank = (fc + sub) % 2
                    fns = []
                    for dc in range(8):
                        fns.append(lambda E, dc=dc, bank=bank: E.transpose(
                            psb[bank][:, dc * 128:(dc + 1) * 128],
                            wqn[:, dc * 128:(dc + 1) * 128], identb[:]))
                    p.ops("pe", fns, reads=["wqn", "identb"], writes=[PSB[bank]])
                    p.op("dve" if (fc + sub) % 2 == 0 else "act",
                         lambda E, fc=fc, sub=sub, bank=bank: (
                             (E.tensor_copy if (fc + sub) % 2 == 0 else E.copy)(
                                 dest3d(fc, sub),
                                 psb[bank][:].rearrange("p (c t) -> p c t", c=8))),
                         reads=[PSB[bank]], writes=[wkey])

        def dest_upgate(which):
            def f(fc, sub):
                return (arena[:].rearrange("p (w c f) -> p w c f", w=2, c=8)
                        [:, which, :, fc * 128:(fc + 1) * 128])
            return f

        def dest_down(fc, sub):
            return (arena[:].rearrange("p (c f) -> p c f", f=1024)
                    [:, sub * 8:(sub + 1) * 8, fc * 128:(fc + 1) * 128])

        dequant("w_up_n", DFF1, D, dest_upgate(0), "arena0")
        dequant("w_gate_n", DFF1, D, dest_upgate(1), "arena1")

        amxP = [cols[:, 14:15], cols[:, 15:16]]
        for s in range(SEQ_PER_CORE):
            for i in range(NT):
                t0 = s * S
                tb = i % 2
                xv = xtile[:, tb * D:(tb + 1) * D]
                xk = f"s4x{tb}"
                idx = s * NT + i
                p.dma("sp" if i % 2 == 0 else "act",
                      lambda E, xv=xv, t0=t0, i=i: E.dma_start(
                    xv, src.ap()[t0 + i * 128:t0 + (i + 1) * 128, :]), writes=[xk])
                p.op("dve", lambda E, xv=xv, idx=idx: E.tensor_reduce(
                    amxs[:, idx:idx + 1], xv, axis=AX, op=ALU.max,
                    apply_absolute_value=True),
                    reads=[xk], writes=[f"am1_{idx}"])
        p.op("dve", lambda E: E.tensor_reduce(
            mxc[:, 0:1], amxs[:, 0:32], axis=AX, op=ALU.max),
            reads=[f"am1_{j}" for j in range(32)], writes=["mxc", "am1done"])
        part_amax(mxc[:, 0:1], srow[0:1, 0:1], ["mxc"], "s4sx")
        allreduce_max(0, srow[0:1, 0:1], 1, ["s4sx"], "s4cc0")
        p.op("dve", lambda E: E.tensor_scalar(
            srow[0:1, 1:2], ccs[0:1, 0:1], 1.0 / 32767.0, 1.0e-12,
            ALU.mult, ALU.max), reads=["s4cc0"], writes=["s4sxr"])
        p.op("dve", lambda E: E.reciprocal(srow[0:1, 2:3], srow[0:1, 1:2]),
             reads=["s4sxr"], writes=["s4rxr"])
        bcast_scalar(srow[0:1, 1:2], scal4[:, 0:1], ["s4sxr"], "s4sxb")
        bcast_scalar(srow[0:1, 2:3], scal4[:, 1:2], ["s4rxr"], "s4rxb")

        for s in range(SEQ_PER_CORE):
            for tc in range(4):
                t0 = s * S + tc * 512
                gt0 = t0
                for it in range(4):
                    pj = it % 2
                    xv = xtile[:, pj * D:(pj + 1) * D]
                    xk = f"s4x{pj}"
                    p.dma("act", lambda E, xv=xv, t0=t0, it=it: E.dma_start(
                        xv, src.ap()[t0 + it * 128:t0 + (it + 1) * 128, :]),
                        writes=[xk])
                    p.op("dve", lambda E, xv=xv, pj=pj: E.tensor_scalar_mul(
                        i32P[pj][:], xv, scal4[:, 1:2]),
                        reads=[xk], sreads=["s4rxb"], writes=[f"i32_{pj}"])
                    p.op("dve", lambda E, pj=pj: E.tensor_copy(
                        scrbP[pj][:], i32P[pj][:]),
                        reads=[f"i32_{pj}"], writes=[f"K_scrb{pj}"])
                    bank = pj
                    fns = []
                    for dc in range(8):
                        fns.append(lambda E, dc=dc, bank=bank, pj=pj: E.transpose(
                            psb[bank][:, dc * 128:(dc + 1) * 128],
                            scrbP[pj][:, dc * 128:(dc + 1) * 128], identb[:]))
                    p.ops("pe", fns, reads=[f"K_scrb{pj}", "identb"], writes=[PSB[bank]])
                    p.op("dve", lambda E, it=it, bank=bank: E.tensor_copy(
                        XTt[:].rearrange("p (c t) -> p c t", c=8)
                        [:, :, it * 128:(it + 1) * 128],
                        psb[bank][:].rearrange("p (c t) -> p c t", c=8)),
                        reads=[PSB[bank]], writes=[f"s4XT{it}"])
                xtk = [f"s4XT{it}" for it in range(4)]
                for fc in range(FCN):
                    for which, wk, mxi, hb in ((0, "arena0", 1, up_h),
                                               (1, "arena1", 2, gate_h)):
                        bank = which * 2 + fc % 2
                        pj = fc % 2
                        fns = []
                        for dc in range(8):
                            fns.append(lambda E, fc=fc, dc=dc, which=which,
                                       bank=bank: E.matmul(
                                psf[bank][:],
                                arena[:, which * 8 * DFF1 + dc * DFF1 +
                                      fc * 128:which * 8 * DFF1 + dc * DFF1 +
                                      (fc + 1) * 128],
                                XTt[:, dc * 512:(dc + 1) * 512],
                                start=(dc == 0), stop=(dc == 7)))
                        p.ops("pe", fns, reads=[wk] + xtk, writes=[PSF[bank]])
                        ek = f"K_scrb2{pj}" if which else f"K_scrb{pj}"
                        ebuf = (scrb2P if which else scrbP)[pj]
                        p.op("act", lambda E, bank=bank, ebuf=ebuf: E.activation(
                            ebuf[:, 0:512], psf[bank][:], ACT.Copy,
                            scale=scal4[:, 0:1]),
                            reads=[PSF[bank], "s4sxb"], writes=[ek])
                        idx = which * 192 + (s * 4 + tc) * 24 + fc
                        p.op("dve", lambda E, ebuf=ebuf, idx=idx: E.tensor_reduce(
                            amxs[:, idx:idx + 1], ebuf[:, 0:512], axis=AX,
                            op=ALU.max, apply_absolute_value=True),
                            reads=[ek, "am1done"], writes=[f"am2_{idx}"])
                        p.dma("sp", lambda E, fc=fc, gt0=gt0, hb=hb,
                              ebuf=ebuf: E.dma_start(
                            hb.ap()[fc * 128:(fc + 1) * 128, gt0:gt0 + 512],
                            ebuf[:, 0:512]),
                            reads=[ek], writes=[f"s4sp{which}_{fc}_{gt0}"])
        p.op("dve", lambda E: E.tensor_reduce(
            mxc[:, 1:2], amxs[:, 0:192], axis=AX, op=ALU.max),
            reads=[f"am2_{j}" for j in range(192)], writes=["mxc", "am2udone"])
        p.op("dve", lambda E: E.tensor_reduce(
            mxc[:, 2:3], amxs[:, 192:384], axis=AX, op=ALU.max),
            reads=[f"am2_{j}" for j in range(192, 384)],
            writes=["mxc", "am2gdone"])
        part_amax(mxc[:, 1:2], srow[0:1, 3:4], ["mxc"], "sr3")
        part_amax(mxc[:, 2:3], srow[0:1, 4:5], ["mxc"], "sr4")
        p.op("dve", lambda E: E.tensor_copy(srow[0:1, 5:7], srow[0:1, 3:5]),
             reads=["sr3", "sr4"], writes=["sr5"])
        allreduce_max(1, srow[0:1, 5:7], 2, ["sr5"], "s4cc1")
        p.op("dve", lambda E: E.tensor_scalar(
            srow[0:1, 8:10], ccs[0:1, 0:2], 1.0 / 32767.0, 1.0e-12,
            ALU.mult, ALU.max), reads=["s4cc1"], writes=["sr8", "sr9"])
        p.op("dve", lambda E: E.reciprocal(srow[0:1, 10:12], srow[0:1, 8:10]),
             reads=["sr8", "sr9"], writes=["sr10", "sr11"])
        bcast_scalar(srow[0:1, 10:11], scal4[:, 2:3], ["sr10"], "s4rub")
        bcast_scalar(srow[0:1, 11:12], scal4[:, 3:4], ["sr11"], "s4rgb")
        bcast_scalar(srow[0:1, 9:10], scal4[:, 5:6], ["sr9"], "s4sgb")

        NSL_F, NSL_T = FCN, 4
        slabs3 = [(fc, ts) for fc in range(NSL_F) for ts in range(NSL_T)]

        def p3_ctx(idx):
            fc, ts = slabs3[idx]
            pj = idx % 2
            return (fc, ts * 1024, pj, scr2P[pj], scrbP[pj], i32P[pj],
                    f"K_scr2{pj}", f"K_scrb{pj}", f"i32_{pj}")

        def p3_partA(idx):
            fc, co, pj, sB, bA, iv, kB, kC, kI = p3_ctx(idx)
            p.dma("sp", lambda E, fc=fc, co=co, bA=bA: E.dma_start(
                bA[:], gate_h.ap()[fc * 128:(fc + 1) * 128, co:co + 1024]),
                writes=[kC])
            p.op("dve", lambda E, iv=iv, bA=bA: E.tensor_scalar_mul(
                iv[:], bA[:], scal4[:, 3:4]),
                reads=[kC], sreads=["s4rgb"], writes=[kI])
            p.op("dve", lambda E, sB=sB, iv=iv: E.tensor_copy(sB[:], iv[:]),
                 reads=[kI], writes=[kB])
            p.op("act", lambda E, bA=bA, sB=sB: E.activation(
                bA[:], sB[:], ACT.Sigmoid, scale=scal4[:, 5:6]),
                reads=[kB, "s4sgb"], writes=[kC])

        def p3_partB(idx):
            fc, co, pj, sB, bA, iv, kB, kC, kI = p3_ctx(idx)
            p.op("dve", lambda E, iv=iv, bA=bA: E.tensor_scalar_mul(
                iv[:], bA[:], 65536.0), reads=[kC], writes=[kI])
            p.op("dve", lambda E, bA=bA, sB=sB, iv=iv: E.tensor_mul(
                bA[:], sB[:], iv[:]), reads=[kB, kI], writes=[kC])
            p.op("dve", lambda E, bA=bA, idx=idx: E.tensor_reduce(
                amxs[:, idx:idx + 1], bA[:], axis=AX, op=ALU.max,
                apply_absolute_value=True),
                reads=[kC, "am2udone"], writes=[f"am3_{idx}"])
            p.dma("sp", lambda E, fc=fc, co=co, bA=bA: E.dma_start(
                p2_h.ap()[fc * 128:(fc + 1) * 128, co:co + 1024], bA[:]),
                reads=[kC], writes=[f"s4p2s_{fc}_{co}"])

        p3_partA(0)
        p3_partA(1)
        for i in range(len(slabs3)):
            p3_partB(i)
            if i + 2 < len(slabs3):
                p3_partA(i + 2)
        p.op("dve", lambda E: E.tensor_reduce(
            mxc[:, 3:4], amxs[:, 0:96], axis=AX, op=ALU.max),
            reads=[f"am3_{j}" for j in range(96)], writes=["mxp2", "am3done"])
        part_amax(mxc[:, 3:4], srow[0:1, 12:13], ["mxp2"], "sr12")
        allreduce_max(2, srow[0:1, 12:13], 1, ["sr12"], "s4cc2")
        p.op("dve", lambda E: E.tensor_mul(
            srow[0:1, 13:14], ccs[0:1, 0:1], srow[0:1, 9:10]),
            reads=["s4cc2", "sr9"], writes=["sr13"])
        p.op("dve", lambda E: E.tensor_scalar(
            srow[0:1, 13:14], srow[0:1, 13:14],
            1.0 / 65536.0 / 32767.0, 1.0e-12, ALU.mult, ALU.max),
            sreads=["sr13"], writes=["sr13"])
        p.op("dve", lambda E: E.reciprocal(srow[0:1, 14:15], srow[0:1, 13:14]),
             reads=["sr13"], writes=["sr14"])
        p.op("dve", lambda E: E.tensor_mul(
            srow[0:1, 15:16], srow[0:1, 9:10], srow[0:1, 14:15]),
            reads=["sr9", "sr14"], writes=["sr15"])
        p.op("dve", lambda E: E.tensor_scalar(
            srow[0:1, 15:16], srow[0:1, 15:16], 1.0 / 65536.0, None, ALU.mult),
            sreads=["sr15"], writes=["sr15"])
        bcast_scalar(srow[0:1, 15:16], scal4[:, 4:5], ["sr15"], "s4c2b")

        def p4_ctx(idx):
            fc, ts = slabs3[idx]
            pj = idx % 2
            return (fc, ts * 1024, pj, scr2P[pj], scrbP[pj], scrb2P[pj],
                    i32P[pj], f"K_scr2{pj}", f"K_scrb{pj}", f"K_scrb2{pj}",
                    f"i32_{pj}")

        def p4_partA(idx):
            fc, co, pj, sB, bA, bB, iv, kB, kC, kD, kI = p4_ctx(idx)
            p.dma("sp", lambda E, fc=fc, co=co, bA=bA: E.dma_start(
                bA[:], p2_h.ap()[fc * 128:(fc + 1) * 128, co:co + 1024]),
                writes=[kC])
            p.dma("act", lambda E, fc=fc, co=co, bB=bB: E.dma_start(
                bB[:], up_h.ap()[fc * 128:(fc + 1) * 128, co:co + 1024]),
                writes=[kD])

        def p4_partB(idx):
            fc, co, pj, sB, bA, bB, iv, kB, kC, kD, kI = p4_ctx(idx)
            p.op("dve", lambda E, iv=iv, bA=bA: E.tensor_scalar_mul(
                iv[:], bA[:], scal4[:, 4:5]),
                reads=[kC], sreads=["s4c2b"], writes=[kI])
            p.op("dve", lambda E, sB=sB, iv=iv: E.tensor_copy(sB[:], iv[:]),
                 reads=[kI], writes=[kB])
            p.op("dve", lambda E, iv=iv, bB=bB: E.tensor_scalar_mul(
                iv[:], bB[:], scal4[:, 2:3]),
                reads=[kD], sreads=["s4rub"], writes=[kI])
            p.op("dve", lambda E, bB=bB, sB=sB, iv=iv: E.tensor_mul(
                bB[:], sB[:], iv[:]), reads=[kB, kI], writes=[kD])
            p.op("dve", lambda E, bB=bB, idx=idx: E.tensor_reduce(
                amxs[:, idx:idx + 1], bB[:], axis=AX, op=ALU.max,
                apply_absolute_value=True),
                reads=[kD, "am3done"], writes=[f"am4_{idx}"])
            p.dma("sp", lambda E, fc=fc, co=co, bB=bB: E.dma_start(
                p3_h.ap()[fc * 128:(fc + 1) * 128, co:co + 1024], bB[:]),
                reads=[kD], writes=[f"s4p3s_{fc}_{co}"])

        p4_partA(0)
        p4_partA(1)
        for i in range(len(slabs3)):
            p4_partB(i)
            if i + 2 < len(slabs3):
                p4_partA(i + 2)
        p.op("dve", lambda E: E.tensor_reduce(
            mxc[:, 3:4], amxs[:, 0:96], axis=AX, op=ALU.max),
            reads=[f"am4_{j}" for j in range(96)], writes=["mxp2", "am4done"])
        part_amax(mxc[:, 3:4], srow[0:1, 16:17], ["mxp2"], "sr16")
        allreduce_max(3, srow[0:1, 16:17], 1, ["sr16"], "s4cc3")
        p.op("dve", lambda E: E.tensor_mul(
            srow[0:1, 17:18], ccs[0:1, 0:1], srow[0:1, 13:14]),
            reads=["s4cc3", "sr13"], writes=["sr17"])
        p.op("dve", lambda E: E.tensor_mul(
            srow[0:1, 17:18], srow[0:1, 17:18], srow[0:1, 8:9]),
            reads=["sr8"], sreads=["sr17"], writes=["sr17"])
        p.op("dve", lambda E: E.tensor_scalar(
            srow[0:1, 17:18], srow[0:1, 17:18], 1.0 / 32767.0, 1.0e-12,
            ALU.mult, ALU.max), sreads=["sr17"], writes=["sr17"])
        p.op("dve", lambda E: E.reciprocal(srow[0:1, 18:19], srow[0:1, 17:18]),
             reads=["sr17"], writes=["sr18"])
        p.op("dve", lambda E: E.tensor_mul(
            srow[0:1, 19:20], srow[0:1, 13:14], srow[0:1, 8:9]),
            reads=["sr13", "sr8"], writes=["sr19"])
        p.op("dve", lambda E: E.tensor_mul(
            srow[0:1, 19:20], srow[0:1, 19:20], srow[0:1, 18:19]),
            reads=["sr18"], sreads=["sr19"], writes=["sr19"])
        bcast_scalar(srow[0:1, 19:20], scal4[:, 6:7], ["sr19"], "s4c3b")
        bcast_scalar(srow[0:1, 17:18], scal4[:, 7:8], ["sr17"], "s4sob")

        dequant("w_down_n", D, DFF1, dest_down, "arena0")

        for s in range(SEQ_PER_CORE):
            for tc in range(4):
                t0 = s * S + tc * 512
                gt0 = t0
                for fc in range(FCN):
                    pj = fc % 2
                    bA, iv = scrbP[pj], i32P[pj]
                    kC, kI = f"K_scrb{pj}", f"i32_{pj}"
                    p.dma("sp", lambda E, fc=fc, gt0=gt0, bA=bA: E.dma_start(
                        bA[:, 0:512],
                        p3_h.ap()[fc * 128:(fc + 1) * 128, gt0:gt0 + 512]),
                        writes=[kC])
                    p.op("dve", lambda E, iv=iv, bA=bA: E.tensor_scalar_mul(
                        iv[:, 0:512], bA[:, 0:512], scal4[:, 6:7]),
                        reads=[kC], sreads=["s4c3b"], writes=[kI])
                    p.op("dve", lambda E, fc=fc, iv=iv: E.tensor_copy(
                        Otc[:, fc * 512:(fc + 1) * 512], iv[:, 0:512]),
                        reads=[kI], writes=[f"s4Otc{fc}"])
                okeys = [f"s4Otc{fc}" for fc in range(FCN)]
                for it in range(4):
                    tb = it % 2
                    xv = xtile[:, tb * D:(tb + 1) * D]
                    xk = f"s4x{tb}"
                    p.dma("sp", lambda E, xv=xv, t0=t0, it=it: E.dma_start(
                        xv, src.ap()[t0 + it * 128:t0 + (it + 1) * 128, :]),
                        writes=[xk])
                    for hh in range(2):
                        bank = hh
                        fns = []
                        for fc in range(FCN):
                            fns.append(lambda E, fc=fc, it=it, hh=hh,
                                       bank=bank: E.matmul(
                                psf[bank][:],
                                Otc[:, fc * 512 + it * 128:fc * 512 + (it + 1) * 128],
                                arena[:, fc * 1024 + hh * 512:fc * 1024 + (hh + 1) * 512],
                                start=(fc == 0), stop=(fc == FCN - 1)))
                        p.ops("pe", fns, reads=["arena0"] + okeys, writes=[PSF[bank]])
                    for hh in range(2):
                        p.op("act", lambda E, hh=hh: E.activation(
                            scr1P[hh][:, 0:512], psf[hh][:], ACT.Copy,
                            scale=scal4[:, 7:8]),
                            reads=[PSF[hh], "s4sob"], writes=[f"K_scr1{hh}"])
                        p.op("dve", lambda E, xv=xv, hh=hh: E.tensor_add(
                            xv[:, hh * 512:(hh + 1) * 512],
                            xv[:, hh * 512:(hh + 1) * 512], scr1P[hh][:, 0:512]),
                            reads=[f"K_scr1{hh}", xk], writes=[xk])
                    emit_ln(xv, xk, dst, t0, it, f"s4dst{t0}_{it}", slot=it % 2)
        s4.close()

        s4.close()

    stage0(ext["x"], dsts[0])
    if n_stages >= 2:
        p.barrier()
        stage1(dsts[0], dsts[1])
    if n_stages >= 3:
        p.barrier()
        stage2(dsts[1], dsts[2])
    if n_stages >= 4:
        p.barrier()
        stage3(dsts[2], dsts[3])
    if n_stages >= 5:
        p.barrier()
        stage4(dsts[3], dsts[4])

    for e in DMA_ENGS:
        for j in range(DMA_K[e]):
            n = p.dma_count[e] // DMA_K[e] + (1 if p.dma_count[e] % DMA_K[e] > j else 0)
            if n:
                p.raw(e, lambda E, s=p.dma_sems[e][j], v=16 * n: E.wait_ge(s, v))

    block = st.enter_context(nc.Block())
    p.emit(block)
    st.close()
    return nc


_PROG_CACHE = {}


def kernel(**inputs):
    n_stages = int(inputs.pop("_n_stages", 5))
    x = np.ascontiguousarray(np.asarray(inputs["L0_attn_x"], np.float32)).reshape(B, S, D)
    consts = host_layouts(inputs)
    nc = build_core_program(n_stages)
    in_maps = []
    for c in range(N_CORES):
        m = dict(consts)
        m["x"] = np.ascontiguousarray(
            x[c * SEQ_PER_CORE:(c + 1) * SEQ_PER_CORE].reshape(TOK, D))
        in_maps.append(m)
    res = run_bass_kernel_spmd(nc, in_maps, core_ids=list(range(N_CORES)))
    outs = [np.asarray(r["out"]).reshape(SEQ_PER_CORE, S, D) for r in res.results]
    return np.concatenate(outs, axis=0)


# version 06
# speedup vs baseline: 1.0256x; 1.0256x over previous
import contextlib
import os as _os
import numpy as np
K_DBG = _os.environ.get("K_DBG", "") == "1"

import concourse.bass as bass
import concourse.mybir as mybir
from concourse.bass_utils import run_bass_kernel_spmd

F32 = mybir.dt.float32
BF16 = mybir.dt.bfloat16
I32 = mybir.dt.int32
ALU = mybir.AluOpType
ACT = mybir.ActivationFunctionType
AX = mybir.AxisListType.X

N_CORES = 8
B, S, D = 16, 2048, 1024
SEQ_PER_CORE = B // N_CORES
TOK = SEQ_PER_CORE * S
NT = S // 128
NSTATE = 256
DFF0 = 2736
DFF1 = 3072
NH, HD = 16, 64
NB = 128
WIN = 256
SCALE_ATT = 0.125
EPS_RMS = 1e-6
EPS_LN = 1e-5
NEG = -3.0e38

ENGS = ("pe", "act", "dve", "pool", "sp")
DMA_ENGS = ("sp", "pool", "act")


DMA_K = {"sp": 6, "pool": 6, "act": 2}


class Prog:
    def __init__(self, nc):
        self.nc = nc
        self.plan = {e: [] for e in ENGS}
        self.count = {e: 0 for e in ENGS}
        self.dma_count = {e: 0 for e in DMA_ENGS}
        self.writes = {}
        self.readers = {}
        self.last_wait = {e: {} for e in ENGS}
        self.sems = {}
        self.dma_sems = {}

    def setup_sems(self, stack):
        for e in ENGS:
            self.sems[e] = stack.enter_context(self.nc.semaphore(f"sem_{e}"))
        for e in DMA_ENGS:
            self.dma_sems[e] = [stack.enter_context(self.nc.semaphore(f"dsem_{e}{j}"))
                                for j in range(DMA_K[e])]

    def _wait(self, eng, oeng, kind, tick, force=False):
        if tick <= 0:
            return
        cur = self.last_wait[eng].get((oeng, kind), 0)
        if tick <= cur:
            return
        self.last_wait[eng][(oeng, kind)] = tick
        if kind == "c":
            sem, val = self.sems[oeng], tick
        else:
            sem, val = self.dma_sems[oeng][kind[1]], 16 * tick
        self.plan[eng].append(lambda E, sem=sem, val=val: E.wait_ge(sem, val))

    def dep(self, eng, reads=(), writes=(), sreads=()):
        for k in tuple(reads) + tuple(sreads):
            for w in self.writes.get(k, ()):
                self._wait(eng, w[0], w[1], w[2])
        for k in writes:
            for w in self.writes.get(k, ()):
                self._wait(eng, w[0], w[1], w[2])
            for (oeng, okind), t in self.readers.get(k, {}).items():
                self._wait(eng, oeng, okind, t)

    def commit(self, eng, kind, reads=(), writes=()):
        if kind == "c":
            self.count[eng] += 1
            tick = self.count[eng]
            sem = self.sems[eng]
        else:
            m = self.dma_count[eng]
            self.dma_count[eng] = m + 1
            slot = m % DMA_K[eng]
            tick = m // DMA_K[eng] + 1
            kind = ("d", slot)
            sem = self.dma_sems[eng][slot]
        for k in writes:
            if self.readers.get(k):
                self.writes[k] = [(eng, kind, tick)]
                self.readers[k] = {}
            else:
                self.writes.setdefault(k, []).append((eng, kind, tick))
        for k in reads:
            self.readers.setdefault(k, {})[(eng, kind)] = tick
        return sem, tick

    def op(self, eng, fn, reads=(), writes=(), sreads=()):
        self.dep(eng, reads, writes, sreads)
        sem, _ = self.commit(eng, "c", tuple(reads) + tuple(sreads), writes)
        self.plan[eng].append(lambda E, fn=fn, s=sem: fn(E).then_inc(s, 1))

    def ops(self, eng, fns, reads=(), writes=(), sreads=()):
        self.dep(eng, reads, writes, sreads)
        sem, _ = self.commit(eng, "c", tuple(reads) + tuple(sreads), writes)
        for fn in fns[:-1]:
            self.plan[eng].append(lambda E, fn=fn: fn(E))
        self.plan[eng].append(lambda E, fn=fns[-1], s=sem: fn(E).then_inc(s, 1))

    def dma(self, eng, fn, reads=(), writes=()):
        self.dep(eng, reads, writes)
        m = self.dma_count[eng]
        slot, gen = m % DMA_K[eng], m // DMA_K[eng]
        if gen > 0:
            self._wait(eng, eng, ("d", slot), gen)
        sem, tick = self.commit(eng, "d", reads, writes)
        self.plan[eng].append(lambda E, fn=fn, s=sem: fn(E).then_inc(s, 16))

    def raw(self, eng, fn):
        self.plan[eng].append(fn)

    def barrier(self):
        for e in ENGS:
            for o in ENGS:
                if o != e:
                    self._wait(e, o, "c", self.count[o])
            for q in DMA_ENGS:
                m = self.dma_count[q]
                for j in range(DMA_K[q]):
                    n = m // DMA_K[q] + (1 if m % DMA_K[q] > j else 0)
                    self._wait(e, q, ("d", j), n)
        self.writes.clear()
        self.readers.clear()

    def emit(self, block):
        def body(E, eng):
            for fn in self.plan[eng]:
                fn(E)
        block.tensor(lambda E: body(E, "pe"))
        block.scalar(lambda E: body(E, "act"))
        block.vector(lambda E: body(E, "dve"))
        block.gpsimd(lambda E: body(E, "pool"))
        block.sync(lambda E: body(E, "sp"))



def host_layouts(inputs):
    f32 = lambda a: np.ascontiguousarray(a, np.float32)
    hd_perm = np.concatenate([np.arange(0, HD, 2), np.arange(1, HD, 2)])
    perm = np.concatenate([h * HD + hd_perm for h in range(NH)])

    c = {}
    c["BwT"] = f32(np.asarray(inputs["L0_attn_B_w"]).T)
    c["CwT"] = f32(np.asarray(inputs["L0_attn_C_w"]).T)
    c["log_lambda"] = f32(inputs["L0_attn_log_lambda"])
    c["ssm_w"] = f32(inputs["L0_attn_ssm_norm_w"])
    c["D_skip"] = f32(inputs["L0_attn_D_skip"])
    c["ffn_w"] = f32(inputs["L0_attn_ffn_norm_w"])
    c["w1T"] = f32(np.asarray(inputs["L0_attn_w1"]).T)
    c["w3T"] = f32(np.asarray(inputs["L0_attn_w3"]).T)
    c["w2T"] = f32(np.asarray(inputs["L0_attn_w2"]).T)

    G1R = np.asarray(inputs["L0_ffn_G1_R"]).reshape(32, 4, 32)
    G2R = np.asarray(inputs["L0_ffn_G2_R"]).reshape(4, 16, 16)
    G1I = np.asarray(inputs["L0_ffn_G1_I"]).reshape(32, 4, 32)
    G2I = np.asarray(inputs["L0_ffn_G2_I"]).reshape(4, 16, 16)

    def tt_place(G1, G2):
        lhs = np.zeros((4, 512, 512), np.float32)
        rhs = np.zeros((4, 512, 512), np.float32)
        bg_b = np.repeat(np.arange(32), 16)
        bg_g = np.tile(np.arange(16), 32)
        for cc in range(4):
            for d in range(32):
                lhs[cc, np.arange(512), d * 16 + bg_g] = G1[bg_b, cc, d]
            for b in range(32):
                rows = b * 16 + np.arange(16)
                for e in range(16):
                    rhs[cc, rows, b * 16 + e] = G2[cc, e, np.arange(16)]
        return lhs, rhs

    c["ttlR"], c["ttrR"] = [f32(a) for a in tt_place(G1R, G2R)]
    c["ttlI"], c["ttrI"] = [f32(a) for a in tt_place(G1I, G2I)]
    c["bias_ri"] = f32(np.concatenate([np.asarray(inputs["L0_ffn_bias_real"]),
                                       np.asarray(inputs["L0_ffn_bias_imag"])]))

    c["WqP"] = f32(np.asarray(inputs["L1_attn_Wq"])[:, perm])
    c["WkP"] = f32(np.asarray(inputs["L1_attn_Wk"])[:, perm])
    c["Wv"] = f32(inputs["L1_attn_Wv"])
    c["Wo"] = f32(inputs["L1_attn_Wo"])
    c["bo"] = f32(inputs["L1_attn_bo"])
    c["peP"] = f32(np.asarray(inputs["L1_attn_pe"])[:, perm])
    c["pe_raw"] = f32(inputs["L1_attn_pe"])
    c["Wg"] = f32(inputs["L1_attn_Wg"])
    c["bg"] = f32(inputs["L1_attn_bg"])
    wk = np.asarray(inputs["L1_attn_wk_comp"]); wv = np.asarray(inputs["L1_attn_wv_comp"])
    Wck = np.zeros((128, 8), np.float32); Wcv = np.zeros((128, 8), np.float32)
    for j in range(8):
        Wck[16 * j + np.arange(16), j] = wk
        Wcv[16 * j + np.arange(16), j] = wv
    c["Wck"], c["Wcv"] = Wck, Wcv
    c["wk_comp"], c["wv_comp"] = f32(wk), f32(wv)

    half = HD // 2
    inv = (1.0 / 10000.0 ** (np.arange(half, dtype=np.float32) * 2.0 / HD)).astype(np.float32)
    ang = np.arange(S, dtype=np.float32)[:, None] * inv[None, :]
    c["cos_t"] = f32(np.cos(ang))
    c["sin_t"] = f32(np.sin(ang))

    tt = np.arange(S)[:, None]
    nn = np.arange(NB)[None, :]
    mc = np.where((nn + 1) * 16 <= tt, 0.0, NEG).astype(np.float32)
    c["mask_comp"] = f32(mc.reshape(16, 128, NB))
    mw = np.zeros((3, 128, 384), np.float32)
    for var in range(3):
        for pp in range(128):
            for j in range(384):
                if var == 0:
                    r, kt = pp, j
                elif var == 1:
                    r, kt = 128 + pp, j
                else:
                    r, kt = pp, j
    j = np.arange(384)[None, :]
    pp = np.arange(128)[:, None]
    mw0 = np.where((j <= pp) & (j > pp - WIN), 0.0, NEG)
    mw1 = np.where((j <= pp + 128) & (j > pp + 128 - WIN), 0.0, NEG)
    mw2 = np.where((j > pp) & (j <= pp + WIN), 0.0, NEG)
    c["mask_win"] = f32(np.stack([mw0[:, :384], mw1[:, :384], mw2]))

    c["w_upT"] = f32(np.asarray(inputs["L1_ffn_w_up"]).T)
    c["w_gateT"] = f32(np.asarray(inputs["L1_ffn_w_gate"]).T)
    c["w_downT"] = f32(np.asarray(inputs["L1_ffn_w_down"]).T)
    c["w_up_n"] = f32(inputs["L1_ffn_w_up"])
    c["w_gate_n"] = f32(inputs["L1_ffn_w_gate"])
    c["w_down_n"] = f32(inputs["L1_ffn_w_down"])
    repu = np.zeros((4, 128), np.float32)
    repu[np.arange(128) // 32, np.arange(128)] = 1.0
    c["REPU"] = repu

    c["identf"] = np.eye(128, dtype=np.float32)
    c["ones_row"] = np.ones((1, 128), np.float32)
    c["iota_p"] = np.arange(128, dtype=np.float32).reshape(128, 1)
    return c


HOST_KEYS = None



def build_core_program(n_stages=5, cc_world=8):
    nc = bass.Bass("TRN2", target_bir_lowering=False)
    st = contextlib.ExitStack()

    ext = {}
    def ein(name, shape):
        ext[name] = nc.dram_tensor(name, list(shape), F32, kind="ExternalInput")

    ein("x", (TOK, D))
    ein("BwT", (D, NSTATE)); ein("CwT", (NSTATE, D))
    ein("log_lambda", (NSTATE,)); ein("ssm_w", (D,)); ein("D_skip", (D,))
    ein("ffn_w", (D,))
    ein("w1T", (D, DFF0)); ein("w3T", (D, DFF0)); ein("w2T", (DFF0, D))
    ein("ttlR", (4, 512, 512)); ein("ttrR", (4, 512, 512))
    ein("ttlI", (4, 512, 512)); ein("ttrI", (4, 512, 512))
    ein("bias_ri", (2 * 512,))
    ein("WqP", (D, D)); ein("WkP", (D, D)); ein("Wv", (D, D)); ein("Wo", (D, D))
    ein("bo", (D,)); ein("peP", (16, D)); ein("pe_raw", (16, D)); ein("Wg", (D, 3)); ein("bg", (3,))
    ein("Wck", (128, 8)); ein("Wcv", (128, 8))
    ein("wk_comp", (16,)); ein("wv_comp", (16,))
    ein("cos_t", (S, 32)); ein("sin_t", (S, 32))
    ein("mask_comp", (16, 128, NB)); ein("mask_win", (3, 128, 384))
    ein("w_upT", (D, DFF1)); ein("w_gateT", (D, DFF1)); ein("w_downT", (DFF1, D))
    ein("w_up_n", (DFF1, D)); ein("w_gate_n", (DFF1, D)); ein("w_down_n", (D, DFF1))
    ein("REPU", (4, 128))
    ein("identf", (128, 128)); ein("ones_row", (1, 128)); ein("iota_p", (128, 1))

    out = nc.dram_tensor("out", [TOK, D], F32, kind="ExternalOutput")
    dbg = {}
    if K_DBG:
        for nm, shp in (("d_vseq", [128, NT * 1024]), ("d_uT", [128, 8 * S]),
                        ("d_Bu", [128, 2 * S]), ("d_hs", [128, 2 * S]),
                        ("d_BwTw", [128, 8 * NSTATE]), ("d_CwT", [128, 2 * D]),
                        ("d_lam", [128, 2]), ("d_Dw", [128, D])):
            dbg[nm] = nc.dram_tensor(nm, shp, F32, kind="ExternalOutput")
    xr = [nc.dram_tensor(f"xr{i}", [TOK, D], F32, kind="Internal")
          for i in range(1, 5)]
    dsts = xr[:4] + [out]
    if n_stages <= 4:
        dsts[n_stages - 1] = out

    p = Prog(nc)
    p.setup_sems(st)

    def sb(name, shape, dt=F32):
        return st.enter_context(nc.sbuf_tensor(name, list(shape), dt))

    identf = sb("identf_sb", [128, 128])
    identb = sb("identb_sb", [128, 128], BF16)
    ones_row = sb("ones_row_sb", [1, 128])
    iota_p = sb("iota_p_sb", [128, 1])
    sq_scr = sb("sq_scr", [128, 1024])
    cols = sb("cols", [128, 16])
    cpool = sb("cpool", [128, 4])
    colsB = sb("colsB", [128, 8])
    i32_scr = sb("i32_scr", [128, 1024], I32)

    p.dma("sp", lambda E: E.dma_start(identf[:], ext["identf"].ap()), writes=["identf"])
    p.dma("pool", lambda E: E.dma_start(identb[:], ext["identf"].ap()), writes=["identb"])
    p.dma("sp", lambda E: E.dma_start(ones_row[:], ext["ones_row"].ap()), writes=["ones_row"])
    p.dma("sp", lambda E: E.dma_start(iota_p[:], ext["iota_p"].ap()), writes=["iota_p"])
    for j, val in enumerate((EPS_RMS, EPS_LN)):
        p.op("dve", lambda E, j=j, val=val: E.memset(cpool[:, j:j + 1], val),
             writes=["cpool"])
        nc.const_aps.register(F32, val, cpool[:, j:j + 1])

    psf = [st.enter_context(nc.psum_tensor(f"psf{i}", [128, 512], F32))
           for i in range(6)]
    psb = [st.enter_context(nc.psum_tensor(f"psb{i}", [128, 1024], BF16))
           for i in range(2)]
    PSF = [f"psf{i}" for i in range(6)]
    PSB = [f"psb{i}" for i in range(2)]

    def bcast_row(row_ap, out_sb, n, rkeys, wkey, banks=(0, 1), dt_out=None):
        o = 0
        bi = 0
        while o < n:
            w = min(512, n - o)
            bank = banks[bi % len(banks)]
            bi += 1
            p.op("pe", lambda E, o=o, w=w, bank=bank: E.matmul(
                psf[bank][:, 0:w], ones_row[0:1, :], row_ap[0:1, o:o + w],
                start=True, stop=True),
                reads=list(rkeys) + ["ones_row"], writes=[PSF[bank]])
            p.op("dve", lambda E, o=o, w=w, bank=bank: E.tensor_copy(
                out_sb[:, o:o + w], psf[bank][:, 0:w]),
                reads=[PSF[bank]], writes=[wkey])
            o += w

    lnpool = sb("lnpool", [128, 12])

    def emit_ln(xv, xk, dst, t0, it, dkey, slot=0):
        c0 = slot * 6
        mu = lnpool[:, c0:c0 + 1]
        ssv = lnpool[:, c0 + 1:c0 + 2]
        var = lnpool[:, c0 + 2:c0 + 3]
        rst = lnpool[:, c0 + 3:c0 + 4]
        nmr = lnpool[:, c0 + 4:c0 + 5]
        kmu, kss, kvar, krst, knmr = (f"ln{slot}mu", f"ln{slot}ss",
                                      f"ln{slot}var", f"ln{slot}rstd",
                                      f"ln{slot}nmr")
        p.op("dve", lambda E: E.reduce_sum(mu, xv, axis=AX),
             reads=[xk], writes=[kmu])
        p.op("act", lambda E: E.activation(
            sq_scr[:], xv, ACT.Square, accum_out=ssv),
            reads=[xk], writes=["sq_scr", kss])
        p.op("dve", lambda E: E.tensor_scalar_mul(mu, mu, 1.0 / D),
             reads=[kmu], writes=[kmu])
        p.op("dve", lambda E: E.tensor_mul(var, mu, mu),
             sreads=[kmu], writes=[kvar])
        p.op("dve", lambda E: E.scalar_tensor_tensor(
            var, ssv, 1.0 / D, var, ALU.mult, ALU.subtract),
            reads=[kss], sreads=[kvar], writes=[kvar])
        p.op("act", lambda E: E.activation(
            var, var, ACT.Sqrt, scale=1.0, bias=EPS_LN),
            reads=[kvar, "cpool"], writes=[kvar])
        p.op("dve", lambda E: E.reciprocal(rst, var),
             reads=[kvar], writes=[krst])
        p.op("dve", lambda E: E.tensor_mul(nmr, mu, rst),
             sreads=[kmu, krst], writes=[knmr])
        p.op("dve", lambda E: E.tensor_scalar_mul(nmr, nmr, -1.0),
             sreads=[knmr], writes=[knmr])
        p.op("dve", lambda E: E.tensor_scalar(
            xv, xv, rst, nmr, ALU.mult, ALU.add),
            reads=[xk], sreads=[krst, knmr], writes=[xk])
        p.dma("sp", lambda E: E.dma_start(
            dst.ap()[t0 + it * 128:t0 + (it + 1) * 128, :], xv),
            reads=[xk], writes=[dkey])

    def stage0(src, dst):
        s0 = contextlib.ExitStack()
        def lsb(name, shape, dt=F32):
            return s0.enter_context(nc.sbuf_tensor(name, list(shape), dt))

        BwTw = lsb("BwTw", [128, 8 * NSTATE], BF16)
        CwT = lsb("CwT_sb", [128, 2 * D], BF16)
        lam_col = lsb("lam_col", [128, 2])
        Dw_bc = lsb("Dw_bc", [128, D], BF16)
        wrow = lsb("wrow", [1, 2 * D])
        wcol = lsb("wcol", [128, 8])
        xseq = lsb("xseq", [128, NT * 1024])
        vseq = lsb("vseq", [128, NT * 1024], BF16)
        uT = lsb("uT", [128, 8 * S], BF16)
        Bu = lsb("Bu", [128, 2 * S])
        du_scr = lsb("du_scr", [128, 1024], BF16)
        hs = lsb("hs", [128, 2 * S], BF16)

        for dc in range(8):
            p.dma("pool", lambda E, dc=dc: E.dma_start(
                BwTw[:, dc * NSTATE:(dc + 1) * NSTATE],
                ext["BwT"].ap()[dc * 128:(dc + 1) * 128, :]), writes=[f"BwTw{dc}"])
        p.dma("pool", lambda E: E.dma_start(
            wcol[:], bass.AP(ext["ssm_w"], 0, [[1, 128], [128, 8]]),
            allow_slow_non_contiguous=True), writes=["wcol"])
        for dc in range(8):
            p.op("dve", lambda E, dc=dc: E.tensor_scalar_mul(
                BwTw[:, dc * NSTATE:(dc + 1) * NSTATE],
                BwTw[:, dc * NSTATE:(dc + 1) * NSTATE], wcol[:, dc:dc + 1]),
                reads=[f"BwTw{dc}", "wcol"], writes=[f"BwTw{dc}"])
        for c in range(2):
            p.dma("pool", lambda E, c=c: E.dma_start(
                CwT[:, c * D:(c + 1) * D],
                ext["CwT"].ap()[c * 128:(c + 1) * 128, :]), writes=["CwT"])
        p.dma("pool", lambda E: E.dma_start(
            lam_col[:], bass.AP(ext["log_lambda"], 0, [[1, 128], [128, 2]]),
            allow_slow_non_contiguous=True), writes=["lam_col"])
        p.op("act", lambda E: E.activation(lam_col[:], lam_col[:], ACT.Sigmoid),
             reads=["lam_col"], writes=["lam_col"])
        p.dma("sp", lambda E: E.dma_start(wrow[0:1, 0:D], ext["ssm_w"].ap()[None, :]),
              writes=["wrow"])
        p.dma("sp", lambda E: E.dma_start(wrow[0:1, D:2 * D], ext["D_skip"].ap()[None, :]),
              writes=["wrow2"])
        p.op("dve", lambda E: E.tensor_mul(wrow[0:1, D:2 * D], wrow[0:1, 0:D],
                                           wrow[0:1, D:2 * D]),
             reads=["wrow", "wrow2"], writes=["wrow2"])
        bcast_row(wrow[0:1, D:2 * D], Dw_bc, D, ["wrow2"], "Dw_bc", banks=(0, 1))

        rms = cols[:, 0:1]
        rstd = cols[:, 1:2]

        for s in range(SEQ_PER_CORE):
            t0 = s * S
            for i in range(NT):
                xv = xseq[:, i * 1024:(i + 1) * 1024]
                vv = vseq[:, i * 1024:(i + 1) * 1024]
                p.dma("sp", lambda E, i=i, xv=xv, t0=t0: E.dma_start(
                    xv, ext["x"].ap()[t0 + i * 128:t0 + (i + 1) * 128, :]),
                    writes=[f"xt{i}"])
                p.op("act", lambda E, xv=xv: E.activation(
                    sq_scr[:], xv, ACT.Square, accum_out=rms),
                    reads=[f"xt{i}"], writes=["sq_scr", "rms"])
                p.op("act", lambda E: E.activation(
                    rms, rms, ACT.Sqrt, scale=1.0 / D, bias=EPS_RMS),
                    reads=["rms", "cpool"], writes=["rms"])
                p.op("dve", lambda E: E.reciprocal(rstd, rms),
                     reads=["rms"], writes=["rstd"])
                p.op("dve", lambda E, xv=xv, vv=vv: E.tensor_scalar_mul(
                    vv, xv, rstd),
                    reads=[f"xt{i}"], sreads=["rstd"], writes=[f"v{i}"])
                bank = i % 2
                fns = []
                for dc in range(8):
                    fns.append(lambda E, vv=vv, dc=dc, bank=bank: E.transpose(
                        psb[bank][:, dc * 128:(dc + 1) * 128],
                        vv[:, dc * 128:(dc + 1) * 128], identb[:]))
                p.ops("pe", fns, reads=[f"v{i}", "identb"], writes=[PSB[bank]])
                p.op("dve" if i % 2 == 0 else "act",
                     lambda E, i=i, bank=bank: (E.tensor_copy if i % 2 == 0 else E.copy)(
                         uT[:].rearrange("p (c t) -> p c t", c=8)[:, :, i * 128:(i + 1) * 128],
                         psb[bank][:].rearrange("p (c t) -> p c t", c=8)),
                     reads=[PSB[bank]], writes=[f"uTt{i}"])
            for ncnk in range(2):
                for tc in range(4):
                    bank = 2 + (tc % 2)
                    fns = []
                    for dc in range(8):
                        fns.append(lambda E, ncnk=ncnk, tc=tc, dc=dc, bank=bank: E.matmul(
                            psf[bank][:],
                            BwTw[:, dc * NSTATE + ncnk * 128: dc * NSTATE + (ncnk + 1) * 128],
                            uT[:, dc * S + tc * 512:dc * S + (tc + 1) * 512],
                            start=(dc == 0), stop=(dc == 7)))
                    p.ops("pe", fns,
                          reads=[f"BwTw{dc}" for dc in range(8)] +
                                [f"uTt{i}" for i in range(NT)],
                          writes=[PSF[bank]])
                    p.op("act", lambda E, ncnk=ncnk, tc=tc, bank=bank: E.copy(
                        Bu[:, ncnk * S + tc * 512:ncnk * S + (tc + 1) * 512],
                        psf[bank][:]),
                        reads=[PSF[bank]], writes=[f"Bu{ncnk}"])
            for ncnk in range(2):
                p.op("dve", lambda E, ncnk=ncnk: E.tensor_tensor_scan(
                    hs[:, ncnk * S:(ncnk + 1) * S],
                    lam_col[:, ncnk:ncnk + 1].broadcast_to([128, S]),
                    Bu[:, ncnk * S:(ncnk + 1) * S],
                    0.0, ALU.mult, ALU.add),
                    reads=[f"Bu{ncnk}", "lam_col"], writes=[f"hs{ncnk}"])
            if K_DBG and s == 0:
                dbg_keys = {"d_vseq": [f"v{i}" for i in range(NT)],
                            "d_uT": [f"uTt{i}" for i in range(NT)],
                            "d_Bu": ["Bu0", "Bu1"], "d_hs": ["hs0", "hs1"],
                            "d_BwTw": [f"BwTw{dc}" for dc in range(8)],
                            "d_CwT": ["CwT"], "d_lam": ["lam_col"],
                            "d_Dw": ["Dw_bc"]}
                for nm, t_ in (("d_vseq", vseq), ("d_uT", uT), ("d_Bu", Bu),
                               ("d_hs", hs), ("d_BwTw", BwTw), ("d_CwT", CwT),
                               ("d_lam", lam_col), ("d_Dw", Dw_bc)):
                    p.dma("pool", lambda E, nm=nm, t_=t_: E.dma_start(
                        dbg[nm].ap(), t_[:]),
                        reads=dbg_keys[nm], writes=[f"dbg{nm}"])
            for i in range(NT):
                xv = xseq[:, i * 1024:(i + 1) * 1024]
                vv = vseq[:, i * 1024:(i + 1) * 1024]
                b0, b1 = 4, 5
                for dh, bank in ((0, b0), (1, b1)):
                    fns = []
                    for ncnk in range(2):
                        fns.append(lambda E, i=i, dh=dh, ncnk=ncnk, bank=bank: E.matmul(
                            psf[bank][:],
                            hs[:, ncnk * S + i * 128:ncnk * S + (i + 1) * 128],
                            CwT[:, ncnk * D + dh * 512:ncnk * D + (dh + 1) * 512],
                            start=(ncnk == 0), stop=(ncnk == 1)))
                    p.ops("pe", fns, reads=["hs0", "hs1", "CwT"], writes=[PSF[bank]])
                p.op("dve", lambda E, vv=vv: E.tensor_mul(
                    du_scr[:], vv, Dw_bc[:]),
                    reads=[f"v{i}", "Dw_bc"], writes=["du"])
                p.op("dve", lambda E, xv=xv: E.tensor_add(
                    xv, xv, du_scr[:]),
                    reads=["du", f"xt{i}"], writes=[f"xt{i}"])
                for dh, bank in ((0, b0), (1, b1)):
                    p.op("dve", lambda E, xv=xv, dh=dh, bank=bank: E.tensor_add(
                        xv[:, dh * 512:(dh + 1) * 512],
                        xv[:, dh * 512:(dh + 1) * 512], psf[bank][:]),
                        reads=[PSF[bank], f"xt{i}"], writes=[f"xt{i}"])
                p.dma("sp", lambda E, i=i, xv=xv, t0=t0: E.dma_start(
                    dst.ap()[t0 + i * 128:t0 + (i + 1) * 128, :], xv),
                    reads=[f"xt{i}"], writes=[f"dst0_{i}_{s}"])
        s0.close()

    def stage1(src, dst):
        s1 = contextlib.ExitStack()
        def lsb(name, shape, dt=F32):
            return s1.enter_context(nc.sbuf_tensor(name, list(shape), dt))

        FCH = [128] * 21 + [48]
        w1w = [lsb(f"w1w{dc}", [128, DFF0], BF16) for dc in range(8)]
        w3w = [lsb(f"w3w{dc}", [128, DFF0], BF16) for dc in range(8)]
        w2r = [lsb(f"w2r{j}", [128, 1024], BF16) for j in range(22)]
        wfcol = lsb("wfcol", [128, 8])
        xs = lsb("xs", [128, 4 * 1024])
        zT = lsb("zT", [128, 8 * 512], BF16)
        gvT = lsb("gvT", [128, 22 * 512], BF16)
        sg = lsb("sg", [128, 2 * 512], BF16)
        x0t = lsb("x0t", [128, 2 * 1024])
        zscr = lsb("zscr", [128, 2 * 1024], BF16)

        for dc in range(8):
            p.dma("pool", lambda E, dc=dc: E.dma_start(
                w1w[dc][:], ext["w1T"].ap()[dc * 128:(dc + 1) * 128, :]),
                writes=[f"w1w{dc}"])
            p.dma("pool", lambda E, dc=dc: E.dma_start(
                w3w[dc][:], ext["w3T"].ap()[dc * 128:(dc + 1) * 128, :]),
                writes=[f"w3w{dc}"])
        p.dma("sp", lambda E: E.dma_start(
            wfcol[:], bass.AP(ext["ffn_w"], 0, [[1, 128], [128, 8]]),
            allow_slow_non_contiguous=True), writes=["wfcol"])
        for fc in range(22):
            p.dma("pool", lambda E, fc=fc: E.dma_start(
                w2r[fc][0:FCH[fc], :],
                ext["w2T"].ap()[fc * 128:fc * 128 + FCH[fc], :]),
                writes=[f"s1w2_{fc}"])
        for dc in range(8):
            p.op("dve", lambda E, dc=dc: E.tensor_scalar_mul(
                w1w[dc][:], w1w[dc][:], wfcol[:, dc:dc + 1]),
                reads=[f"w1w{dc}", "wfcol"], writes=[f"w1w{dc}"])
            p.op("dve", lambda E, dc=dc: E.tensor_scalar_mul(
                w3w[dc][:], w3w[dc][:], wfcol[:, dc:dc + 1]),
                reads=[f"w3w{dc}", "wfcol"], writes=[f"w3w{dc}"])

        rms = cols[:, 2:3]
        rstd = cols[:, 3:4]

        for s in range(SEQ_PER_CORE):
            for tc in range(4):
                t0 = s * S + tc * 512
                tcb = 0
                for it in range(4):
                    pr = it % 2
                    xv = xs[:, (tcb * 4 + it) * 1024:(tcb * 4 + it + 1) * 1024]
                    xk = f"s1x{tcb}_{it}"
                    x0v = x0t[:, pr * 1024:(pr + 1) * 1024]
                    x0k = f"s1x0t{pr}"
                    rmsv = cols[:, 2 + pr:3 + pr]
                    rstdv = colsB[:, pr:pr + 1]
                    rk = f"s1rms{pr}"
                    rsk = f"s1rstd{pr}"
                    p.dma("sp", lambda E, x0v=x0v, t0=t0, it=it: E.dma_start(
                        x0v, src.ap()[t0 + it * 128:t0 + (it + 1) * 128, :]),
                        writes=[x0k])
                    p.dma("act", lambda E, xv=xv, t0=t0, it=it: E.dma_start(
                        xv, ext["x"].ap()[t0 + it * 128:t0 + (it + 1) * 128, :]),
                        writes=[xk])
                    p.op("act", lambda E, x0v=x0v, pr=pr, rmsv=rmsv: E.activation(
                        sq_scr[:, pr * 512:pr * 512 + 512]
                        if False else sq_scr[:], x0v, ACT.Square, accum_out=rmsv),
                        reads=[x0k], writes=["sq_scr", rk])
                    p.op("act", lambda E, rmsv=rmsv: E.activation(
                        rmsv, rmsv, ACT.Sqrt, scale=1.0 / D, bias=EPS_RMS),
                        reads=[rk, "cpool"], writes=[rk])
                    p.op("dve", lambda E, rmsv=rmsv, rstdv=rstdv: E.reciprocal(
                        rstdv, rmsv), reads=[rk], writes=[rsk])
                    p.op("dve", lambda E, xv=xv, x0v=x0v: E.tensor_add(xv, xv, x0v),
                         reads=[xk, x0k], writes=[xk])
                    zv = zscr[:, pr * 1024:(pr + 1) * 1024]
                    zwk = f"s1z{pr}"
                    p.op("dve", lambda E, zv=zv, x0v=x0v, rstdv=rstdv: E.tensor_scalar_mul(
                        zv, x0v, rstdv),
                        reads=[x0k], sreads=[rsk],
                        writes=[zwk])
                    bank = it % 2
                    fns = []
                    for dc in range(8):
                        fns.append(lambda E, zv=zv, dc=dc, bank=bank: E.transpose(
                            psb[bank][:, dc * 128:(dc + 1) * 128],
                            zv[:, dc * 128:(dc + 1) * 128], identb[:]))
                    p.ops("pe", fns, reads=[zwk, "identb"], writes=[PSB[bank]])
                    p.op("dve" if it % 2 == 0 else "act",
                         lambda E, it=it, bank=bank, tcb=tcb: (
                             (E.tensor_copy if it % 2 == 0 else E.copy)(
                                 zT[:].rearrange("p (c t) -> p c t", c=8)
                                 [:, :, it * 128:(it + 1) * 128],
                                 psb[bank][:].rearrange("p (c t) -> p c t", c=8))),
                         reads=[PSB[bank]], writes=[f"s1zT{tcb}_{it}"])
                zkeys = [f"s1zT{tcb}_{it}" for it in range(4)]
                fo = 0
                for fc in range(22):
                    fw = FCH[fc]
                    gb = (fc % 2) * 2
                    vb = gb + 1
                    sgp = fc % 2
                    for which, wsb, bank in (("g", w1w, gb), ("v", w3w, vb)):
                        fns = []
                        for dc in range(8):
                            fns.append(lambda E, dc=dc, fo=fo, fw=fw, wsb=wsb,
                                       bank=bank, tcb=tcb: E.matmul(
                                psf[bank][0:fw, :],
                                wsb[dc][:, fo:fo + fw],
                                zT[:, tcb * 8 * 512 + dc * 512:
                                   tcb * 8 * 512 + (dc + 1) * 512],
                                start=(dc == 0), stop=(dc == 7)))
                        p.ops("pe", fns,
                              reads=[f"w{1 if which == 'g' else 3}w{dc}" for dc in range(8)] + zkeys,
                              writes=[PSF[bank]])
                    p.op("act", lambda E, fw=fw, gb=gb, sgp=sgp: E.activation(
                        sg[0:fw, sgp * 512:(sgp + 1) * 512], psf[gb][0:fw, :],
                        ACT.Silu),
                        reads=[PSF[gb]], writes=[f"s1sg{sgp}"])
                    p.op("dve", lambda E, fc=fc, fw=fw, vb=vb, sgp=sgp: E.tensor_mul(
                        gvT[0:fw, fc * 512:(fc + 1) * 512],
                        sg[0:fw, sgp * 512:(sgp + 1) * 512], psf[vb][0:fw, :]),
                        reads=[f"s1sg{sgp}", PSF[vb]], writes=[f"s1gv{fc}"])
                    fo += fw
                gvkeys = [f"s1gv{fc}" for fc in range(22)]
                for half in range(2):
                    for fc in range(22):
                        for ih in range(2):
                            it = half * 2 + ih
                            for dh in range(2):
                                bank = ih * 2 + dh
                                p.op("pe", lambda E, fc=fc, it=it, dh=dh,
                                     bank=bank: E.matmul(
                                    psf[bank][:],
                                    gvT[0:FCH[fc], fc * 512 + it * 128:
                                        fc * 512 + (it + 1) * 128],
                                    w2r[fc][0:FCH[fc], dh * 512:(dh + 1) * 512],
                                    start=(fc == 0), stop=(fc == 21)),
                                    reads=[f"s1gv{fc}", f"s1w2_{fc}"],
                                    writes=[PSF[bank]])
                    for ih in range(2):
                        it = half * 2 + ih
                        xv = xs[:, (tcb * 4 + it) * 1024:(tcb * 4 + it + 1) * 1024]
                        xk = f"s1x{tcb}_{it}"
                        for dh in range(2):
                            bank = ih * 2 + dh
                            p.op("dve", lambda E, xv=xv, dh=dh, bank=bank: E.tensor_add(
                                xv[:, dh * 512:(dh + 1) * 512],
                                xv[:, dh * 512:(dh + 1) * 512], psf[bank][:]),
                                reads=[PSF[bank], xk], writes=[xk])
                        emit_ln(xv, xk, dst, t0, it, f"s1dst{t0}_{it}", slot=it % 2)
        s1.close()

    def stage2(src, dst):
        s2 = contextlib.ExitStack()
        def lsb(name, shape, dt=F32):
            return s2.enter_context(nc.sbuf_tensor(name, list(shape), dt))

        ttl = {pp: lsb(f"sttl{pp}", [128, 4 * 4 * 512], BF16) for pp in "RI"}
        ttr = {pp: lsb(f"sttr{pp}", [128, 4 * 4 * 512], BF16) for pp in "RI"}
        WT = {pp: lsb(f"WT{pp}", [128, 4 * 512], BF16) for pp in "RI"}
        WnI = lsb("WnI", [128, 4 * 512], BF16)
        bias_bc = lsb("bias_bc", [128, 1024])
        brow = lsb("brow", [1, 1024])
        xbuf = lsb("s2xbuf", [128, 2 * 1024])
        xTb = lsb("s2xT", [128, 2 * 8 * 128], BF16)
        zb = lsb("s2zb", [128, 1024], BF16)

        for pp in "RI":
            for cc in range(4):
                for kc in range(4):
                    off = (cc * 4 + kc) * 512
                    p.dma("pool", lambda E, pp=pp, cc=cc, kc=kc, off=off: E.dma_start(
                        ttl[pp][:, off:off + 512],
                        ext["ttl" + pp].ap()[cc, kc * 128:(kc + 1) * 128, :]),
                        writes=[f"ttl{pp}"])
                    p.dma("pool", lambda E, pp=pp, cc=cc, kc=kc, off=off: E.dma_start(
                        ttr[pp][:, off:off + 512],
                        ext["ttr" + pp].ap()[cc, kc * 128:(kc + 1) * 128, :]),
                        writes=[f"ttr{pp}"])
        for pp in "RI":
            for ic in range(4):
                bank = ic % 2
                fns = []
                for cc in range(4):
                    for kc in range(4):
                        off = (cc * 4 + kc) * 512
                        fns.append(lambda E, pp=pp, ic=ic, cc=cc, kc=kc, off=off,
                                   bank=bank: E.matmul(
                            psf[bank][:],
                            ttl[pp][:, off + ic * 128:off + (ic + 1) * 128],
                            ttr[pp][:, off:off + 512],
                            start=(cc == 0 and kc == 0),
                            stop=(cc == 3 and kc == 3)))
                p.ops("pe", fns, reads=[f"ttl{pp}", f"ttr{pp}"],
                      writes=[PSF[bank]])
                p.op("dve", lambda E, pp=pp, ic=ic, bank=bank: E.tensor_copy(
                    WT[pp][:, ic * 512:(ic + 1) * 512], psf[bank][:]),
                    reads=[PSF[bank]], writes=[f"WT{pp}"])
        p.op("dve", lambda E: E.tensor_scalar_mul(WnI[:], WT["I"][:], -1.0),
             reads=["WTI"], writes=["WnI"])
        p.dma("sp", lambda E: E.dma_start(brow[0:1, :], ext["bias_ri"].ap()[None, :]),
              writes=["brow"])
        bcast_row(brow[0:1, :], bias_bc, 1024, ["brow"], "bias_bc", banks=(2, 3))

        for s in range(SEQ_PER_CORE):
            for it in range(NT):
                t0 = s * S
                tb = it % 2
                xv = xbuf[:, tb * 1024:(tb + 1) * 1024]
                xk = f"s2x{tb}"
                p.dma("sp", lambda E, xv=xv, t0=t0, it=it: E.dma_start(
                    xv, src.ap()[t0 + it * 128:t0 + (it + 1) * 128, :]),
                    writes=[xk])
                p.op("dve", lambda E, xv=xv: E.tensor_copy(zb[:], xv),
                     reads=[xk], writes=["s2z"])
                fns = []
                for dc in range(8):
                    fns.append(lambda E, dc=dc, tb=tb: E.transpose(
                        psb[tb][:, dc * 128:(dc + 1) * 128],
                        zb[:, dc * 128:(dc + 1) * 128], identb[:]))
                p.ops("pe", fns, reads=["s2z", "identb"], writes=[PSB[tb]])
                p.op("dve", lambda E, tb=tb: E.tensor_copy(
                    xTb[:, tb * 1024:(tb + 1) * 1024], psb[tb][:]),
                    reads=[PSB[tb]], writes=[f"s2xT{tb}"])
                for om, bank in ((0, 4), (1, 5)):
                    fns = []
                    for dc in range(8):
                        if om == 0:
                            wsb, wk = (WT["R"], "WTR") if dc < 4 else (WnI, "WnI")
                        else:
                            wsb, wk = (WT["I"], "WTI") if dc < 4 else (WT["R"], "WTR")
                        ic = dc % 4
                        fns.append(lambda E, dc=dc, ic=ic, wsb=wsb, tb=tb,
                                   bank=bank: E.matmul(
                            psf[bank][:],
                            xTb[:, tb * 1024 + dc * 128:tb * 1024 + (dc + 1) * 128],
                            wsb[:, ic * 512:(ic + 1) * 512],
                            start=(dc == 0), stop=(dc == 7)))
                    p.ops("pe", fns,
                          reads=[f"s2xT{tb}", "WTR", "WTI", "WnI"],
                          writes=[PSF[bank]])
                for om, bank in ((0, 4), (1, 5)):
                    p.op("dve", lambda E, xv=xv, om=om, bank=bank: E.tensor_add(
                        xv[:, om * 512:(om + 1) * 512],
                        xv[:, om * 512:(om + 1) * 512], psf[bank][:]),
                        reads=[PSF[bank], xk], writes=[xk])
                p.op("dve", lambda E, xv=xv: E.tensor_add(xv, xv, bias_bc[:]),
                     reads=[xk, "bias_bc"], writes=[xk])
                emit_ln(xv, xk, dst, t0, it, f"s2dst{t0}_{it}", slot=it % 2)
        s2.close()

    def stage3(src, dst):
        s3 = contextlib.ExitStack()
        def lsb(name, shape, dt=F32):
            return s3.enter_context(nc.sbuf_tensor(name, list(shape), dt))

        xT_h = nc.dram_tensor("xT_h", [128, 8 * S], BF16, kind="Internal")
        qT_h = nc.dram_tensor("qT_h", [128, 8 * S], BF16, kind="Internal")
        kt_h = nc.dram_tensor("kt_h", [128, NT * D], BF16, kind="Internal")

        kT = lsb("kT", [128, 8 * S], BF16)
        vtok = lsb("vtok", [128, NT * D], BF16)
        wbuf = lsb("wbuf", [128, 8 * D], BF16)
        wgb = lsb("wgb", [128, 8 * 3], BF16)
        PcP = [lsb(f"PcA{j}", [128, NH * NB], BF16) for j in range(2)]
        PwP = [lsb(f"PwA{j}", [128, NH * 384], BF16) for j in range(2)]
        PsP = [lsb(f"PsA{j}", [128, NH * 16], BF16) for j in range(2)]
        ckT = lsb("ckT", [128, 8 * NB], BF16)
        cvt = lsb("cvt", [128, D], BF16)
        cvstg = lsb("cvstg", [8, D], BF16)
        selkT = lsb("selkT", [128, 8 * 16], BF16)
        selv = lsb("selv", [16, D], BF16)
        selacc = lsb("selacc", [128, 8 * 16])
        selvacc = lsb("selvacc", [16, D])
        mkc = lsb("mkc", [128, NT * NB], BF16)
        mkw = lsb("mkw", [128, 3 * 384], BF16)
        cosb = lsb("cosb", [128, NT * 32], BF16)
        sinb = lsb("sinb", [128, NT * 32], BF16)
        g_all = lsb("g_all", [128, NT * 4])
        RSP = [lsb(f"RS{j}", [128, 3 * NH]) for j in range(2)]
        RCP = [lsb(f"RC{j}", [128, 3 * NH]) for j in range(2)]
        qb8 = lsb("qb8", [128, 8])
        qb8b = lsb("qb8b", [128, 8], BF16)
        xtile = lsb("s3xt", [128, 2 * D])
        xTt = lsb("s3xTt", [128, 2 * 8 * 128], BF16)
        qTt = lsb("s3qTt", [128, 2 * 8 * 128], BF16)
        qraw = lsb("qraw", [128, D], BF16)
        qrope = lsb("qrope", [128, D], BF16)
        rsc1 = lsb("rsc1", [128, 512], BF16)
        rsc2 = lsb("rsc2", [128, 512], BF16)
        NMP = [lsb(f"NMA{j}", [128, 3 * NH]) for j in range(2)]
        pscr = [lsb(f"pscr{j}", [128, 384], BF16) for j in range(2)]
        ptb = [lsb(f"ptb{j}", [128, 384], BF16) for j in range(2)]
        attnTP = [lsb(f"attnTA{j}", [128, 8 * 128], BF16) for j in range(2)]
        imps = lsb("imps", [1, 4 * NB])
        m12 = lsb("m12", [1, 8])
        posr = lsb("posr", [1, 16])
        posb = lsb("posb", [128, 16])
        selmP = [lsb(f"selmA{j}", [128, 16]) for j in range(2)]
        selm = selmP[0]
        selmb = lsb("selmb", [128, 16], BF16)
        rcolP = [lsb(f"rcolA{j}", [128, 1]) for j in range(2)]
        rcol = rcolP[0]
        iota_row = lsb("iota_row", [1, 128])
        cpe_r = lsb("cpe_r", [1, 2 * D])
        wkv_c = lsb("wkv_c", [16, 2])
        bgr = lsb("bgr", [1, 4])
        bor = lsb("bor", [1, D])
        pep = lsb("pep", [16, D])
        pep2 = lsb("pep2", [16, D])
        wcmp = lsb("wcmp", [128, 16], BF16)

        p.op("pe", lambda E: E.transpose(psf[0][0:1, 0:128], iota_p[:], identf[:]),
             reads=["iota_p", "identf"], writes=[PSF[0]])
        p.op("dve", lambda E: E.tensor_copy(iota_row[:], psf[0][0:1, 0:128]),
             reads=[PSF[0]], writes=["iota_row"])
        for i in range(NT):
            p.dma("pool", lambda E, i=i: E.dma_start(
                mkc[:, i * NB:(i + 1) * NB], ext["mask_comp"].ap()[i]),
                writes=["mkc"])
        for vv_ in range(3):
            p.dma("pool", lambda E, vv_=vv_: E.dma_start(
                mkw[:, vv_ * 384:(vv_ + 1) * 384], ext["mask_win"].ap()[vv_]),
                writes=["mkw"])
        for i in range(NT):
            p.dma("pool", lambda E, i=i: E.dma_start(
                cosb[:, i * 32:(i + 1) * 32],
                ext["cos_t"].ap()[i * 128:(i + 1) * 128, :]), writes=["cosb"])
            p.dma("pool", lambda E, i=i: E.dma_start(
                sinb[:, i * 32:(i + 1) * 32],
                ext["sin_t"].ap()[i * 128:(i + 1) * 128, :]), writes=["sinb"])
        for dc in range(8):
            p.dma("pool", lambda E, dc=dc: E.dma_start(
                wgb[:, dc * 3:(dc + 1) * 3],
                ext["Wg"].ap()[dc * 128:(dc + 1) * 128, :]), writes=["wgb"])
        p.dma("sp", lambda E: E.dma_start(bgr[0:1, 0:3], ext["bg"].ap()[None, :]),
              writes=["bgr"])
        p.dma("sp", lambda E: E.dma_start(bor[0:1, :], ext["bo"].ap()[None, :]),
              writes=["bor"])
        p.dma("sp", lambda E: E.dma_start(pep[:], ext["peP"].ap()), writes=["pep"])
        p.dma("sp", lambda E: E.dma_start(pep2[:], ext["pe_raw"].ap()), writes=["pep2"])
        p.dma("sp", lambda E: E.dma_start(
            wkv_c[:, 0:1], ext["wk_comp"].ap()[:, None]), writes=["wkv"])
        p.dma("sp", lambda E: E.dma_start(
            wkv_c[:, 1:2], ext["wv_comp"].ap()[:, None]), writes=["wkv"])
        p.dma("pool", lambda E: E.dma_start(wcmp[:, 0:8], ext["Wck"].ap()), writes=["wcmp"])
        p.dma("pool", lambda E: E.dma_start(wcmp[:, 8:16], ext["Wcv"].ap()), writes=["wcmp"])
        for which in range(2):
            for hh in range(2):
                bank = which
                p.op("pe", lambda E, which=which, hh=hh, bank=bank: E.matmul(
                    psf[bank][0:1, :],
                    wkv_c[:, which:which + 1],
                    (pep if which == 0 else pep2)[:, hh * 512:(hh + 1) * 512],
                    start=True, stop=True),
                    reads=["wkv", "pep", "pep2"], writes=[PSF[bank]])
                p.op("dve", lambda E, which=which, hh=hh, bank=bank: E.tensor_copy(
                    cpe_r[0:1, which * D + hh * 512:which * D + (hh + 1) * 512],
                    psf[bank][0:1, :]),
                    reads=[PSF[bank]], writes=["cpe_r"])

        def load_w(name, wkey=None):
            for dc in range(8):
                p.dma("pool", lambda E, dc=dc, name=name: E.dma_start(
                    wbuf[:, dc * D:(dc + 1) * D],
                    ext[name].ap()[dc * 128:(dc + 1) * 128, :]),
                    writes=["wbuf"])

        def rope(i, srcb, srck, dstb, dstk):
            cs = cosb[:, i * 32:(i + 1) * 32].unsqueeze(1).broadcast_to([128, 16, 32])
            sn = sinb[:, i * 32:(i + 1) * 32].unsqueeze(1).broadcast_to([128, 16, 32])
            e = srcb[:].rearrange("p (h c) -> p h c", c=64)[:, :, 0:32]
            o = srcb[:].rearrange("p (h c) -> p h c", c=64)[:, :, 32:64]
            de = dstb[:].rearrange("p (h c) -> p h c", c=64)[:, :, 0:32]
            do = dstb[:].rearrange("p (h c) -> p h c", c=64)[:, :, 32:64]
            r1 = rsc1[:].rearrange("p (h c) -> p h c", c=32)
            r2 = rsc2[:].rearrange("p (h c) -> p h c", c=32)
            p.op("dve", lambda E, e=e, cs=cs, r1=r1: E.tensor_mul(r1, e, cs),
                 reads=[srck, "cosb"], writes=["rsc1"])
            p.op("dve", lambda E, o=o, sn=sn, r2=r2: E.tensor_mul(r2, o, sn),
                 reads=[srck, "sinb"], writes=["rsc2"])
            p.op("dve", lambda E, de=de, r1=r1, r2=r2: E.tensor_sub(de, r1, r2),
                 reads=["rsc1", "rsc2"], writes=[dstk])
            p.op("dve", lambda E, e=e, sn=sn, r1=r1: E.tensor_mul(r1, e, sn),
                 reads=[srck, "sinb"], writes=["rsc1"])
            p.op("dve", lambda E, o=o, cs=cs, r2=r2: E.tensor_mul(r2, o, cs),
                 reads=[srck, "cosb"], writes=["rsc2"])
            p.op("dve", lambda E, do=do, r1=r1, r2=r2: E.tensor_add(do, r1, r2),
                 reads=["rsc1", "rsc2"], writes=[dstk])

        def proj_tile(i, outb, outk):
            tb = i % 2
            xTk = f"s3xTl{tb}"
            p.dma("sp", lambda E, tb=tb, i=i: E.dma_start(
                xTt[:, tb * 1024:(tb + 1) * 1024].rearrange("p (c t) -> p c t", c=8),
                xT_h.ap().rearrange("p (c t) -> p c t", c=8)
                [:, :, i * 128:(i + 1) * 128]),
                writes=[xTk])
            for hh in range(2):
                bank = hh
                fns = []
                for dc in range(8):
                    fns.append(lambda E, dc=dc, tb=tb, hh=hh, bank=bank: E.matmul(
                        psf[bank][:],
                        xTt[:, tb * 1024 + dc * 128:tb * 1024 + (dc + 1) * 128],
                        wbuf[:, dc * D + hh * 512:dc * D + (hh + 1) * 512],
                        start=(dc == 0), stop=(dc == 7)))
                p.ops("pe", fns, reads=[xTk, "wbuf"], writes=[PSF[bank]])
            for hh in range(2):
                p.op("act", lambda E, hh=hh, outb=outb: E.copy(
                    outb[:, hh * 512:(hh + 1) * 512], psf[hh][:]),
                    reads=[PSF[hh]], writes=[outk])
            return tb

        for s in range(SEQ_PER_CORE):
            t0 = s * S
            for i in range(NT):
                tb = i % 2
                xv = xtile[:, tb * D:(tb + 1) * D]
                xk = f"s3x{tb}"
                p.dma("sp", lambda E, xv=xv, t0=t0, i=i: E.dma_start(
                    xv, src.ap()[t0 + i * 128:t0 + (i + 1) * 128, :]), writes=[xk])
                p.op("dve", lambda E, xv=xv: E.tensor_copy(qraw[:], xv),
                     reads=[xk], writes=["qraw"])
                fns = []
                for dc in range(8):
                    fns.append(lambda E, dc=dc, tb=tb: E.transpose(
                        psb[tb][:, dc * 128:(dc + 1) * 128],
                        qraw[:, dc * 128:(dc + 1) * 128], identb[:]))
                p.ops("pe", fns, reads=["qraw", "identb"], writes=[PSB[tb]])
                xTk = f"s3xTl{tb}"
                p.op("dve", lambda E, tb=tb: E.tensor_copy(
                    xTt[:, tb * 1024:(tb + 1) * 1024], psb[tb][:]),
                    reads=[PSB[tb]], writes=[xTk])
                p.dma("sp", lambda E, tb=tb, i=i: E.dma_start(
                    xT_h.ap().rearrange("p (c t) -> p c t", c=8)
                    [:, :, i * 128:(i + 1) * 128],
                    xTt[:, tb * 1024:(tb + 1) * 1024].rearrange(
                        "p (c t) -> p c t", c=8)),
                    reads=[xTk], writes=[f"xTh{i}"])
                fns = []
                for dc in range(8):
                    fns.append(lambda E, dc=dc, tb=tb: E.matmul(
                        psf[4][:, 0:3],
                        xTt[:, tb * 1024 + dc * 128:tb * 1024 + (dc + 1) * 128],
                        wgb[:, dc * 3:(dc + 1) * 3],
                        start=(dc == 0), stop=False))
                fns.append(lambda E: E.matmul(
                    psf[4][:, 0:3], ones_row[0:1, :], bgr[0:1, 0:3],
                    start=False, stop=True))
                p.ops("pe", fns, reads=[xTk, "wgb", "bgr", "ones_row"],
                      writes=[PSF[4]])
                gm = cols[:, 9:10]
                gd = cols[:, 10:11]
                p.op("dve", lambda E: E.reduce_max(gm, psf[4][:, 0:3], axis=AX),
                     reads=[PSF[4]], writes=["s3gm"])
                p.op("dve", lambda E: E.tensor_scalar_mul(gm, gm, -1.0),
                     sreads=["s3gm"], writes=["s3gm"])
                gv = g_all[:, i * 4:i * 4 + 3]
                p.op("act", lambda E, gv=gv: E.activation(
                    gv, psf[4][:, 0:3], ACT.Exp, bias=gm, accum_out=gd),
                    reads=[PSF[4], "s3gm"], writes=[f"s3g{i}", "s3gd"])
                p.op("dve", lambda E: E.reciprocal(gd, gd),
                     reads=["s3gd"], writes=["s3gd"])
                p.op("dve", lambda E, gv=gv: E.tensor_scalar_mul(gv, gv, gd),
                     reads=[f"s3g{i}"], sreads=["s3gd"], writes=[f"s3g{i}"])
            load_w("WqP")
            qbar = cols[:, 11:12]
            p.op("dve", lambda E: E.memset(qb8[:], 0.0), writes=["qbar8"])
            for i in range(NT):
                tb = proj_tile(i, qraw, "qraw")
                rope(i, qraw, "qraw", qrope, "qrope")
                fns = []
                for dc in range(8):
                    fns.append(lambda E, dc=dc, tb=tb: E.transpose(
                        psb[tb][:, dc * 128:(dc + 1) * 128],
                        qrope[:, dc * 128:(dc + 1) * 128], identb[:]))
                p.ops("pe", fns, reads=["qrope", "identb"], writes=[PSB[tb]])
                qTk = f"s3qT{tb}"
                p.op("dve", lambda E, tb=tb: E.tensor_copy(
                    qTt[:, tb * 1024:(tb + 1) * 1024], psb[tb][:]),
                    reads=[PSB[tb]], writes=[qTk])
                p.dma("sp", lambda E, tb=tb, i=i: E.dma_start(
                    qT_h.ap().rearrange("p (c t) -> p c t", c=8)
                    [:, :, i * 128:(i + 1) * 128],
                    qTt[:, tb * 1024:(tb + 1) * 1024].rearrange(
                        "p (c t) -> p c t", c=8)),
                    reads=[qTk], writes=[f"qTh{i}"])
                for dc in range(8):
                    p.op("dve", lambda E, tb=tb, dc=dc: E.reduce_sum(
                        qbar, qTt[:, tb * 1024 + dc * 128:tb * 1024 + (dc + 1) * 128],
                        axis=AX), reads=[qTk], writes=["s3qbar"])
                    p.op("dve", lambda E, dc=dc: E.tensor_add(
                        qb8[:, dc:dc + 1], qb8[:, dc:dc + 1], qbar),
                        reads=["qbar8"], sreads=["s3qbar"], writes=["qbar8"])
            load_w("Wv")
            for i in range(NT):
                tb = i % 2
                vk = f"s3v{i}"
                proj_tile(i, qraw, "qraw")
                p.op("dve", lambda E, i=i: E.tensor_copy(
                    vtok[:, i * D:(i + 1) * D], qraw[:]),
                    reads=["qraw"], writes=[vk])
                for hh in range(2):
                    bank = 2 + hh
                    fns = [lambda E, i=i, hh=hh, bank=bank: E.matmul(
                        psf[bank][0:8, :],
                        wcmp[:, 8:16],
                        vtok[:, i * D + hh * 512:i * D + (hh + 1) * 512],
                        start=True, stop=False),
                        lambda E, hh=hh, bank=bank: E.matmul(
                        psf[bank][0:8, :], ones_row[0:1, 0:8],
                        cpe_r[0:1, D + hh * 512:D + (hh + 1) * 512],
                        start=False, stop=True)]
                    p.ops("pe", fns, reads=[vk, "wcmp", "ones_row", "cpe_r"],
                          writes=[PSF[bank]])
                    p.op("dve", lambda E, i=i, hh=hh, bank=bank: E.tensor_copy(
                        cvstg[:, hh * 512:(hh + 1) * 512],
                        psf[bank][0:8, :]),
                        reads=[PSF[bank]], writes=["cvstg"])
                p.dma("pool", lambda E, i=i: E.dma_start(
                    cvt[8 * i:8 * i + 8, :], cvstg[:]),
                    reads=["cvstg"], writes=["cvt"])
            load_w("WkP")
            for i in range(NT):
                tb = i % 2
                proj_tile(i, qraw, "qraw")
                rope(i, qraw, "qraw", qrope, "qrope")
                p.dma("sp", lambda E, i=i: E.dma_start(
                    kt_h.ap()[:, i * D:(i + 1) * D], qrope[:]),
                    reads=["qrope"], writes=[f"kth{i}"])
                fns = []
                for dc in range(8):
                    fns.append(lambda E, dc=dc, tb=tb: E.transpose(
                        psb[tb][:, dc * 128:(dc + 1) * 128],
                        qrope[:, dc * 128:(dc + 1) * 128], identb[:]))
                p.ops("pe", fns, reads=["qrope", "identb"], writes=[PSB[tb]])
                p.op("dve", lambda E, tb=tb, i=i: E.tensor_copy(
                    kT[:].rearrange("p (c t) -> p c t", c=8)
                    [:, :, i * 128:(i + 1) * 128],
                    psb[tb][:].rearrange("p (c t) -> p c t", c=8)),
                    reads=[PSB[tb]], writes=[f"kTt{i}"])
                fns = []
                for dc in range(8):
                    fns.append(lambda E, i=i, dc=dc: E.matmul(
                        psf[4][:, dc * 8:(dc + 1) * 8],
                        qrope[:, dc * 128:(dc + 1) * 128],
                        wcmp[:, 0:8],
                        start=True, stop=False))
                    fns.append(lambda E, dc=dc: E.matmul(
                        psf[4][:, dc * 8:(dc + 1) * 8],
                        cpe_r[0:1, dc * 128:(dc + 1) * 128],
                        ones_row[0:1, 0:8],
                        start=False, stop=True))
                p.ops("pe", fns, reads=["qrope", "wcmp", "ones_row", "cpe_r"],
                      writes=[PSF[4]])
                p.op("dve", lambda E, i=i: E.tensor_copy(
                    ckT[:].rearrange("p (c n) -> p c n", c=8)[:, :, 8 * i:8 * i + 8],
                    psf[4][:, 0:64].rearrange("p (c n) -> p c n", n=8)),
                    reads=[PSF[4]], writes=["ckT"])
            p.op("dve", lambda E: E.tensor_copy(qb8b[:], qb8[:]),
                 reads=["qbar8"], writes=["qb8b"])
            fns = []
            for dc in range(8):
                fns.append(lambda E, dc=dc: E.matmul(
                    psf[0][0:1, 0:NB],
                    qb8b[:, dc:dc + 1],
                    ckT[:, dc * NB:(dc + 1) * NB],
                    start=(dc == 0), stop=(dc == 7)))
            p.ops("pe", fns, reads=["qb8b", "ckT"], writes=[PSF[0]])
            imp = imps[0:1, 0:NB]
            d1 = imps[0:1, NB:2 * NB]
            e1 = imps[0:1, 2 * NB:3 * NB]
            p.op("dve", lambda E: E.tensor_copy(imp, psf[0][0:1, 0:NB]),
                 reads=[PSF[0]], writes=["imp"])
            m1 = m12[0:1, 0:1]; m2 = m12[0:1, 1:2]
            i1v = m12[0:1, 2:3]; i2v = m12[0:1, 3:4]
            p.op("dve", lambda E: E.reduce_max(m1, imp, axis=AX),
                 reads=["imp"], writes=["m1"])
            p.op("dve", lambda E: E.tensor_scalar(d1, imp, m1, -1.0e12,
                                                  ALU.subtract, ALU.mult),
                 reads=["imp"], sreads=["m1"], writes=["d1"])
            p.op("dve", lambda E: E.tensor_add(d1, d1, iota_row[:]),
                 reads=["iota_row"], sreads=["d1"], writes=["d1"])
            p.op("dve", lambda E: E.tensor_reduce(i1v, d1, axis=AX, op=ALU.min),
                 reads=["d1"], writes=["i1v"])
            p.op("dve", lambda E: E.tensor_scalar(e1, imp, m1, -1.0e18,
                                                  ALU.is_ge, ALU.mult),
                 reads=["imp"], sreads=["m1"], writes=["e1"])
            p.op("dve", lambda E: E.tensor_add(e1, e1, imp),
                 reads=["imp"], sreads=["e1"], writes=["e1"])
            p.op("dve", lambda E: E.reduce_max(m2, e1, axis=AX),
                 reads=["e1"], writes=["m2"])
            p.op("dve", lambda E: E.tensor_scalar(d1, e1, m2, -1.0e12,
                                                  ALU.subtract, ALU.mult),
                 reads=["e1"], sreads=["m2"], writes=["d1"])
            p.op("dve", lambda E: E.tensor_add(d1, d1, iota_row[:]),
                 reads=["iota_row"], sreads=["d1"], writes=["d1"])
            p.op("dve", lambda E: E.tensor_reduce(i2v, d1, axis=AX, op=ALU.min),
                 reads=["d1"], writes=["i2v"])
            p.op("dve", lambda E: E.tensor_scalar_mul(i1v, i1v, 16.0),
                 sreads=["i1v"], writes=["i1v"])
            p.op("dve", lambda E: E.tensor_scalar_mul(i2v, i2v, 16.0),
                 sreads=["i2v"], writes=["i2v"])
            p.op("dve", lambda E: E.tensor_scalar(
                posr[0:1, 0:8], iota_row[0:1, 0:8], i1v, None, ALU.add),
                reads=["iota_row"], sreads=["i1v"], writes=["posr"])
            p.op("dve", lambda E: E.tensor_scalar(
                posr[0:1, 8:16], iota_row[0:1, 0:8], i2v, None, ALU.add),
                reads=["iota_row"], sreads=["i2v"], writes=["posr"])
            for hh in range(1):
                p.op("pe", lambda E: E.matmul(
                    psf[1][:, 0:16], ones_row[0:1, :], posr[0:1, :],
                    start=True, stop=True),
                    reads=["posr", "ones_row"], writes=[PSF[1]])
            p.op("dve", lambda E: E.tensor_copy(posb[:], psf[1][:, 0:16]),
                 reads=[PSF[1]], writes=["posb"])
            p.op("dve", lambda E: E.memset(selacc[:], 0.0), writes=["selacc"])
            p.op("dve", lambda E: E.memset(selvacc[:], 0.0), writes=["selvacc"])
            for i in range(NT):
                p.dma("sp", lambda E, i=i: E.dma_start(
                    qraw[:], kt_h.ap()[:, i * D:(i + 1) * D]),
                    writes=["qraw"])
                gk = "s3G"
                p.op("dve", lambda E, i=i: E.tensor_scalar(
                    rcol[:], iota_p[:], 1.0, float(i * 128), ALU.mult, ALU.add),
                    reads=["iota_p"], writes=["s3rcol"])
                p.op("dve", lambda E: E.tensor_scalar(
                    selmb[:], posb[:], rcol[:, 0:1], None, ALU.is_equal),
                    reads=["posb"], sreads=["s3rcol"], writes=[gk])
                fns = []
                for dc in range(8):
                    fns.append(lambda E, i=i, dc=dc: E.matmul(
                        psf[2][:, dc * 16:(dc + 1) * 16],
                        qraw[:, dc * 128:(dc + 1) * 128],
                        selmb[:],
                        start=(dc == 0), stop=(dc == 7)))
                p.ops("pe", fns, reads=["qraw", gk], writes=[PSF[2]])
                p.op("dve", lambda E: E.tensor_add(
                    selacc[:], selacc[:], psf[2][:, 0:128]),
                    reads=[PSF[2], "selacc"], writes=["selacc"])
                for hh in range(2):
                    p.op("pe", lambda E, i=i, hh=hh: E.matmul(
                        psf[3][0:16, 0:512] if hh == 0 else psf[5][0:16, 0:512],
                        selmb[:],
                        vtok[:, i * D + hh * 512:i * D + (hh + 1) * 512],
                        start=True, stop=True),
                        reads=[gk, f"s3v{i}"],
                        writes=[PSF[3] if hh == 0 else PSF[5]])
                    p.op("dve", lambda E, hh=hh: E.tensor_add(
                        selvacc[0:16, hh * 512:(hh + 1) * 512],
                        selvacc[0:16, hh * 512:(hh + 1) * 512],
                        (psf[3] if hh == 0 else psf[5])[0:16, :]),
                        reads=[PSF[3] if hh == 0 else PSF[5], "selvacc"],
                        writes=["selvacc"])
            p.op("dve", lambda E: E.tensor_copy(selkT[:], selacc[:]),
                 reads=["selacc"], writes=["selkT"])
            p.op("dve", lambda E: E.tensor_copy(selv[:], selvacc[0:16, :]),
                 reads=["selvacc"], writes=["selv"])
            load_w("Wo")
            def c_p1(i):
                tb = i % 2
                qTk = f"s3qTl{tb}"
                p.dma("sp", lambda E, tb=tb, i=i: E.dma_start(
                    qTt[:, tb * 1024:(tb + 1) * 1024].rearrange(
                        "p (c t) -> p c t", c=8),
                    qT_h.ap().rearrange("p (c t) -> p c t", c=8)
                    [:, :, i * 128:(i + 1) * 128]),
                    writes=[qTk])
                if i == 0:
                    W0, wlen, mv = 0, 128, 0
                elif i == 1:
                    W0, wlen, mv = 0, 256, 1
                else:
                    W0, wlen, mv = (i - 2) * 128, 384, 2
                ip = i % 2
                Pc, Pw, Ps = PcP[ip], PwP[ip], PsP[ip]
                mskC, mskW, mskS = Pc, Pw, Ps
                NM, RS, RC = NMP[ip], RSP[ip], RCP[ip]
                attnT = attnTP[ip]
                selmv = selmP[ip]
                rcolv = rcolP[ip]
                X_ = lambda k, ip=ip: f"{k}_{ip}"
                p.op("dve", lambda E, i=i, rcolv=rcolv: E.tensor_scalar(
                    rcolv[:], iota_p[:], 1.0, float(i * 128), ALU.mult, ALU.add),
                    reads=["iota_p"], writes=[X_("s3rcol")])
                p.op("dve", lambda E, selmv=selmv, rcolv=rcolv: E.tensor_scalar(
                    selmv[:], posb[:], rcolv[:, 0:1], None, ALU.is_le),
                    reads=["posb"], sreads=[X_("s3rcol")], writes=[X_("s3selm")])
                p.op("dve", lambda E, selmv=selmv: E.tensor_scalar(
                    selmv[:], selmv[:], 3.0e38, NEG, ALU.mult, ALU.add),
                    sreads=[X_("s3selm")], writes=[X_("s3selm")])
                for h in range(NH):
                    qsl = qTt[(h % 2) * 64:(h % 2) * 64 + 64,
                              tb * 1024 + (h // 2) * 128:
                              tb * 1024 + (h // 2) * 128 + 128]
                    bank = h % 4
                    p.op("pe", lambda E, h=h, bank=bank, qsl=qsl: E.matmul(
                        psf[bank][:, 0:NB],
                        qsl,
                        ckT[(h % 2) * 64:(h % 2) * 64 + 64,
                            (h // 2) * NB:(h // 2) * NB + NB],
                        start=True, stop=True),
                        reads=[qTk, "ckT"], writes=[PSF[bank]])
                    p.op("dve", lambda E, i=i, h=h, bank=bank, mskC=mskC: E.scalar_tensor_tensor(
                        mskC[:, h * NB:(h + 1) * NB], psf[bank][:, 0:NB], SCALE_ATT,
                        mkc[:, i * NB:(i + 1) * NB], ALU.mult, ALU.add),
                        reads=[PSF[bank], "mkc"], writes=[X_("Pc") + str(h)])
                    p.op("dve", lambda E, h=h, mskC=mskC, NM=NM: E.tensor_reduce(
                        NM[:, h:h + 1], mskC[:, h * NB:(h + 1) * NB], axis=AX,
                        op=ALU.max, negate=True),
                        reads=[X_("Pc") + str(h)], writes=[X_("nmc") + str(h)])
                    p.op("act", lambda E, h=h, Pc=Pc, mskC=mskC, NM=NM, RS=RS: E.activation(
                        Pc[:, h * NB:(h + 1) * NB], mskC[:, h * NB:(h + 1) * NB],
                        ACT.Exp, bias=NM[:, h:h + 1], accum_out=RS[:, h:h + 1]),
                        reads=[X_("Pc") + str(h), X_("nmc") + str(h)], writes=[X_("Pc") + str(h), X_("RSc") + str(h)])
                    bank = (h + 1) % 4
                    p.op("pe", lambda E, h=h, bank=bank, qsl=qsl, W0=W0,
                         wlen=wlen: E.matmul(
                        psf[bank][:, 0:wlen],
                        qsl,
                        kT[(h % 2) * 64:(h % 2) * 64 + 64,
                           (h // 2) * S + W0:(h // 2) * S + W0 + wlen],
                        start=True, stop=True),
                        reads=[qTk, f"kTt{i}", f"kTt{max(i-1,0)}", f"kTt{max(i-2,0)}"],
                        writes=[PSF[bank]])
                    p.op("dve", lambda E, h=h, bank=bank, mv=mv, wlen=wlen, mskW=mskW: E.scalar_tensor_tensor(
                        mskW[:, h * 384:h * 384 + wlen], psf[bank][:, 0:wlen], SCALE_ATT,
                        mkw[:, mv * 384:mv * 384 + wlen], ALU.mult, ALU.add),
                        reads=[PSF[bank], "mkw"], writes=[X_("Pw") + str(h)])
                    p.op("dve", lambda E, h=h, wlen=wlen, mskW=mskW, NM=NM: E.tensor_reduce(
                        NM[:, NH + h:NH + h + 1], mskW[:, h * 384:h * 384 + wlen],
                        axis=AX, op=ALU.max, negate=True),
                        reads=[X_("Pw") + str(h)], writes=[X_("nmw") + str(h)])
                    p.op("act", lambda E, h=h, wlen=wlen, Pw=Pw, mskW=mskW, NM=NM, RS=RS: E.activation(
                        Pw[:, h * 384:h * 384 + wlen], mskW[:, h * 384:h * 384 + wlen],
                        ACT.Exp, bias=NM[:, NH + h:NH + h + 1],
                        accum_out=RS[:, NH + h:NH + h + 1]),
                        reads=[X_("Pw") + str(h), X_("nmw") + str(h)], writes=[X_("Pw") + str(h), X_("RSw") + str(h)])
                    bank = (h + 2) % 4
                    p.op("pe", lambda E, h=h, bank=bank, qsl=qsl: E.matmul(
                        psf[bank][:, 0:16],
                        qsl,
                        selkT[(h % 2) * 64:(h % 2) * 64 + 64,
                              (h // 2) * 16:(h // 2) * 16 + 16],
                        start=True, stop=True),
                        reads=[qTk, "selkT"], writes=[PSF[bank]])
                    p.op("dve", lambda E, h=h, bank=bank, mskS=mskS, selmv=selmv: E.scalar_tensor_tensor(
                        mskS[:, h * 16:(h + 1) * 16], psf[bank][:, 0:16], SCALE_ATT,
                        selmv[:], ALU.mult, ALU.add),
                        reads=[PSF[bank], X_("s3selm")], writes=[X_("Ps") + str(h)])
                    p.op("dve", lambda E, h=h, mskS=mskS, NM=NM: E.tensor_reduce(
                        NM[:, 2 * NH + h:2 * NH + h + 1], mskS[:, h * 16:(h + 1) * 16],
                        axis=AX, op=ALU.max, negate=True),
                        reads=[X_("Ps") + str(h)], writes=[X_("nms") + str(h)])
                    p.op("act", lambda E, h=h, Ps=Ps, mskS=mskS, NM=NM, RS=RS: E.activation(
                        Ps[:, h * 16:(h + 1) * 16], mskS[:, h * 16:(h + 1) * 16],
                        ACT.Exp, bias=NM[:, 2 * NH + h:2 * NH + h + 1],
                        accum_out=RS[:, 2 * NH + h:2 * NH + h + 1]),
                        reads=[X_("Ps") + str(h), X_("nms") + str(h)], writes=[X_("Ps") + str(h), X_("RSs") + str(h)])
                rsck = [X_("RSc") + str(h) for h in range(NH)]
                rswk = [X_("RSw") + str(h) for h in range(NH)]
                rssk = [X_("RSs") + str(h) for h in range(NH)]
                nmck = [X_("nmc") + str(h) for h in range(NH)]
                nmsk = [X_("nms") + str(h) for h in range(NH)]
                p.op("dve", lambda E, RS=RS, RC=RC: E.reciprocal(RC[:, 0:NH], RS[:, 0:NH]),
                     reads=rsck, writes=[X_(X_("RCc"))])
                p.op("dve", lambda E, RS=RS, RC=RC: E.reciprocal(RC[:, NH:2 * NH], RS[:, NH:2 * NH]),
                     reads=rswk, writes=[X_(X_("RCw"))])
                p.op("dve", lambda E, RS=RS, RC=RC: E.reciprocal(RC[:, 2 * NH:3 * NH],
                                                   RS[:, 2 * NH:3 * NH]),
                     reads=rssk, writes=[X_(X_("RCs"))])
                p.op("dve", lambda E, i=i, RC=RC: E.tensor_scalar_mul(
                    RC[:, 0:NH], RC[:, 0:NH], g_all[:, i * 4:i * 4 + 1]),
                    reads=[f"s3g{i}"], sreads=[X_(X_("RCc"))], writes=[X_(X_("RCc"))])
                p.op("dve", lambda E, i=i, RC=RC: E.tensor_scalar_mul(
                    RC[:, NH:2 * NH], RC[:, NH:2 * NH], g_all[:, i * 4 + 2:i * 4 + 3]),
                    reads=[f"s3g{i}"], sreads=[X_(X_("RCw"))], writes=[X_(X_("RCw"))])
                p.op("dve", lambda E, i=i, RC=RC: E.tensor_scalar_mul(
                    RC[:, 2 * NH:3 * NH], RC[:, 2 * NH:3 * NH],
                    g_all[:, i * 4 + 1:i * 4 + 2]),
                    reads=[f"s3g{i}"], sreads=[X_(X_("RCs"))], writes=[X_(X_("RCs"))])
                p.op("dve", lambda E, NM=NM: E.tensor_scalar(
                    NM[:, 0:NH], NM[:, 0:NH], 1.0e37, None, ALU.is_lt),
                    reads=nmck, writes=[X_(X_("VMc"))] + nmck)
                p.op("dve", lambda E, NM=NM: E.tensor_scalar(
                    NM[:, 2 * NH:3 * NH], NM[:, 2 * NH:3 * NH], 1.0e37, None,
                    ALU.is_lt), reads=nmsk, writes=[X_(X_("VMs"))] + nmsk)
                p.op("dve", lambda E, NM=NM, RC=RC: E.tensor_mul(
                    RC[:, 0:NH], RC[:, 0:NH], NM[:, 0:NH]),
                    reads=[X_(X_("VMc"))], sreads=[X_(X_("RCc"))], writes=[X_(X_("RCc"))])
                p.op("dve", lambda E, NM=NM, RC=RC: E.tensor_mul(
                    RC[:, 2 * NH:3 * NH], RC[:, 2 * NH:3 * NH], NM[:, 2 * NH:3 * NH]),
                    reads=[X_(X_("VMs"))], sreads=[X_(X_("RCs"))], writes=[X_(X_("RCs"))])
            def c_p2(i):
                tb = i % 2
                if i == 0:
                    W0, wlen, mv = 0, 128, 0
                elif i == 1:
                    W0, wlen, mv = 0, 256, 1
                else:
                    W0, wlen, mv = (i - 2) * 128, 384, 2
                ip = i % 2
                Pc, Pw, Ps = PcP[ip], PwP[ip], PsP[ip]
                NM, RS, RC = NMP[ip], RSP[ip], RCP[ip]
                attnT = attnTP[ip]
                X_ = lambda k, ip=ip: f"{k}_{ip}"
                for h in range(NH):
                    ob = 4 + (h // 2) % 2
                    obp = (h % 2) * 64
                    pj = h % 2
                    ps_ = pscr[pj]
                    pt_ = ptb[pj]
                    pk = f"s3ps{pj}"
                    ptk = f"s3pt{pj}"
                    tbank = pj
                    p.op("dve", lambda E, h=h, ps_=ps_, Pc=Pc, RC=RC: E.tensor_scalar_mul(
                        ps_[:, 0:NB], Pc[:, h * NB:(h + 1) * NB],
                        RC[:, h:h + 1]),
                        reads=[X_("Pc") + str(h)], sreads=[X_(X_("RCc"))], writes=[pk])
                    p.op("pe", lambda E, ps_=ps_, tbank=tbank: E.transpose(
                        psb[tbank][:, 0:128], ps_[:, 0:NB], identb[:]),
                        reads=[pk, "identb"], writes=[PSB[tbank]])
                    p.op("dve", lambda E, pt_=pt_, tbank=tbank: E.tensor_copy(
                        pt_[:, 0:128], psb[tbank][:, 0:128]),
                        reads=[PSB[tbank]], writes=[ptk])
                    p.op("pe", lambda E, h=h, ob=ob, obp=obp, pt_=pt_: E.matmul(
                        psf[ob][obp:obp + 64, 0:128],
                        cvt[:, h * 64:(h + 1) * 64],
                        pt_[:, 0:128],
                        start=True, stop=False,
                        tile_position=(0, obp)),
                        reads=[ptk, "cvt"], writes=[PSF[ob]])
                    p.op("dve", lambda E, h=h, ps_=ps_, Ps=Ps, RC=RC: E.tensor_scalar_mul(
                        ps_[:, 128:144], Ps[:, h * 16:(h + 1) * 16],
                        RC[:, 2 * NH + h:2 * NH + h + 1]),
                        reads=[X_("Ps") + str(h)], sreads=[X_(X_("RCs"))], writes=[pk])
                    p.op("pe", lambda E, ps_=ps_, tbank=tbank: E.transpose(
                        psb[tbank][0:16, 128:256], ps_[:, 128:144], identb[:]),
                        reads=[pk, "identb"], writes=[PSB[tbank]])
                    p.op("dve", lambda E, pt_=pt_, tbank=tbank: E.tensor_copy(
                        pt_[0:16, 128:256], psb[tbank][0:16, 128:256]),
                        reads=[PSB[tbank]], writes=[ptk])
                    p.op("pe", lambda E, h=h, ob=ob, obp=obp, pt_=pt_: E.matmul(
                        psf[ob][obp:obp + 64, 0:128],
                        selv[0:16, h * 64:(h + 1) * 64],
                        pt_[0:16, 128:256],
                        start=False, stop=False,
                        tile_position=(0, obp)),
                        reads=[ptk, "selv"], writes=[PSF[ob]])
                    p.op("dve", lambda E, h=h, wlen=wlen, ps_=ps_, Pw=Pw, RC=RC: E.tensor_scalar_mul(
                        ps_[:, 0:wlen], Pw[:, h * 384:h * 384 + wlen],
                        RC[:, NH + h:NH + h + 1]),
                        reads=[X_("Pw") + str(h)], sreads=[X_(X_("RCw"))], writes=[pk])
                    nwc = wlen // 128
                    fns = []
                    for wc in range(nwc):
                        fns.append(lambda E, wc=wc, ps_=ps_, tbank=tbank, Pw=Pw, RC=RC: E.transpose(
                            psb[tbank][:, 256 + wc * 128:256 + (wc + 1) * 128],
                            ps_[:, wc * 128:(wc + 1) * 128], identb[:]))
                    p.ops("pe", fns, reads=[pk, "identb"], writes=[PSB[tbank]])
                    p.op("dve", lambda E, nwc=nwc, pt_=pt_, tbank=tbank: E.tensor_copy(
                        pt_[:, 0:nwc * 128], psb[tbank][:, 256:256 + nwc * 128]),
                        reads=[PSB[tbank]], writes=[ptk])
                    fns = []
                    for wc in range(nwc):
                        kt_tile = (W0 // 128) + wc
                        fns.append(lambda E, h=h, ob=ob, obp=obp, wc=wc,
                                   kt_tile=kt_tile, nwc=nwc, pt_=pt_: E.matmul(
                            psf[ob][obp:obp + 64, 0:128],
                            vtok[:, kt_tile * D + h * 64:kt_tile * D + (h + 1) * 64],
                            pt_[:, wc * 128:(wc + 1) * 128],
                            start=False, stop=(wc == nwc - 1),
                            tile_position=(0, obp)))
                    p.ops("pe", fns,
                          reads=[ptk] + [f"s3v{(W0 // 128) + wc}" for wc in range(nwc)],
                          writes=[PSF[ob]])
                    if h % 2 == 1:
                        pr = h // 2
                        p.op("dve", lambda E, pr=pr, ob=ob, attnT=attnT: E.tensor_copy(
                            attnT[:, pr * 128:(pr + 1) * 128], psf[ob][:, 0:128]),
                            reads=[PSF[ob]], writes=[X_("attnT")])
                xv = xtile[:, tb * D:(tb + 1) * D]
                xk = f"s3x{tb}"
                p.dma("sp", lambda E, xv=xv, t0=t0, i=i, attnT=attnT: E.dma_start(
                    xv, src.ap()[t0 + i * 128:t0 + (i + 1) * 128, :]), writes=[xk])
                for hh in range(2):
                    bank = hh
                    fns = []
                    for fc in range(8):
                        fns.append(lambda E, fc=fc, hh=hh, bank=bank, attnT=attnT: E.matmul(
                            psf[bank][:],
                            attnT[:, fc * 128:(fc + 1) * 128],
                            wbuf[:, fc * D + hh * 512:fc * D + (hh + 1) * 512],
                            start=(fc == 0), stop=False))
                    fns.append(lambda E, hh=hh, bank=bank, attnT=attnT: E.matmul(
                        psf[bank][:], ones_row[0:1, :],
                        bor[0:1, hh * 512:(hh + 1) * 512],
                        start=False, stop=True))
                    p.ops("pe", fns, reads=[X_("attnT"), "wbuf", "bor", "ones_row"],
                          writes=[PSF[bank]])
                for hh in range(2):
                    p.op("dve", lambda E, xv=xv, hh=hh: E.tensor_add(
                        xv[:, hh * 512:(hh + 1) * 512],
                        xv[:, hh * 512:(hh + 1) * 512], psf[hh][:]),
                        reads=[PSF[hh], xk], writes=[xk])
                emit_ln(xv, xk, dst, t0, i, f"s3dst{t0}_{i}", slot=i % 2)
            for i in range(NT):
                c_p1(i)
                if i >= 1:
                    c_p2(i - 1)
            c_p2(NT - 1)
        s3.close()

    def stage4(src, dst):
        s4 = contextlib.ExitStack()
        def lsb(name, shape, dt=F32):
            return s4.enter_context(nc.sbuf_tensor(name, list(shape), dt))

        FCN = 24
        up_h = nc.dram_tensor("up_h", [DFF1, TOK], BF16, kind="Internal")
        gate_h = nc.dram_tensor("gate_h", [DFF1, TOK], BF16, kind="Internal")
        p2_h = nc.dram_tensor("p2_h", [DFF1, TOK], BF16, kind="Internal")
        p3_h = nc.dram_tensor("p3_h", [DFF1, TOK], BF16, kind="Internal")
        ccio = [(nc.dram_tensor(f"cci{j}", [1, 2], F32, kind="Internal"),
                 nc.dram_tensor(f"cco{j}", [1, 2], F32, kind="Internal"))
                for j in range(4)]
        RG = [list(range(cc_world))]

        arena = lsb("wqarena", [128, 2 * 8 * DFF1], BF16)
        wnat = lsb("wnat", [128, 2 * DFF1], BF16)
        bmax = lsb("bmax", [128, 32])
        sblk = lsb("sblk", [128, 32])
        rblk = lsb("rblk", [128, 32])
        wqn = lsb("wqn", [128, D], BF16)
        scr1P = [lsb(f"s4scr1_{j}", [128, 1024]) for j in range(2)]
        scr2P = [lsb(f"s4scr2_{j}", [128, 1024]) for j in range(2)]
        scrbP = [lsb(f"s4scrb_{j}", [128, 1024], BF16) for j in range(2)]
        scrb2P = [lsb(f"s4scrb2_{j}", [128, 1024], BF16) for j in range(2)]
        i32P = [lsb(f"s4i32_{j}", [128, 1024], I32) for j in range(2)]
        Otc = lsb("Otc", [128, FCN * 512], BF16)
        xtile = lsb("s4xt", [128, 2 * D])
        XTt = lsb("s4XT", [128, 8 * 512], BF16)
        mxc = lsb("mxc", [128, 4])
        amxs = lsb("amxs", [128, 384])
        scal4 = lsb("scal4", [128, 8])
        srow = lsb("srow", [1, 24])
        ccs = lsb("ccs", [1, 4])

        def bcast_scalar(src_ap, dst_col, rkeys, wkey, bank=0):
            p.op("pe", lambda E, src_ap=src_ap, bank=bank: E.matmul(
                psf[bank][:, 0:1], ones_row[0:1, :], src_ap, start=True, stop=True),
                reads=list(rkeys) + ["ones_row"], writes=[PSF[bank]])
            p.op("dve", lambda E, dst_col=dst_col, bank=bank: E.tensor_copy(
                dst_col, psf[bank][:, 0:1]), reads=[PSF[bank]], writes=[wkey])

        def part_amax(col_ap, out_row, rkeys, wkey, bank=1):
            p.op("pe", lambda E, col_ap=col_ap, bank=bank: E.transpose(
                psf[bank][0:1, 0:128], col_ap, identf[:]),
                reads=list(rkeys) + ["identf"], writes=[PSF[bank]])
            p.op("dve", lambda E, out_row=out_row, bank=bank: E.tensor_reduce(
                out_row, psf[bank][0:1, 0:128], axis=AX, op=ALU.max),
                reads=[PSF[bank]], writes=[wkey])

        def allreduce_max(j, in_row, n, rkeys, wkey):
            ci, co = ccio[j]
            p.dma("pool", lambda E, in_row=in_row, ci=ci, n=n: E.dma_start(
                ci.ap()[0:1, 0:n], in_row), reads=list(rkeys), writes=[f"cci{j}"])
            p.op("pool", lambda E, ci=ci, co=co, n=n: E.collective_compute(
                "AllReduce", ALU.max, RG, [ci.ap()[0:1, 0:n]], [co.ap()[0:1, 0:n]]),
                reads=[f"cci{j}"], writes=[f"cco{j}"])
            p.dma("pool", lambda E, co=co, n=n, j=j: E.dma_start(
                ccs[0:1, 0:n], co.ap()[0:1, 0:n]), reads=[f"cco{j}"], writes=[wkey])

        def round_ip(src_ap, dst_ap, n, rkeys, wkey, pj=0):
            iv = i32P[pj][:, 0:n]
            p.op("dve", lambda E: E.tensor_copy(iv, src_ap),
                 reads=list(rkeys), writes=[f"i32_{pj}"])
            p.op("dve", lambda E: E.tensor_copy(dst_ap, iv),
                 reads=[f"i32_{pj}"], writes=[wkey])

        def dequant(name_nat, nrows, ncols, dest3d, wkey):
            b3 = lambda ap: ap.rearrange("p (b c) -> p b c", c=32)
            for fc in range(nrows // 128):
                for sub in range(ncols // 1024):
                    wb = (fc + sub) % 2
                    nat = wnat[:, wb * D:(wb + 1) * D]
                    nk = f"s4nat{wb}"
                    p.dma("pool", lambda E, fc=fc, sub=sub, nat=nat,
                          name_nat=name_nat: E.dma_start(
                        nat, ext[name_nat].ap()[fc * 128:(fc + 1) * 128,
                                                sub * 1024:(sub + 1) * 1024]),
                        writes=[nk])
                    p.op("dve", lambda E, nat=nat: E.tensor_reduce(
                        bmax[:].rearrange("p (b o) -> p b o", o=1),
                        b3(nat), axis=AX, op=ALU.max, apply_absolute_value=True),
                        reads=[nk], writes=["s4bmax"])
                    p.op("dve", lambda E: E.tensor_scalar(
                        sblk[:], bmax[:], 1.0 / 7.0, 1.0e-12, ALU.mult, ALU.max),
                        reads=["s4bmax"], writes=["s4sblk"])
                    p.op("dve", lambda E: E.reciprocal(rblk[:], sblk[:]),
                         reads=["s4sblk"], writes=["s4rblk"])
                    rbc = rblk[:].rearrange("p (b o) -> p b o", o=1).broadcast_to([128, 32, 32])
                    sbc = sblk[:].rearrange("p (b o) -> p b o", o=1).broadcast_to([128, 32, 32])
                    iv, sB = i32P[wb], scr2P[wb]
                    p.op("dve", lambda E, nat=nat, rbc=rbc, iv=iv: E.tensor_mul(
                        iv[:].rearrange("p (b c) -> p b c", c=32), b3(nat), rbc),
                        reads=[nk, "s4rblk"], writes=[f"i32_{wb}"])
                    p.op("dve", lambda E, sB=sB, iv=iv: E.tensor_copy(
                        sB[:], iv[:]), reads=[f"i32_{wb}"], writes=[f"K_scr2{wb}"])
                    p.op("dve", lambda E, sbc=sbc, sB=sB: E.tensor_mul(
                        wqn[:].rearrange("p (b c) -> p b c", c=32),
                        sB[:].rearrange("p (b c) -> p b c", c=32), sbc),
                        reads=[f"K_scr2{wb}", "s4sblk"], writes=["wqn"])
                    bank = (fc + sub) % 2
                    fns = []
                    for dc in range(8):
                        fns.append(lambda E, dc=dc, bank=bank: E.transpose(
                            psb[bank][:, dc * 128:(dc + 1) * 128],
                            wqn[:, dc * 128:(dc + 1) * 128], identb[:]))
                    p.ops("pe", fns, reads=["wqn", "identb"], writes=[PSB[bank]])
                    p.op("dve" if (fc + sub) % 2 == 0 else "act",
                         lambda E, fc=fc, sub=sub, bank=bank: (
                             (E.tensor_copy if (fc + sub) % 2 == 0 else E.copy)(
                                 dest3d(fc, sub),
                                 psb[bank][:].rearrange("p (c t) -> p c t", c=8))),
                         reads=[PSB[bank]], writes=[wkey])

        def dest_upgate(which):
            def f(fc, sub):
                return (arena[:].rearrange("p (w c f) -> p w c f", w=2, c=8)
                        [:, which, :, fc * 128:(fc + 1) * 128])
            return f

        def dest_down(fc, sub):
            return (arena[:].rearrange("p (c f) -> p c f", f=1024)
                    [:, sub * 8:(sub + 1) * 8, fc * 128:(fc + 1) * 128])

        dequant("w_up_n", DFF1, D, dest_upgate(0), "arena0")
        dequant("w_gate_n", DFF1, D, dest_upgate(1), "arena1")

        amxP = [cols[:, 14:15], cols[:, 15:16]]
        for s in range(SEQ_PER_CORE):
            for i in range(NT):
                t0 = s * S
                tb = i % 2
                xv = xtile[:, tb * D:(tb + 1) * D]
                xk = f"s4x{tb}"
                idx = s * NT + i
                p.dma("sp" if i % 2 == 0 else "act",
                      lambda E, xv=xv, t0=t0, i=i: E.dma_start(
                    xv, src.ap()[t0 + i * 128:t0 + (i + 1) * 128, :]), writes=[xk])
                p.op("dve", lambda E, xv=xv, idx=idx: E.tensor_reduce(
                    amxs[:, idx:idx + 1], xv, axis=AX, op=ALU.max,
                    apply_absolute_value=True),
                    reads=[xk], writes=[f"am1_{idx}"])
        p.op("dve", lambda E: E.tensor_reduce(
            mxc[:, 0:1], amxs[:, 0:32], axis=AX, op=ALU.max),
            reads=[f"am1_{j}" for j in range(32)], writes=["mxc", "am1done"])
        part_amax(mxc[:, 0:1], srow[0:1, 0:1], ["mxc"], "s4sx")
        allreduce_max(0, srow[0:1, 0:1], 1, ["s4sx"], "s4cc0")
        p.op("dve", lambda E: E.tensor_scalar(
            srow[0:1, 1:2], ccs[0:1, 0:1], 1.0 / 32767.0, 1.0e-12,
            ALU.mult, ALU.max), reads=["s4cc0"], writes=["s4sxr"])
        p.op("dve", lambda E: E.reciprocal(srow[0:1, 2:3], srow[0:1, 1:2]),
             reads=["s4sxr"], writes=["s4rxr"])
        bcast_scalar(srow[0:1, 1:2], scal4[:, 0:1], ["s4sxr"], "s4sxb")
        bcast_scalar(srow[0:1, 2:3], scal4[:, 1:2], ["s4rxr"], "s4rxb")

        for s in range(SEQ_PER_CORE):
            for tc in range(4):
                t0 = s * S + tc * 512
                gt0 = t0
                for it in range(4):
                    pj = it % 2
                    xv = xtile[:, pj * D:(pj + 1) * D]
                    xk = f"s4x{pj}"
                    p.dma("act", lambda E, xv=xv, t0=t0, it=it: E.dma_start(
                        xv, src.ap()[t0 + it * 128:t0 + (it + 1) * 128, :]),
                        writes=[xk])
                    p.op("dve", lambda E, xv=xv, pj=pj: E.tensor_scalar_mul(
                        i32P[pj][:], xv, scal4[:, 1:2]),
                        reads=[xk], sreads=["s4rxb"], writes=[f"i32_{pj}"])
                    p.op("dve", lambda E, pj=pj: E.tensor_copy(
                        scrbP[pj][:], i32P[pj][:]),
                        reads=[f"i32_{pj}"], writes=[f"K_scrb{pj}"])
                    bank = pj
                    fns = []
                    for dc in range(8):
                        fns.append(lambda E, dc=dc, bank=bank, pj=pj: E.transpose(
                            psb[bank][:, dc * 128:(dc + 1) * 128],
                            scrbP[pj][:, dc * 128:(dc + 1) * 128], identb[:]))
                    p.ops("pe", fns, reads=[f"K_scrb{pj}", "identb"], writes=[PSB[bank]])
                    p.op("dve", lambda E, it=it, bank=bank: E.tensor_copy(
                        XTt[:].rearrange("p (c t) -> p c t", c=8)
                        [:, :, it * 128:(it + 1) * 128],
                        psb[bank][:].rearrange("p (c t) -> p c t", c=8)),
                        reads=[PSB[bank]], writes=[f"s4XT{it}"])
                xtk = [f"s4XT{it}" for it in range(4)]
                for fc in range(FCN):
                    for which, wk, mxi, hb in ((0, "arena0", 1, up_h),
                                               (1, "arena1", 2, gate_h)):
                        bank = which * 2 + fc % 2
                        pj = fc % 2
                        fns = []
                        for dc in range(8):
                            fns.append(lambda E, fc=fc, dc=dc, which=which,
                                       bank=bank: E.matmul(
                                psf[bank][:],
                                arena[:, which * 8 * DFF1 + dc * DFF1 +
                                      fc * 128:which * 8 * DFF1 + dc * DFF1 +
                                      (fc + 1) * 128],
                                XTt[:, dc * 512:(dc + 1) * 512],
                                start=(dc == 0), stop=(dc == 7)))
                        p.ops("pe", fns, reads=[wk] + xtk, writes=[PSF[bank]])
                        ek = f"K_scrb2{pj}" if which else f"K_scrb{pj}"
                        ebuf = (scrb2P if which else scrbP)[pj]
                        p.op("act", lambda E, bank=bank, ebuf=ebuf: E.activation(
                            ebuf[:, 0:512], psf[bank][:], ACT.Copy,
                            scale=scal4[:, 0:1]),
                            reads=[PSF[bank], "s4sxb"], writes=[ek])
                        idx = which * 192 + (s * 4 + tc) * 24 + fc
                        p.op("dve", lambda E, ebuf=ebuf, idx=idx: E.tensor_reduce(
                            amxs[:, idx:idx + 1], ebuf[:, 0:512], axis=AX,
                            op=ALU.max, apply_absolute_value=True),
                            reads=[ek, "am1done"], writes=[f"am2_{idx}"])
                        p.dma("sp", lambda E, fc=fc, gt0=gt0, hb=hb,
                              ebuf=ebuf: E.dma_start(
                            hb.ap()[fc * 128:(fc + 1) * 128, gt0:gt0 + 512],
                            ebuf[:, 0:512]),
                            reads=[ek], writes=[f"s4sp{which}_{fc}_{gt0}"])
        p.op("dve", lambda E: E.tensor_reduce(
            mxc[:, 1:2], amxs[:, 0:192], axis=AX, op=ALU.max),
            reads=[f"am2_{j}" for j in range(192)], writes=["mxc", "am2udone"])
        p.op("dve", lambda E: E.tensor_reduce(
            mxc[:, 2:3], amxs[:, 192:384], axis=AX, op=ALU.max),
            reads=[f"am2_{j}" for j in range(192, 384)],
            writes=["mxc", "am2gdone"])
        part_amax(mxc[:, 1:2], srow[0:1, 3:4], ["mxc"], "sr3")
        part_amax(mxc[:, 2:3], srow[0:1, 4:5], ["mxc"], "sr4")
        p.op("dve", lambda E: E.tensor_copy(srow[0:1, 5:7], srow[0:1, 3:5]),
             reads=["sr3", "sr4"], writes=["sr5"])
        allreduce_max(1, srow[0:1, 5:7], 2, ["sr5"], "s4cc1")
        p.op("dve", lambda E: E.tensor_scalar(
            srow[0:1, 8:10], ccs[0:1, 0:2], 1.0 / 32767.0, 1.0e-12,
            ALU.mult, ALU.max), reads=["s4cc1"], writes=["sr8", "sr9"])
        p.op("dve", lambda E: E.reciprocal(srow[0:1, 10:12], srow[0:1, 8:10]),
             reads=["sr8", "sr9"], writes=["sr10", "sr11"])
        bcast_scalar(srow[0:1, 10:11], scal4[:, 2:3], ["sr10"], "s4rub")
        bcast_scalar(srow[0:1, 11:12], scal4[:, 3:4], ["sr11"], "s4rgb")
        bcast_scalar(srow[0:1, 9:10], scal4[:, 5:6], ["sr9"], "s4sgb")

        NSL_F, NSL_T = FCN, 4
        slabs3 = [(fc, ts) for fc in range(NSL_F) for ts in range(NSL_T)]

        def p3_ctx(idx):
            fc, ts = slabs3[idx]
            pj = idx % 2
            return (fc, ts * 1024, pj, scr2P[pj], scrbP[pj], i32P[pj],
                    f"K_scr2{pj}", f"K_scrb{pj}", f"i32_{pj}")

        def p3_partA(idx):
            fc, co, pj, sB, bA, iv, kB, kC, kI = p3_ctx(idx)
            p.dma("sp", lambda E, fc=fc, co=co, bA=bA: E.dma_start(
                bA[:], gate_h.ap()[fc * 128:(fc + 1) * 128, co:co + 1024]),
                writes=[kC])
            p.op("dve", lambda E, iv=iv, bA=bA: E.tensor_scalar_mul(
                iv[:], bA[:], scal4[:, 3:4]),
                reads=[kC], sreads=["s4rgb"], writes=[kI])
            p.op("dve", lambda E, sB=sB, iv=iv: E.tensor_copy(sB[:], iv[:]),
                 reads=[kI], writes=[kB])
            p.op("act", lambda E, bA=bA, sB=sB: E.activation(
                bA[:], sB[:], ACT.Sigmoid, scale=scal4[:, 5:6]),
                reads=[kB, "s4sgb"], writes=[kC])

        def p3_partB(idx):
            fc, co, pj, sB, bA, iv, kB, kC, kI = p3_ctx(idx)
            p.op("dve", lambda E, iv=iv, bA=bA: E.tensor_scalar_mul(
                iv[:], bA[:], 65536.0), reads=[kC], writes=[kI])
            p.op("dve", lambda E, bA=bA, sB=sB, iv=iv: E.tensor_mul(
                bA[:], sB[:], iv[:]), reads=[kB, kI], writes=[kC])
            p.op("dve", lambda E, bA=bA, idx=idx: E.tensor_reduce(
                amxs[:, idx:idx + 1], bA[:], axis=AX, op=ALU.max,
                apply_absolute_value=True),
                reads=[kC, "am2udone"], writes=[f"am3_{idx}"])
            p.dma("sp", lambda E, fc=fc, co=co, bA=bA: E.dma_start(
                p2_h.ap()[fc * 128:(fc + 1) * 128, co:co + 1024], bA[:]),
                reads=[kC], writes=[f"s4p2s_{fc}_{co}"])

        p3_partA(0)
        p3_partA(1)
        for i in range(len(slabs3)):
            p3_partB(i)
            if i + 2 < len(slabs3):
                p3_partA(i + 2)
        p.op("dve", lambda E: E.tensor_reduce(
            mxc[:, 3:4], amxs[:, 0:96], axis=AX, op=ALU.max),
            reads=[f"am3_{j}" for j in range(96)], writes=["mxp2", "am3done"])
        part_amax(mxc[:, 3:4], srow[0:1, 12:13], ["mxp2"], "sr12")
        allreduce_max(2, srow[0:1, 12:13], 1, ["sr12"], "s4cc2")
        p.op("dve", lambda E: E.tensor_mul(
            srow[0:1, 13:14], ccs[0:1, 0:1], srow[0:1, 9:10]),
            reads=["s4cc2", "sr9"], writes=["sr13"])
        p.op("dve", lambda E: E.tensor_scalar(
            srow[0:1, 13:14], srow[0:1, 13:14],
            1.0 / 65536.0 / 32767.0, 1.0e-12, ALU.mult, ALU.max),
            sreads=["sr13"], writes=["sr13"])
        p.op("dve", lambda E: E.reciprocal(srow[0:1, 14:15], srow[0:1, 13:14]),
             reads=["sr13"], writes=["sr14"])
        p.op("dve", lambda E: E.tensor_mul(
            srow[0:1, 15:16], srow[0:1, 9:10], srow[0:1, 14:15]),
            reads=["sr9", "sr14"], writes=["sr15"])
        p.op("dve", lambda E: E.tensor_scalar(
            srow[0:1, 15:16], srow[0:1, 15:16], 1.0 / 65536.0, None, ALU.mult),
            sreads=["sr15"], writes=["sr15"])
        bcast_scalar(srow[0:1, 15:16], scal4[:, 4:5], ["sr15"], "s4c2b")

        def p4_ctx(idx):
            fc, ts = slabs3[idx]
            pj = idx % 2
            return (fc, ts * 1024, pj, scr2P[pj], scrbP[pj], scrb2P[pj],
                    i32P[pj], f"K_scr2{pj}", f"K_scrb{pj}", f"K_scrb2{pj}",
                    f"i32_{pj}")

        def p4_partA(idx):
            fc, co, pj, sB, bA, bB, iv, kB, kC, kD, kI = p4_ctx(idx)
            p.dma("sp", lambda E, fc=fc, co=co, bA=bA: E.dma_start(
                bA[:], p2_h.ap()[fc * 128:(fc + 1) * 128, co:co + 1024]),
                writes=[kC])
            p.dma("act", lambda E, fc=fc, co=co, bB=bB: E.dma_start(
                bB[:], up_h.ap()[fc * 128:(fc + 1) * 128, co:co + 1024]),
                writes=[kD])

        def p4_partB(idx):
            fc, co, pj, sB, bA, bB, iv, kB, kC, kD, kI = p4_ctx(idx)
            p.op("dve", lambda E, iv=iv, bA=bA: E.tensor_scalar_mul(
                iv[:], bA[:], scal4[:, 4:5]),
                reads=[kC], sreads=["s4c2b"], writes=[kI])
            p.op("dve", lambda E, sB=sB, iv=iv: E.tensor_copy(sB[:], iv[:]),
                 reads=[kI], writes=[kB])
            p.op("dve", lambda E, iv=iv, bB=bB: E.tensor_scalar_mul(
                iv[:], bB[:], scal4[:, 2:3]),
                reads=[kD], sreads=["s4rub"], writes=[kI])
            p.op("dve", lambda E, bB=bB, sB=sB, iv=iv: E.tensor_mul(
                bB[:], sB[:], iv[:]), reads=[kB, kI], writes=[kD])
            p.op("dve", lambda E, bB=bB, idx=idx: E.tensor_reduce(
                amxs[:, idx:idx + 1], bB[:], axis=AX, op=ALU.max,
                apply_absolute_value=True),
                reads=[kD, "am3done"], writes=[f"am4_{idx}"])
            p.dma("sp", lambda E, fc=fc, co=co, bB=bB: E.dma_start(
                p3_h.ap()[fc * 128:(fc + 1) * 128, co:co + 1024], bB[:]),
                reads=[kD], writes=[f"s4p3s_{fc}_{co}"])

        p4_partA(0)
        p4_partA(1)
        for i in range(len(slabs3)):
            p4_partB(i)
            if i + 2 < len(slabs3):
                p4_partA(i + 2)
        p.op("dve", lambda E: E.tensor_reduce(
            mxc[:, 3:4], amxs[:, 0:96], axis=AX, op=ALU.max),
            reads=[f"am4_{j}" for j in range(96)], writes=["mxp2", "am4done"])
        part_amax(mxc[:, 3:4], srow[0:1, 16:17], ["mxp2"], "sr16")
        allreduce_max(3, srow[0:1, 16:17], 1, ["sr16"], "s4cc3")
        p.op("dve", lambda E: E.tensor_mul(
            srow[0:1, 17:18], ccs[0:1, 0:1], srow[0:1, 13:14]),
            reads=["s4cc3", "sr13"], writes=["sr17"])
        p.op("dve", lambda E: E.tensor_mul(
            srow[0:1, 17:18], srow[0:1, 17:18], srow[0:1, 8:9]),
            reads=["sr8"], sreads=["sr17"], writes=["sr17"])
        p.op("dve", lambda E: E.tensor_scalar(
            srow[0:1, 17:18], srow[0:1, 17:18], 1.0 / 32767.0, 1.0e-12,
            ALU.mult, ALU.max), sreads=["sr17"], writes=["sr17"])
        p.op("dve", lambda E: E.reciprocal(srow[0:1, 18:19], srow[0:1, 17:18]),
             reads=["sr17"], writes=["sr18"])
        p.op("dve", lambda E: E.tensor_mul(
            srow[0:1, 19:20], srow[0:1, 13:14], srow[0:1, 8:9]),
            reads=["sr13", "sr8"], writes=["sr19"])
        p.op("dve", lambda E: E.tensor_mul(
            srow[0:1, 19:20], srow[0:1, 19:20], srow[0:1, 18:19]),
            reads=["sr18"], sreads=["sr19"], writes=["sr19"])
        bcast_scalar(srow[0:1, 19:20], scal4[:, 6:7], ["sr19"], "s4c3b")
        bcast_scalar(srow[0:1, 17:18], scal4[:, 7:8], ["sr17"], "s4sob")

        dequant("w_down_n", D, DFF1, dest_down, "arena0")

        for s in range(SEQ_PER_CORE):
            for tc in range(4):
                t0 = s * S + tc * 512
                gt0 = t0
                for fc in range(FCN):
                    pj = fc % 2
                    bA, iv = scrbP[pj], i32P[pj]
                    kC, kI = f"K_scrb{pj}", f"i32_{pj}"
                    p.dma("sp", lambda E, fc=fc, gt0=gt0, bA=bA: E.dma_start(
                        bA[:, 0:512],
                        p3_h.ap()[fc * 128:(fc + 1) * 128, gt0:gt0 + 512]),
                        writes=[kC])
                    p.op("dve", lambda E, iv=iv, bA=bA: E.tensor_scalar_mul(
                        iv[:, 0:512], bA[:, 0:512], scal4[:, 6:7]),
                        reads=[kC], sreads=["s4c3b"], writes=[kI])
                    p.op("dve", lambda E, fc=fc, iv=iv: E.tensor_copy(
                        Otc[:, fc * 512:(fc + 1) * 512], iv[:, 0:512]),
                        reads=[kI], writes=[f"s4Otc{fc}"])
                okeys = [f"s4Otc{fc}" for fc in range(FCN)]
                for it in range(4):
                    tb = it % 2
                    xv = xtile[:, tb * D:(tb + 1) * D]
                    xk = f"s4x{tb}"
                    p.dma("sp", lambda E, xv=xv, t0=t0, it=it: E.dma_start(
                        xv, src.ap()[t0 + it * 128:t0 + (it + 1) * 128, :]),
                        writes=[xk])
                    for hh in range(2):
                        bank = hh
                        fns = []
                        for fc in range(FCN):
                            fns.append(lambda E, fc=fc, it=it, hh=hh,
                                       bank=bank: E.matmul(
                                psf[bank][:],
                                Otc[:, fc * 512 + it * 128:fc * 512 + (it + 1) * 128],
                                arena[:, fc * 1024 + hh * 512:fc * 1024 + (hh + 1) * 512],
                                start=(fc == 0), stop=(fc == FCN - 1)))
                        p.ops("pe", fns, reads=["arena0"] + okeys, writes=[PSF[bank]])
                    for hh in range(2):
                        p.op("act", lambda E, hh=hh: E.activation(
                            scr1P[hh][:, 0:512], psf[hh][:], ACT.Copy,
                            scale=scal4[:, 7:8]),
                            reads=[PSF[hh], "s4sob"], writes=[f"K_scr1{hh}"])
                        p.op("dve", lambda E, xv=xv, hh=hh: E.tensor_add(
                            xv[:, hh * 512:(hh + 1) * 512],
                            xv[:, hh * 512:(hh + 1) * 512], scr1P[hh][:, 0:512]),
                            reads=[f"K_scr1{hh}", xk], writes=[xk])
                    emit_ln(xv, xk, dst, t0, it, f"s4dst{t0}_{it}", slot=it % 2)
        s4.close()

        s4.close()

    stage0(ext["x"], dsts[0])
    if n_stages >= 2:
        p.barrier()
        stage1(dsts[0], dsts[1])
    if n_stages >= 3:
        p.barrier()
        stage2(dsts[1], dsts[2])
    if n_stages >= 4:
        p.barrier()
        stage3(dsts[2], dsts[3])
    if n_stages >= 5:
        p.barrier()
        stage4(dsts[3], dsts[4])

    for e in DMA_ENGS:
        for j in range(DMA_K[e]):
            n = p.dma_count[e] // DMA_K[e] + (1 if p.dma_count[e] % DMA_K[e] > j else 0)
            if n:
                p.raw(e, lambda E, s=p.dma_sems[e][j], v=16 * n: E.wait_ge(s, v))

    block = st.enter_context(nc.Block())
    p.emit(block)
    st.close()
    return nc


_PROG_CACHE = {}


def kernel(**inputs):
    n_stages = int(inputs.pop("_n_stages", 5))
    x = np.ascontiguousarray(np.asarray(inputs["L0_attn_x"], np.float32)).reshape(B, S, D)
    consts = host_layouts(inputs)
    nc = build_core_program(n_stages)
    in_maps = []
    for c in range(N_CORES):
        m = dict(consts)
        m["x"] = np.ascontiguousarray(
            x[c * SEQ_PER_CORE:(c + 1) * SEQ_PER_CORE].reshape(TOK, D))
        in_maps.append(m)
    res = run_bass_kernel_spmd(nc, in_maps, core_ids=list(range(N_CORES)))
    outs = [np.asarray(r["out"]).reshape(SEQ_PER_CORE, S, D) for r in res.results]
    return np.concatenate(outs, axis=0)
